# version 08
# baseline (speedup 1.0000x reference)
import contextlib

import numpy as np
import ml_dtypes

import concourse.bass as bass
import concourse.mybir as mybir
from concourse.bass_utils import run_bass_kernel_spmd

F32 = mybir.dt.float32
BF16 = mybir.dt.bfloat16
I32 = mybir.dt.int32
AF = mybir.ActivationFunctionType
ALU = mybir.AluOpType
AX = mybir.AxisListType

D = 4096
S = 4096
NTOK = 8192
NCORE = 8
TPC = 1024
IFF = 2048
EPS = 1e-5
MAGIC = float(1.5 * 2.0 ** 23)
SCL = float(1.0 / np.float32(np.sqrt(np.float32(128.0))))
FBLK = 256
NFB = TPC // FBLK


class Ctr:
    def __init__(self, sem):
        self.sem = sem
        self.v = 0

    def inc(self, instr, n=1):
        self.v += n
        instr.then_inc(self.sem, n)
        return instr

    def dma(self, instr):
        return self.inc(instr, 16)

    def wt(self, eng, v=None):
        eng.wait_ge(self.sem, self.v if v is None else v)


def build_program(phases=6, p2u=32, p5b=NFB, p5s=9, p5q=9, dbg=False):
    nc = bass.Bass("TRN2", target_bir_lowering=False)
    dbgk = "ExternalOutput" if dbg else "Internal"

    apd = lambda name, shape, dt, kind: nc.dram_tensor(name, shape, dt, kind=kind).ap()
    x_my = apd("x_my", [TPC, D], F32, "ExternalInput")
    wqkv = apd("wqkv", [D, 768], F32, "ExternalInput")
    wo = apd("wo", [D, D], F32, "ExternalInput")
    wgT = apd("wgT", [D, IFF], F32, "ExternalInput")
    wuT = apd("wuT", [D, IFF], F32, "ExternalInput")
    wdT = apd("wdT", [IFF, D], F32, "ExternalInput")
    swg = apd("swg", [128, IFF], F32, "ExternalInput")
    swu = apd("swu", [128, IFF], F32, "ExternalInput")
    swd = apd("swd", [128, D], F32, "ExternalInput")
    invg = apd("invg", [128, D], F32, "ExternalInput")
    invu = apd("invu", [128, D], F32, "ExternalInput")
    invi = apd("invi", [128, IFF], F32, "ExternalInput")
    cosq = apd("cosq", [S, 256], F32, "ExternalInput")
    sinq = apd("sinq", [S, 256], F32, "ExternalInput")
    cosk = apd("cosk", [S, 64], F32, "ExternalInput")
    sink = apd("sink", [S, 64], F32, "ExternalInput")
    masks = apd("masks", [128, 2048], F32, "ExternalInput")
    identb = apd("identb", [128, 128], BF16, "ExternalInput")
    identf = apd("identf", [128, 128], F32, "ExternalInput")
    col0 = apd("col0", [1, 1], I32, "ExternalInput")
    y = apd("y", [TPC, D], F32, "ExternalOutput")

    ag1_in = apd("ag1_in", [64 * 128, TPC], BF16, "Internal")
    ag1_out = apd("ag1_out", [NCORE * 64 * 128, TPC], BF16, "Internal")
    qt_sc = apd("qt_sc", [16 * 128, S], BF16, dbgk)
    ag2_in = apd("ag2_in", [512, NTOK], F32, "Internal")
    if dbg:
        ag2_cp = apd("ag2_cp", [512, NTOK], F32, "ExternalOutput")
        kv_cp = apd("kv_cp", [4 * 128, 64 * 128], BF16, "ExternalOutput")
        ag1_cp = apd("ag1_cp", [NCORE * 64 * 128, TPC], BF16, "ExternalOutput")
        qkr_cp = apd("qkr_cp", [64 * 128, 640], F32, "ExternalOutput")
    ag2_out = apd("ag2_out", [D, NTOK], F32, "Internal")
    x2_sc = apd("x2_sc", [TPC, D], F32, dbgk)
    if dbg:
        qg_sc = apd("qg_sc", [NFB * 128, 2 * D], BF16, "ExternalOutput")
        qu_sc = apd("qu_sc", [NFB * 128, 2 * D], BF16, "ExternalOutput")
        qi_sc = apd("qi_sc", [NFB * 128, 2 * IFF], BF16, "ExternalOutput")
        int_sc = apd("int_sc", [8 * 128, IFF], BF16, "ExternalOutput")
        scl_sc = apd("scl_sc", [NFB * 128, 6], F32, "ExternalOutput")

    sbl = lambda name, shape, dt=F32: nc.alloc_sbuf_tensor(name, shape, dt).ap()
    identb_s = sbl("identb_s", [128, 128], BF16)
    identf_s = sbl("identf_s", [128, 128], F32)
    onesrow = sbl("onesrow", [1, 128], F32)
    onescol = sbl("onescol", [128, 1], BF16)
    epsap = sbl("epsap", [128, 1], F32)

    es0 = contextlib.ExitStack()
    SEM = {n: Ctr(es0.enter_context(nc.semaphore(n))) for n in
           ["lda", "ldb", "ldc", "ldd", "pea", "peb", "pec",
            "dva", "dvb", "dvc", "aca", "acb", "gpa", "gpb",
            "sta", "stb", "cca",
            "lda2", "ldb2", "ldc2", "ldd2", "sta2"]}
    lda, ldb, ldc, ldd = SEM["lda"], SEM["ldb"], SEM["ldc"], SEM["ldd"]
    lda2, ldb2, ldc2, ldd2 = SEM["lda2"], SEM["ldb2"], SEM["ldc2"], SEM["ldd2"]
    sta2 = SEM["sta2"]
    pea, peb, pec = SEM["pea"], SEM["peb"], SEM["pec"]
    dva, dvb, dvc = SEM["dva"], SEM["dvb"], SEM["dvc"]
    aca, acb = SEM["aca"], SEM["acb"]
    gpa, gpb = SEM["gpa"], SEM["gpb"]
    sta, stb, cca = SEM["sta"], SEM["stb"], SEM["cca"]

    with nc.Block() as blk:
        @blk.sync
        def _(sync):
            lda.dma(sync.dma_start(out=identb_s[:, :], in_=identb[:, :]))
            lda.dma(sync.dma_start(out=identf_s[:, :], in_=identf[:, :]))
            lda.wt(sync)

        @blk.vector
        def _(v):
            v.memset(onesrow[:, :], 1.0)
            v.memset(onescol[:, :], 1.0)
            dva.inc(v.memset(epsap[:, :], EPS))
            dva.wt(v)

    es123 = contextlib.ExitStack()
    kbuf_h = es123.enter_context(nc.sbuf_tensor("kbuf_h", [128, 64 * 128], BF16)).ap()
    kbuf_l = es123.enter_context(nc.sbuf_tensor("kbuf_l", [128, 64 * 128], BF16)).ap()
    vbuf_h = es123.enter_context(nc.sbuf_tensor("vbuf_h", [128, 64 * 128], BF16)).ap()
    vbuf_l = es123.enter_context(nc.sbuf_tensor("vbuf_l", [128, 64 * 128], BF16)).ap()

    with contextlib.ExitStack() as es:
        sb = lambda n_, s_, d_=F32: es.enter_context(nc.sbuf_tensor(n_, s_, d_)).ap()
        ps = lambda n_, s_, d_=F32: es.enter_context(nc.psum_tensor(n_, s_, d_)).ap()
        xt = sb("p1_x", [128, D])
        xc = sb("p1_xc", [128, D])
        h1f = sb("p1_h1f", [128, D])
        sqj = sb("p1_sqj", [128, D], BF16)
        h1h = sb("p1_h1h", [128, D], BF16)
        h1l = sb("p1_h1l", [128, D], BF16)
        ssum = sb("p1_sum", [128, 1])
        mmean = sb("p1_m", [128, 1])
        ssq = sb("p1_ssq", [128, 1])
        sstd = sb("p1_std", [128, 1])
        rstd = sb("p1_rstd", [128, 1])
        stg1 = [sb(f"p1_stg{i}", [128, 1024], BF16) for i in range(2)]
        pst = [ps(f"p1_ps{i}", [128, 1024], BF16) for i in range(2)]

        for tt in range(8):
            with nc.Block() as blk:
                lda0, dva0, aca0, acb0, gpa0, gpb0, pea0, sta0, sta20 = (
                    lda.v, dva.v, aca.v, acb.v, gpa.v, gpb.v, pea.v, sta.v,
                    sta2.v)

                @blk.sync
                def _(sync, tt=tt):
                    lda.dma(sync.dma_start(out=xt[:, :],
                                           in_=x_my[tt * 128:(tt + 1) * 128, :]))

                @blk.vector
                def _(v, lda0=lda0, aca0=aca0):
                    lda.wt(v, lda0 + 16)
                    dva.inc(v.reduce_sum(ssum[:, :], xt[:, :], axis=AX.X))
                    dva.wt(v)
                    dva.inc(v.tensor_scalar(mmean[:, :], ssum[:, :], float(1.0 / D),
                                            None, ALU.mult))
                    dva.wt(v)
                    dva.inc(v.tensor_scalar(xc[:, :], xt[:, :], mmean[:, 0:1],
                                            None, ALU.subtract))
                    aca.wt(v, aca0 + 2)
                    dva.inc(v.reciprocal(rstd[:, :], sstd[:, :]))
                    dva.wt(v)
                    dva.inc(v.tensor_scalar(h1f[:, :], xc[:, :], rstd[:, 0:1],
                                            None, ALU.mult))

                @blk.scalar
                def _(sc, dva0=dva0, pea0=pea0, sta0=sta0, sta20=sta20):
                    dva.wt(sc, dva0 + 3)
                    aca.inc(sc.activation(sqj[:, :], xc[:, :], AF.Square,
                                          accum_out=ssq[:, 0:1]))
                    aca.wt(sc)
                    aca.inc(sc.activation(sstd[:, :], ssq[:, 0:1], AF.Sqrt,
                                          bias=epsap[:, 0:1], scale=float(1.0 / D)))
                    dva.wt(sc, dva0 + 5)
                    acb.inc(sc.copy(h1h[:, :], h1f[:, :]))
                    for g in range(8):
                        pea.wt(sc, pea0 + g + 1)
                        if g >= 2:
                            if g % 2 == 0:
                                sta.wt(sc, sta0 + (g // 2) * 128)
                            else:
                                sta2.wt(sc, sta20 + (g // 2) * 128)
                        acb.inc(sc.copy(stg1[g % 2][:, :], pst[g % 2][:, :]))

                @blk.gpsimd
                def _(gp, acb0=acb0):
                    acb.wt(gp, acb0 + 1)
                    gpa.inc(gp.tensor_tensor(h1l[:, :], h1f[:, :], h1h[:, :],
                                             ALU.subtract))

                @blk.tensor
                def _(te, acb0=acb0, gpa0=gpa0):
                    g = 0
                    for hl in range(2):
                        if hl == 0:
                            acb.wt(te, acb0 + 1)
                        else:
                            gpa.wt(te, gpa0 + 1)
                        src = h1h if hl == 0 else h1l
                        for grp in range(4):
                            if g >= 2:
                                acb.wt(te, acb0 + 1 + g - 1)
                            pb = pst[g % 2]
                            for j in range(8):
                                dc = grp * 8 + j
                                ins = te.transpose(pb[:, j * 128:(j + 1) * 128],
                                                   src[:, dc * 128:(dc + 1) * 128],
                                                   identb_s[:, :])
                                if j == 7:
                                    pea.inc(ins)
                            g += 1

                @blk.gpsimd
                def _(gp, tt=tt, acb0=acb0):
                    for g in range(8):
                        hl, grp = divmod(g, 4)
                        acb.wt(gp, acb0 + 2 + g)
                        stg = stg1[g % 2]
                        stsem = sta if g % 2 == 0 else sta2
                        for j in range(8):
                            dc = grp * 8 + j
                            row0 = (hl * 32 + dc) * 128
                            stsem.dma(gp.dma_start(
                                out=ag1_in[row0:row0 + 128,
                                           tt * 128:(tt + 1) * 128],
                                in_=stg[:, j * 128:(j + 1) * 128]))
                    sta.wt(gp)
                    sta2.wt(gp)

        with nc.Block() as blk:
            @blk.gpsimd
            def _(gp):
                cca.inc(gp.collective_compute(
                    "AllGather", ALU.bypass,
                    replica_groups=[list(range(NCORE))],
                    ins=[ag1_in[:, :]],
                    outs=[ag1_out[:, :]]))
                cca.wt(gp)

    if dbg:
        with nc.Block() as blk:
            @blk.gpsimd
            def _(gp):
                for r in range(16):
                    stb.dma(gp.dma_start(
                        out=ag1_cp[r * 4096:(r + 1) * 4096, :],
                        in_=ag1_out[r * 4096:(r + 1) * 4096, :]))
                stb.wt(gp)

    if phases <= 1:
        return nc
    with contextlib.ExitStack() as es:
        sb = lambda n_, s_, d_=F32: es.enter_context(nc.sbuf_tensor(n_, s_, d_)).ap()
        ps = lambda n_, s_, d_=F32: es.enter_context(nc.psum_tensor(n_, s_, d_)).ap()
        wqh = sb("p2_wqh", [128, 32 * 768], BF16)
        wql = sb("p2_wql", [128, 32 * 768], BF16)
        wstage = [sb(f"p2_wst{i}", [128, 768]) for i in range(2)]
        xst = [[sb(f"p2_x{h}_{i}", [128, 256], BF16) for i in range(2)]
               for h in range(2)]
        rope_t = [sb(f"p2_rt{i}", [128, 640]) for i in range(2)]
        re1 = sb("p2_re1", [128, 320])
        re2 = sb("p2_re2", [128, 320])
        ro1 = sb("p2_ro1", [128, 320])
        ro2 = sb("p2_ro2", [128, 320])
        qkr = sb("p2_qkr", [128, 640])
        qkh = sb("p2_qkh", [128, 640], BF16)
        qkl = sb("p2_qkl", [128, 640], BF16)
        qkstg = [sb(f"p2_qkstg{i}", [128, 512], BF16) for i in range(2)]
        q_ps = [ps(f"p2_qps{i}", [128, 512]) for i in range(2)]
        kv_ps = [ps(f"p2_kvps{i}", [128, 256]) for i in range(2)]
        pst2 = [ps(f"p2_pst{i}", [128, 640], BF16) for i in range(2)]

        with nc.Block() as blk:
            ldb0, ldb20, dva0 = ldb.v, ldb2.v, dva.v

            @blk.sync
            def _(sync, dva0=dva0):
                for dc in range(32):
                    if dc >= 2:
                        dva.wt(sync, dva0 + 2 * (dc - 1))
                    sem = ldb if dc % 2 == 0 else ldb2
                    sem.dma(sync.dma_start(out=wstage[dc % 2][:, :],
                                           in_=wqkv[dc * 128:(dc + 1) * 128, :]))

            @blk.vector
            def _(v, ldb0=ldb0, ldb20=ldb20):
                for dc in range(32):
                    if dc % 2 == 0:
                        ldb.wt(v, ldb0 + (dc // 2 + 1) * 16)
                    else:
                        ldb2.wt(v, ldb20 + (dc // 2 + 1) * 16)
                    dva.inc(v.tensor_copy(wqh[:, dc * 768:(dc + 1) * 768],
                                          wstage[dc % 2][:, :]))
                    dva.wt(v)
                    dva.inc(v.tensor_tensor(wql[:, dc * 768:(dc + 1) * 768],
                                            wstage[dc % 2][:, :],
                                            wqh[:, dc * 768:(dc + 1) * 768],
                                            ALU.subtract))

        for u in range(p2u):
            s_, qr = divmod(u, 4)
            hf2 = qr
            with nc.Block() as blk:
                ldc0, ldc20, ldd0, peb0, pec0 = (
                    ldc.v, ldc2.v, ldd.v, peb.v, pec.v)
                dva0, dvb0, dvc0 = dva.v, dvb.v, dvc.v
                aca0, acb0, gpa0, gpb0, stb0 = aca.v, acb.v, gpa.v, gpb.v, stb.v

                @blk.sync
                def _(sync, u=u, s_=s_, hf2=hf2, peb0=peb0):
                    for ti in range(2):
                        tg = u * 2 + ti
                        trow = (tg * 128) % S
                        buf = rope_t[ti]
                        ldd.dma(sync.dma_start(out=buf[:, 0:256],
                                               in_=cosq[trow:trow + 128, :]))
                        ldd.dma(sync.dma_start(out=buf[:, 256:512],
                                               in_=sinq[trow:trow + 128, :]))
                        ldd.dma(sync.dma_start(out=buf[:, 512:576],
                                               in_=cosk[trow:trow + 128, :]))
                        ldd.dma(sync.dma_start(out=buf[:, 576:640],
                                               in_=sink[trow:trow + 128, :]))
                    for dc in range(32):
                        if dc >= 2:
                            peb.wt(sync, peb0 + dc - 1)
                        sem = ldc if dc % 2 == 0 else ldc2
                        for hl in range(2):
                            r0 = (s_ * 64 + hl * 32 + dc) * 128
                            sem.dma(sync.dma_start(
                                out=xst[hl][dc % 2][:, :],
                                in_=ag1_out[r0:r0 + 128,
                                            hf2 * 256:(hf2 + 1) * 256]))

                @blk.tensor
                def _(te, ldc0=ldc0, ldc20=ldc20):
                    for dc in range(32):
                        if dc % 2 == 0:
                            ldc.wt(te, ldc0 + (dc // 2 + 1) * 32)
                        else:
                            ldc2.wt(te, ldc20 + (dc // 2 + 1) * 32)
                        xh, xl = xst[0][dc % 2], xst[1][dc % 2]
                        wh = wqh[:, dc * 768:(dc + 1) * 768]
                        wl = wql[:, dc * 768:(dc + 1) * 768]
                        st = (dc == 0)
                        sp = (dc == 31)
                        for ti in range(2):
                            lwh = xh[:, ti * 128:(ti + 1) * 128]
                            lwl = xl[:, ti * 128:(ti + 1) * 128]
                            qp = q_ps[ti]
                            kp = kv_ps[ti]
                            te.matmul(qp[:, :], lwh, wh[:, 0:512], start=st, stop=False)
                            te.matmul(kp[:, :], lwh, wh[:, 512:768], start=st, stop=False)
                            te.matmul(qp[:, :], lwh, wl[:, 0:512], start=False, stop=False)
                            te.matmul(kp[:, :], lwh, wl[:, 512:768], start=False, stop=False)
                            te.matmul(qp[:, :], lwl, wh[:, 0:512], start=False, stop=sp)
                            ins = te.matmul(kp[:, :], lwl, wh[:, 512:768],
                                            start=False, stop=sp)
                            if ti == 1:
                                peb.inc(ins)

                @blk.vector
                def _(v, u=u, peb0=peb0, ldd0=ldd0, gpa0=gpa0, acb0=acb0,
                      pec0=pec0, stb0=stb0):
                    peb.wt(v, peb0 + 32)
                    ldd.wt(v, ldd0 + 8 * 16)
                    for ti in range(2):
                        tg = u * 2 + ti
                        if ti >= 1:
                            gpa.wt(v, gpa0 + 4 * ti)
                        buf = rope_t[ti]
                        qp = q_ps[ti]
                        kp = kv_ps[ti]
                        v.tensor_tensor(re1[:, 0:256], qp[:, 0:512:2], buf[:, 0:256], ALU.mult)
                        v.tensor_tensor(re2[:, 0:256], qp[:, 1:512:2], buf[:, 256:512], ALU.mult)
                        v.tensor_tensor(ro1[:, 0:256], qp[:, 0:512:2], buf[:, 256:512], ALU.mult)
                        v.tensor_tensor(ro2[:, 0:256], qp[:, 1:512:2], buf[:, 0:256], ALU.mult)
                        v.tensor_tensor(re1[:, 256:320], kp[:, 0:128:2],
                                        buf[:, 512:576], ALU.mult)
                        v.tensor_tensor(re2[:, 256:320], kp[:, 1:128:2],
                                        buf[:, 576:640], ALU.mult)
                        v.tensor_tensor(ro1[:, 256:320], kp[:, 0:128:2],
                                        buf[:, 576:640], ALU.mult)
                        dva.inc(v.tensor_tensor(ro2[:, 256:320], kp[:, 1:128:2],
                                                buf[:, 512:576], ALU.mult))
                        dvb.inc(v.tensor_copy(vbuf_h[:, tg * 128:(tg + 1) * 128],
                                              kp[:, 128:256]))
                        dvb.wt(v)
                        dvb.inc(v.tensor_tensor(
                            vbuf_l[:, tg * 128:(tg + 1) * 128],
                            kp[:, 128:256],
                            vbuf_h[:, tg * 128:(tg + 1) * 128], ALU.subtract))
                        acb.wt(v, acb0 + ti + 1)
                        dvc.inc(v.tensor_tensor(qkl[:, :], qkr[:, :], qkh[:, :],
                                                ALU.subtract))
                        pec.wt(v, pec0 + 2 * (ti + 1))
                        if ti >= 1:
                            stb.wt(v, stb0 + 8 * 16 * ti)
                        for hl in range(2):
                            pb = pst2[hl]
                            dvb.inc(v.tensor_copy(qkstg[hl][:, :], pb[:, 0:512]))
                            kdst = kbuf_h if hl == 0 else kbuf_l
                            dvb.inc(v.tensor_copy(
                                kdst[:, tg * 128:(tg + 1) * 128], pb[:, 512:640]))

                @blk.gpsimd
                def _(gp, u=u, dva0=dva0, acb0=acb0, dvc0=dvc0):
                    for ti in range(2):
                        dva.wt(gp, dva0 + ti + 1)
                        if ti >= 1:
                            acb.wt(gp, acb0 + ti)
                            dvc.wt(gp, dvc0 + ti)
                            if dbg:
                                stb.wt(gp)
                        gpa.inc(gp.tensor_tensor(qkr[:, 0:512:2], re1[:, 0:256],
                                                 re2[:, 0:256], ALU.subtract))
                        gpa.wt(gp)
                        gpa.inc(gp.tensor_tensor(qkr[:, 1:512:2], ro1[:, 0:256],
                                                 ro2[:, 0:256], ALU.add))
                        gpa.wt(gp)
                        gpa.inc(gp.tensor_tensor(qkr[:, 512:640:2], re1[:, 256:320],
                                                 re2[:, 256:320], ALU.subtract))
                        gpa.wt(gp)
                        gpa.inc(gp.tensor_tensor(qkr[:, 513:640:2], ro1[:, 256:320],
                                                 ro2[:, 256:320], ALU.add))
                        if dbg:
                            gpa.wt(gp)
                            tg = u * 2 + ti
                            stb.dma(gp.dma_start(
                                out=qkr_cp[tg * 128:(tg + 1) * 128, :],
                                in_=qkr[:, :]))
                            if ti == 1:
                                stb.wt(gp)

                @blk.scalar
                def _(sc, u=u, gpa0=gpa0, dvc0=dvc0, pec0=pec0):
                    for ti in range(2):
                        gpa.wt(sc, gpa0 + 4 * (ti + 1))
                        if ti >= 1:
                            dvc.wt(sc, dvc0 + ti)
                            pec.wt(sc, pec0 + 2 * ti)
                        acb.inc(sc.copy(qkh[:, :], qkr[:, :]))

                @blk.tensor
                def _(te, dvc0=dvc0, dvb0=dvb0):
                    for ti in range(2):
                        dvc.wt(te, dvc0 + ti + 1)
                        if ti >= 1:
                            te.wait_ge(SEM["dvb"].sem, dvb0 + 6 * ti)
                        for hl, srcb in ((0, qkh), (1, qkl)):
                            pb = pst2[hl]
                            for j in range(5):
                                ins = te.transpose(pb[:, j * 128:(j + 1) * 128],
                                                   srcb[:, j * 128:(j + 1) * 128],
                                                   identb_s[:, :])
                                if j == 4:
                                    pec.inc(ins)

                @blk.gpsimd
                def _(gp, u=u, dvb0=dvb0):
                    for ti in range(2):
                        tg = u * 2 + ti
                        p_ = tg // 32
                        tok0 = (tg % 32) * 128
                        for hl in range(2):
                            gp.wait_ge(SEM["dvb"].sem, dvb0 + 6 * ti + 3 + 2 * hl)
                            for h in range(4):
                                r0 = (p_ * 8 + h * 2 + hl) * 128
                                stb.dma(gp.dma_start(
                                    out=qt_sc[r0:r0 + 128, tok0:tok0 + 128],
                                    in_=qkstg[hl][:, h * 128:(h + 1) * 128]))
                    stb.wt(gp)

    if dbg:
        with nc.Block() as blk:
            @blk.gpsimd
            def _(gp):
                for i, src in enumerate((kbuf_h, kbuf_l, vbuf_h, vbuf_l)):
                    stb.dma(gp.dma_start(
                        out=kv_cp[i * 128:(i + 1) * 128, :], in_=src[:, :]))
                stb.wt(gp)

    with contextlib.ExitStack() as es:
        sb = lambda n_, s_, d_=F32: es.enter_context(nc.sbuf_tensor(n_, s_, d_)).ap()
        ps = lambda n_, s_, d_=F32: es.enter_context(nc.psum_tensor(n_, s_, d_)).ap()
        maskb = sb("p3_mask", [128, 2048])
        qgh = sb("p3_qh", [128, 512], BF16)
        qgl = sb("p3_ql", [128, 512], BF16)
        sbuf_s = sb("p3_s", [128, 32 * 512])
        mx8 = sb("p3_mx8", [128, 8])
        mcol = sb("p3_mcol", [128, 1])
        negm = sb("p3_negm", [1, 1])
        bias_s = sb("p3_bias", [128, 1])
        e32 = [sb(f"p3_e32{i}", [128, 512]) for i in range(2)]
        ehb = [sb(f"p3_eh{i}", [128, 512], BF16) for i in range(3)]
        elb = [sb(f"p3_el{i}", [128, 512], BF16) for i in range(3)]
        dn_row = sb("p3_dnr", [1, 512])
        bc_sb = sb("p3_bcsb", [128, 512])
        ao32 = sb("p3_ao", [128, 512])
        s_ps = [ps(f"p3_sps{i}", [128, 512]) for i in range(2)]
        av_ps = ps("p3_avps", [128, 512])
        dn_ps = ps("p3_dnps", [1, 512])
        mt_ps = ps("p3_mtps", [1, 128])
        nb_ps = ps("p3_nbps", [128, 1])
        bc_ps = ps("p3_bcps", [128, 512])

        with nc.Block() as blk:
            @blk.sync
            def _(sync):
                ldd.dma(sync.dma_start(out=maskb[:, :], in_=masks[:, :]))
                ldd.wt(sync)

        for p_ in range(2):
            for h in range(4):
                for g in range(8):
                    L = 4 * g + 4
                    with nc.Block() as blk:
                        lda0, pea0, peb0, pec0 = lda.v, pea.v, peb.v, pec.v
                        dva0, dvb0, dvc0 = dva.v, dvb.v, dvc.v
                        aca0, acb0, gpa0, gpb0 = aca.v, acb.v, gpa.v, gpb.v

                        @blk.sync
                        def _(sync, p_=p_, h=h, g=g):
                            r0 = (p_ * 8 + h * 2) * 128
                            c0 = g * 512
                            lda.dma(sync.dma_start(
                                out=qgh[:, :], in_=qt_sc[r0:r0 + 128, c0:c0 + 512]))
                            lda.dma(sync.dma_start(
                                out=qgl[:, :],
                                in_=qt_sc[r0 + 128:r0 + 256, c0:c0 + 512]))

                        @blk.tensor
                        def _(te, p_=p_, L=L, lda0=lda0, dva0=dva0):
                            lda.wt(te, lda0 + 32)
                            for kt in range(L):
                                tile = 32 * p_ + kt
                                pb = s_ps[kt % 2]
                                if kt >= 2:
                                    dva.wt(te, dva0 + kt - 1)
                                kh = kbuf_h[:, tile * 128:(tile + 1) * 128]
                                kl = kbuf_l[:, tile * 128:(tile + 1) * 128]
                                te.matmul(pb[:, :], kh, qgh[:, :], start=True, stop=False)
                                te.matmul(pb[:, :], kh, qgl[:, :], start=False, stop=False)
                                pea.inc(te.matmul(pb[:, :], kl, qgh[:, :],
                                                  start=False, stop=True))

                        @blk.vector
                        def _(v, g=g, L=L, pea0=pea0):
                            for kt in range(L):
                                pea.wt(v, pea0 + kt + 1)
                                pb = s_ps[kt % 2]
                                dst = sbuf_s[:, kt * 512:(kt + 1) * 512]
                                if kt >= 4 * g:
                                    jm = kt - 4 * g
                                    dva.inc(v.scalar_tensor_tensor(
                                        dst, pb[:, :], SCL,
                                        maskb[:, jm * 512:(jm + 1) * 512],
                                        ALU.mult, ALU.add))
                                else:
                                    dva.inc(v.tensor_scalar(dst, pb[:, :], SCL,
                                                            None, ALU.mult))
                            nch = (L * 512) // 2048
                            for j in range(nch):
                                dvb.inc(v.reduce_max(
                                    mx8[:, j:j + 1],
                                    sbuf_s[:, j * 2048:(j + 1) * 2048], axis=AX.X))
                            dvb.wt(v)
                            dvb.inc(v.reduce_max(mcol[:, :], mx8[:, 0:nch], axis=AX.X))

                        @blk.tensor
                        def _(te, dvb0=dvb0, L=L):
                            nch = (L * 512) // 2048
                            dvb.wt(te, dvb0 + nch + 1)
                            pec.inc(te.transpose(mt_ps[:, :], mcol[:, :],
                                                 identf_s[:, :]))

                        @blk.vector
                        def _(v, pec0=pec0):
                            pec.wt(v, pec0 + 1)
                            dvc.inc(v.reduce_max(negm[:, :], mt_ps[0:1, :], axis=AX.X,
                                                 negate=True))

                        @blk.tensor
                        def _(te, dvc0=dvc0):
                            dvc.wt(te, dvc0 + 1)
                            pec.inc(te.matmul(nb_ps[:, :], onesrow[:, :], negm[:, :],
                                              start=True, stop=True))

                        @blk.vector
                        def _(v, pec0=pec0):
                            pec.wt(v, pec0 + 2)
                            dvc.inc(v.tensor_copy(bias_s[:, :], nb_ps[:, :]))

                        @blk.scalar
                        def _(sc, L=L, dvc0=dvc0, dva0=dva0, gpb0=gpb0, peb0=peb0):
                            dvc.wt(sc, dvc0 + 2)
                            for kt in range(L):
                                dva.wt(sc, dva0 + kt + 1)
                                if kt >= 2:
                                    gpb.wt(sc, gpb0 + kt - 1)
                                if kt >= 3:
                                    peb.wt(sc, peb0 + kt - 2)
                                aca.inc(sc.activation(
                                    e32[kt % 2][:, :],
                                    sbuf_s[:, kt * 512:(kt + 1) * 512],
                                    AF.Exp, bias=bias_s[:, 0:1], scale=1.0))
                                aca.wt(sc)
                                acb.inc(sc.copy(ehb[kt % 3][:, :], e32[kt % 2][:, :]))

                        @blk.gpsimd
                        def _(gp, L=L, acb0=acb0, peb0=peb0):
                            for kt in range(L):
                                acb.wt(gp, acb0 + kt + 1)
                                if kt >= 3:
                                    peb.wt(gp, peb0 + kt - 2)
                                gpb.inc(gp.tensor_tensor(
                                    elb[kt % 3][:, :], e32[kt % 2][:, :],
                                    ehb[kt % 3][:, :], ALU.subtract))

                        @blk.tensor
                        def _(te, p_=p_, L=L, gpb0=gpb0):
                            for kt in range(L):
                                tile = 32 * p_ + kt
                                gpb.wt(te, gpb0 + kt + 1)
                                vh = vbuf_h[:, tile * 128:(tile + 1) * 128]
                                vl = vbuf_l[:, tile * 128:(tile + 1) * 128]
                                eh_, el_ = ehb[kt % 3], elb[kt % 3]
                                st = (kt == 0)
                                sp = (kt == L - 1)
                                te.matmul(av_ps[:, :], vh, eh_[:, :], start=st, stop=False)
                                te.matmul(av_ps[:, :], vh, el_[:, :], start=False,
                                          stop=False)
                                te.matmul(av_ps[:, :], vl, eh_[:, :], start=False, stop=sp)
                                te.matmul(dn_ps[:, :], onescol[:, :], eh_[:, :],
                                          start=st, stop=False)
                                peb.inc(te.matmul(dn_ps[:, :], onescol[:, :], el_[:, :],
                                                  start=False, stop=sp))

                        @blk.vector
                        def _(v, L=L, peb0=peb0):
                            peb.wt(v, peb0 + L)
                            dvc.inc(v.reciprocal(dn_row[:, :], dn_ps[0:1, :]))

                        @blk.tensor
                        def _(te, dvc0=dvc0):
                            dvc.wt(te, dvc0 + 3)
                            pec.inc(te.matmul(bc_ps[:, :], onesrow[:, :], dn_row[:, :],
                                              start=True, stop=True))

                        @blk.vector
                        def _(v, pec0=pec0):
                            pec.wt(v, pec0 + 3)
                            dvc.inc(v.tensor_copy(bc_sb[:, :], bc_ps[:, :]))
                            dvc.wt(v)
                            dvc.inc(v.tensor_tensor(ao32[:, :], av_ps[:, :],
                                                    bc_sb[:, :], ALU.mult))

                        @blk.gpsimd
                        def _(gp, p_=p_, h=h, g=g, dvc0=dvc0):
                            dvc.wt(gp, dvc0 + 5)
                            r0 = h * 128
                            c0 = p_ * S + g * 512
                            stb.dma(gp.dma_start(
                                out=ag2_in[r0:r0 + 128, c0:c0 + 512], in_=ao32[:, :]))
                            if dbg:
                                stb.dma(gp.dma_start(
                                    out=ag2_cp[r0:r0 + 128, c0:c0 + 512],
                                    in_=ao32[:, :]))
                            stb.wt(gp)

        with nc.Block() as blk:
            @blk.gpsimd
            def _(gp):
                cca.inc(gp.collective_compute(
                    "AllGather", ALU.bypass,
                    replica_groups=[list(range(NCORE))],
                    ins=[ag2_in[:, :]],
                    outs=[ag2_out[:, :]]))
                cca.wt(gp)

    es123.close()
    if phases <= 3:
        return nc

    with contextlib.ExitStack() as es:
        sb = lambda n_, s_, d_=F32: es.enter_context(nc.sbuf_tensor(n_, s_, d_)).ap()
        ps = lambda n_, s_, d_=F32: es.enter_context(nc.psum_tensor(n_, s_, d_)).ap()
        aoh = sb("p4_aoh", [128, 32 * 1024], BF16)
        aol = sb("p4_aol", [128, 32 * 1024], BF16)
        aost = [sb(f"p4_aost{i}", [128, 4096]) for i in range(2)]
        c0sb = sb("p4_c0", [1, 1], I32)
        wst = [sb(f"p4_wst{i}", [128, 512]) for i in range(2)]
        woh = [sb(f"p4_woh{i}", [128, 512], BF16) for i in range(2)]
        wol = [sb(f"p4_wol{i}", [128, 512], BF16) for i in range(2)]
        xres = [sb(f"p4_xr{i}", [128, 512]) for i in range(2)]
        x2st = [sb(f"p4_x2{i}", [128, 512]) for i in range(2)]
        wo_ps = [ps(f"p4_wops{i}", [128, 512]) for i in range(8)]

        with nc.Block() as blk:
            ldb0, ldb20, aca0, dva0 = ldb.v, ldb2.v, aca.v, dva.v

            @blk.sync
            def _(sync, ldb0=ldb0, dva0=dva0):
                ldb.dma(sync.dma_start(out=c0sb[:, :], in_=col0[:, :]))
                ldb.wt(sync, ldb0 + 16)
                ag2v = ag2_out.rearrange("(g a p) c -> g p a c", p=128, a=4)
                with sync.register("p4off") as off_reg:
                    sync.reg_load(off_reg, c0sb[0:1, 0:1])
                    off = sync.snap(off_reg)
                    for g4 in range(8):
                        if g4 >= 2:
                            dva.wt(sync, dva0 + 4 * (g4 - 1))
                        dst = aost[g4 % 2].rearrange("p (a b) -> p a b", a=4)
                        sem = ldb if g4 % 2 == 0 else ldb2
                        sem.dma(sync.dma_start(
                            out=dst, in_=ag2v[g4, :, :, bass.ds(off, 1024)]))

            @blk.scalar
            def _(sc, ldb0=ldb0, ldb20=ldb20):
                for g4 in range(8):
                    if g4 % 2 == 0:
                        ldb.wt(sc, ldb0 + 16 + (g4 // 2 + 1) * 16)
                    else:
                        ldb2.wt(sc, ldb20 + (g4 // 2 + 1) * 16)
                    for j in range(4):
                        dc = g4 * 4 + j
                        aca.inc(sc.copy(aoh[:, dc * 1024:(dc + 1) * 1024],
                                        aost[g4 % 2][:, j * 1024:(j + 1) * 1024]))

            @blk.vector
            def _(v, aca0=aca0):
                for g4 in range(8):
                    for j in range(4):
                        dc = g4 * 4 + j
                        aca.wt(v, aca0 + dc + 1)
                        dva.inc(v.tensor_tensor(aol[:, dc * 1024:(dc + 1) * 1024],
                                                aost[g4 % 2][:, j * 1024:(j + 1) * 1024],
                                                aoh[:, dc * 1024:(dc + 1) * 1024],
                                                ALU.subtract))

        for oc in range(8):
            with nc.Block() as blk:
                ldc0, ldc20, lda0, lda20, dva0, dvb0, peb0, sta0, sta20 = (
                    ldc.v, ldc2.v, lda.v, lda2.v, dva.v, dvb.v, peb.v, sta.v,
                    sta2.v)

                @blk.sync
                def _(sync, oc=oc, dva0=dva0, dvb0=dvb0):
                    for dc in range(32):
                        if dc >= 2:
                            dva.wt(sync, dva0 + 2 * (dc - 1))
                        sem = ldc if dc % 2 == 0 else ldc2
                        sem.dma(sync.dma_start(
                            out=wst[dc % 2][:, :],
                            in_=wo[dc * 128:(dc + 1) * 128,
                                   oc * 512:(oc + 1) * 512]))
                    for tt in range(8):
                        if tt >= 2:
                            dvb.wt(sync, dvb0 + tt - 1)
                        sem = lda if tt % 2 == 0 else lda2
                        sem.dma(sync.dma_start(
                            out=xres[tt % 2][:, :],
                            in_=x_my[tt * 128:(tt + 1) * 128,
                                     oc * 512:(oc + 1) * 512]))

                @blk.vector
                def _(v, ldc0=ldc0, ldc20=ldc20, peb0=peb0):
                    for dc in range(32):
                        if dc % 2 == 0:
                            ldc.wt(v, ldc0 + (dc // 2 + 1) * 16)
                        else:
                            ldc2.wt(v, ldc20 + (dc // 2 + 1) * 16)
                        if dc >= 2:
                            peb.wt(v, peb0 + dc - 1)
                        dva.inc(v.tensor_copy(woh[dc % 2][:, :], wst[dc % 2][:, :]))
                        dva.wt(v)
                        dva.inc(v.tensor_tensor(wol[dc % 2][:, :], wst[dc % 2][:, :],
                                                woh[dc % 2][:, :], ALU.subtract))

                @blk.tensor
                def _(te, dva0=dva0):
                    for dc in range(32):
                        dva.wt(te, dva0 + 2 * (dc + 1))
                        st = (dc == 0)
                        sp = (dc == 31)
                        for tt in range(8):
                            ah = aoh[:, dc * 1024 + tt * 128:dc * 1024 + (tt + 1) * 128]
                            al = aol[:, dc * 1024 + tt * 128:dc * 1024 + (tt + 1) * 128]
                            te.matmul(wo_ps[tt][:, :], ah, woh[dc % 2][:, :],
                                      start=st, stop=False)
                            te.matmul(wo_ps[tt][:, :], ah, wol[dc % 2][:, :],
                                      start=False, stop=False)
                            ins = te.matmul(wo_ps[tt][:, :], al, woh[dc % 2][:, :],
                                            start=False, stop=sp)
                            if tt == 7:
                                peb.inc(ins)

                @blk.vector
                def _(v, peb0=peb0, lda0=lda0, lda20=lda20,
                      sta0=sta0, sta20=sta20):
                    for tt in range(8):
                        peb.wt(v, peb0 + 32)
                        if tt % 2 == 0:
                            lda.wt(v, lda0 + (tt // 2 + 1) * 16)
                        else:
                            lda2.wt(v, lda20 + (tt // 2 + 1) * 16)
                        if tt >= 2:
                            if tt % 2 == 0:
                                sta.wt(v, sta0 + (tt // 2) * 16)
                            else:
                                sta2.wt(v, sta20 + (tt // 2) * 16)
                        dvb.inc(v.tensor_tensor(x2st[tt % 2][:, :], wo_ps[tt][:, :],
                                                xres[tt % 2][:, :], ALU.add))

                @blk.gpsimd
                def _(gp, oc=oc, dvb0=dvb0):
                    for tt in range(8):
                        dvb.wt(gp, dvb0 + tt + 1)
                        sem = sta if tt % 2 == 0 else sta2
                        sem.dma(gp.dma_start(
                            out=x2_sc[tt * 128:(tt + 1) * 128,
                                      oc * 512:(oc + 1) * 512],
                            in_=x2st[tt % 2][:, :]))
                    sta.wt(gp)
                    sta2.wt(gp)

    if phases <= 4:
        return nc
    with contextlib.ExitStack() as es:
        sb = lambda n_, s_, d_=F32: es.enter_context(nc.sbuf_tensor(n_, s_, d_)).ap()
        ps = lambda n_, s_, d_=F32: es.enter_context(nc.psum_tensor(n_, s_, d_)).ap()
        swg_s = sb("p5_swg", [128, IFF])
        swu_s = sb("p5_swu", [128, IFF])
        invg_s = sb("p5_invg", [128, D])
        invu_s = sb("p5_invu", [128, D])
        invi_s = sb("p5_invi", [128, IFF])
        xsg = sb("p5_xsg", [128, D])
        xsu = sb("p5_xsu", [128, D])
        h2 = sb("p5_h2", [128, D], BF16)
        st5 = {n_: sb(f"p5_{n_}", [128, 1]) for n_ in
               ["sum", "m", "ssq", "std", "rstd"]}
        qgb = sb("p5_qgb", [128, D], BF16)
        qub = sb("p5_qub", [128, D], BF16)
        rcpc = sb("p5_rcpc", [128, 1])
        sgb = sb("p5_sgb", [128, 2])
        sub = sb("p5_sub", [128, 2])
        sib = sb("p5_sib", [128, 2])
        qgT = sb("p5_qgT", [128, 32 * FBLK], BF16)
        quT = sb("p5_quT", [128, 32 * FBLK], BF16)
        qiT = sb("p5_qiT", [128, 16 * FBLK], BF16)
        interb = [sb(f"p5_int{i}", [128, IFF], BF16) for i in range(2)]
        work2048 = sb("p5_work", [128, IFF])
        xsi = work2048
        gbuf = work2048[:, 0:512]
        silb = work2048[:, 512:1024]
        ubuf = work2048[:, 1024:1536]
        ibuf = work2048[:, 1536:2048]
        qib = sb("p5_qib", [128, IFF], BF16)
        wst5 = [sb(f"p5_wst{i}", [128, 1024]) for i in range(2)]
        wgb = [sb(f"p5_wgb{i}", [128, 512], BF16) for i in range(2)]
        wub = [sb(f"p5_wub{i}", [128, 512], BF16) for i in range(2)]
        wdst = [wst5[i][:, 0:512] for i in range(2)]
        wdb = [sb(f"p5_wdb{i}", [128, 512], BF16) for i in range(2)]
        swdst = [sb(f"p5_swdst{i}", [128, 512]) for i in range(2)]
        x2r = [sb(f"p5_x2r{i}", [128, 512]) for i in range(2)]
        tb5 = sb("p5_tb", [128, 512])
        yb = [sb(f"p5_yb{i}", [128, 512]) for i in range(2)]
        gu_ps = [ps(f"p5_gups{i}", [128, 512]) for i in range(4)]
        d_ps = [ps(f"p5_dps{i}", [128, 512]) for i in range(2)]
        pst5 = [ps(f"p5_pst{i}", [128, 1024], BF16) for i in range(2)]

        with nc.Block() as blk:
            @blk.sync
            def _(sync):
                for dst, src in ((swg_s, swg), (swu_s, swu),
                                 (invg_s, invg), (invu_s, invu), (invi_s, invi)):
                    ldd.dma(sync.dma_start(out=dst[:, :], in_=src[:, :]))
                ldd.wt(sync)

        def quant(eng, ctr, src_bf, inv_s, xs_, sc_col, qb_, n):
            ctr.inc(eng.tensor_tensor(xs_[:, 0:n], src_bf[:, 0:n], inv_s[:, 0:n],
                                      ALU.mult))
            ctr.wt(eng)
            ctr.inc(eng.reduce_max(sc_col, xs_[:, 0:n], axis=AX.X,
                                   apply_absolute_value=True))
            ctr.wt(eng)
            ctr.inc(eng.tensor_scalar(sc_col, sc_col, float(1.0 / 127.0), 1e-8,
                                      ALU.mult, ALU.max))
            ctr.wt(eng)
            ctr.inc(eng.reciprocal(rcpc[:, :], sc_col))
            ctr.wt(eng)
            ctr.inc(eng.tensor_scalar(xs_[:, 0:n], xs_[:, 0:n], rcpc[:, 0:1], MAGIC,
                                      ALU.mult, ALU.add))
            ctr.wt(eng)
            ctr.inc(eng.tensor_scalar(xs_[:, 0:n], xs_[:, 0:n], MAGIC, 127.0,
                                      ALU.subtract, ALU.min))
            ctr.wt(eng)
            ctr.inc(eng.tensor_scalar(qb_[:, 0:n], xs_[:, 0:n], -127.0, None,
                                      ALU.max))
            ctr.wt(eng)

        for b in range(p5b):
            for t2 in range(2):
                tt = b * 2 + t2
                with nc.Block() as blk:
                    lda0, dva0, aca0, acb0, gpa0, pea0, gpb0 = (
                        lda.v, dva.v, aca.v, acb.v, gpa.v, pea.v, gpb.v)

                    @blk.sync
                    def _(sync, tt=tt):
                        lda.dma(sync.dma_start(
                            out=xsu[:, :], in_=x2_sc[tt * 128:(tt + 1) * 128, :]))

                    @blk.vector
                    def _(v, lda0=lda0, aca0=aca0, t2=t2):
                        lda.wt(v, lda0 + 16)
                        dva.inc(v.reduce_sum(st5["sum"][:, :], xsu[:, :], axis=AX.X))
                        dva.wt(v)
                        dva.inc(v.tensor_scalar(st5["m"][:, :], st5["sum"][:, :],
                                                float(1.0 / D), None, ALU.mult))
                        dva.wt(v)
                        dva.inc(v.tensor_scalar(xsg[:, :], xsu[:, :], st5["m"][:, 0:1],
                                                None, ALU.subtract))
                        aca.wt(v, aca0 + 2)
                        dva.inc(v.reciprocal(st5["rstd"][:, :], st5["std"][:, :]))
                        dva.wt(v)
                        dva.inc(v.tensor_scalar(h2[:, :], xsg[:, :],
                                                st5["rstd"][:, 0:1], None, ALU.mult))
                        dva.wt(v)
                        if p5q >= 1:
                            quant(v, dva, h2, invg_s, xsg, sgb[:, t2:t2 + 1], qgb, D)
                        if p5q >= 2:
                            quant(v, dva, h2, invu_s, xsu, sub[:, t2:t2 + 1], qub, D)

                    @blk.scalar
                    def _(sc, t2=t2, dva0=dva0, pea0=pea0):
                        dva.wt(sc, dva0 + 3)
                        aca.inc(sc.activation(h2[:, :], xsg[:, :], AF.Square,
                                              accum_out=st5["ssq"][:, 0:1]))
                        aca.wt(sc)
                        aca.inc(sc.activation(st5["std"][:, :], st5["ssq"][:, 0:1],
                                              AF.Sqrt, bias=epsap[:, 0:1],
                                              scale=float(1.0 / D)))
                        if p5q < 3:
                            return
                        gi = 0
                        for dstT in (qgT, quT):
                            for grp in range(4):
                                gi += 1
                                pea.wt(sc, pea0 + gi)
                                pb = pst5[(gi - 1) % 2]
                                c_ = t2 * 4096 + grp * 1024
                                acb.inc(sc.copy(dstT[:, c_:c_ + 1024], pb[:, :]))

                    @blk.tensor
                    def _(te, dva0=dva0, acb0=acb0):
                        if p5q < 3:
                            return
                        gi = 0
                        for src in (qgb, qub):
                            if src is qgb:
                                dva.wt(te, dva0 + 12)
                            else:
                                dva.wt(te, dva0 + 19)
                            for grp in range(4):
                                if gi >= 2:
                                    acb.wt(te, acb0 + gi - 1)
                                pb = pst5[gi % 2]
                                for j in range(8):
                                    dc = grp * 8 + j
                                    ins = te.transpose(
                                        pb[:, j * 128:(j + 1) * 128],
                                        src[:, dc * 128:(dc + 1) * 128],
                                        identb_s[:, :])
                                    if j == 7:
                                        pea.inc(ins)
                                gi += 1

            if p5s < 2:
                continue
            with nc.Block() as blk:
                ldb0, ldb20, dva0, dvb0, aca0, peb0 = (
                    ldb.v, ldb2.v, dva.v, dvb.v, aca.v, peb.v)

                @blk.sync
                def _(sync, dva0=dva0):
                    for ic in range(4):
                        for hc in range(32):
                            it = ic * 32 + hc
                            if it >= 2:
                                dva.wt(sync, dva0 + 2 * (it - 1))
                            sem = ldb if it % 2 == 0 else ldb2
                            sem.dma(sync.dma_start(
                                out=wst5[it % 2][:, 0:512],
                                in_=wgT[hc * 128:(hc + 1) * 128,
                                        ic * 512:(ic + 1) * 512]))
                            sem.dma(sync.dma_start(
                                out=wst5[it % 2][:, 512:1024],
                                in_=wuT[hc * 128:(hc + 1) * 128,
                                        ic * 512:(ic + 1) * 512]))

                @blk.vector
                def _(v, ldb0=ldb0, ldb20=ldb20, peb0=peb0, aca0=aca0):
                    for ic in range(4):
                        for hc in range(32):
                            it = ic * 32 + hc
                            if it % 2 == 0:
                                ldb.wt(v, ldb0 + (it // 2 + 1) * 32)
                            else:
                                ldb2.wt(v, ldb20 + (it // 2 + 1) * 32)
                            if it >= 2:
                                peb.wt(v, peb0 + it - 1)
                            dva.inc(v.tensor_copy(wgb[it % 2][:, :],
                                                  wst5[it % 2][:, 0:512]))
                            dva.wt(v)
                            dva.inc(v.tensor_copy(wub[it % 2][:, :],
                                                  wst5[it % 2][:, 512:1024]))
                        for t2 in range(2):
                            idx = ic * 2 + t2
                            peb.wt(v, peb0 + (ic + 1) * 32)
                            aca.wt(v, aca0 + idx)
                            dvb.inc(v.scalar_tensor_tensor(
                                gbuf[:, :], gu_ps[t2][:, :], sgb[:, t2:t2 + 1],
                                swg_s[:, ic * 512:(ic + 1) * 512],
                                ALU.mult, ALU.mult))
                            dvb.inc(v.scalar_tensor_tensor(
                                ubuf[:, :], gu_ps[2 + t2][:, :], sub[:, t2:t2 + 1],
                                swu_s[:, ic * 512:(ic + 1) * 512],
                                ALU.mult, ALU.mult))
                            aca.wt(v, aca0 + idx + 1)
                            dvb.inc(v.tensor_tensor(ibuf[:, :], silb[:, :],
                                                    ubuf[:, :], ALU.mult))
                            dvb.wt(v)
                            dvb.inc(v.tensor_scalar(
                                interb[t2][:, ic * 512:(ic + 1) * 512],
                                ibuf[:, :], 10.0, -10.0, ALU.min, ALU.max))

                @blk.tensor
                def _(te, dva0=dva0, dvb0=dvb0):
                    for ic in range(4):
                        for hc in range(32):
                            it = ic * 32 + hc
                            dva.wt(te, dva0 + 2 * (it + 1))
                            if hc == 0 and ic >= 1:
                                dvb.wt(te, dvb0 + 8 * ic)
                            st = (hc == 0)
                            sp = (hc == 31)
                            for t2 in range(2):
                                qg_t = qgT[:, t2 * 4096 + hc * 128:
                                           t2 * 4096 + (hc + 1) * 128]
                                qu_t = quT[:, t2 * 4096 + hc * 128:
                                           t2 * 4096 + (hc + 1) * 128]
                                te.matmul(gu_ps[t2][:, :], qg_t, wgb[it % 2][:, :],
                                          start=st, stop=sp)
                                ins = te.matmul(gu_ps[2 + t2][:, :], qu_t,
                                                wub[it % 2][:, :], start=st, stop=sp)
                                if t2 == 1:
                                    peb.inc(ins)

                @blk.scalar
                def _(sc, dvb0=dvb0):
                    for ic in range(4):
                        for t2 in range(2):
                            idx = ic * 2 + t2
                            dvb.wt(sc, dvb0 + idx * 4 + 1)
                            aca.inc(sc.activation(silb[:, :], gbuf[:, :], AF.Silu))

            if p5s < 3:
                continue
            for t2 in range(2):
                with nc.Block() as blk:
                    dva0, pea0 = dva.v, pea.v

                    @blk.vector
                    def _(v, t2=t2):
                        quant(v, dva, interb[t2], invi_s, xsi, sib[:, t2:t2 + 1],
                              qib, IFF)

                    @blk.tensor
                    def _(te, dva0=dva0):
                        dva.wt(te, dva0 + 7)
                        for grp in range(2):
                            pb = pst5[grp % 2]
                            for j in range(8):
                                dc = grp * 8 + j
                                ins = te.transpose(pb[:, j * 128:(j + 1) * 128],
                                                   qib[:, dc * 128:(dc + 1) * 128],
                                                   identb_s[:, :])
                                if j == 7:
                                    pea.inc(ins)

                    @blk.scalar
                    def _(sc, t2=t2, pea0=pea0):
                        for grp in range(2):
                            pea.wt(sc, pea0 + grp + 1)
                            pb = pst5[grp % 2]
                            c_ = t2 * 2048 + grp * 1024
                            acb.inc(sc.copy(qiT[:, c_:c_ + 1024], pb[:, :]))

            if dbg:
                with nc.Block() as blk:
                    @blk.gpsimd
                    def _(gp, b=b):
                        r0 = b * 128
                        stb.dma(gp.dma_start(out=qg_sc[r0:r0 + 128, :],
                                             in_=qgT[:, :]))
                        stb.dma(gp.dma_start(out=qu_sc[r0:r0 + 128, :],
                                             in_=quT[:, :]))
                        stb.dma(gp.dma_start(out=qi_sc[r0:r0 + 128, :],
                                             in_=qiT[:, :]))
                        stb.dma(gp.dma_start(out=scl_sc[r0:r0 + 128, 0:2],
                                             in_=sgb[:, :]))
                        stb.dma(gp.dma_start(out=scl_sc[r0:r0 + 128, 2:4],
                                             in_=sub[:, :]))
                        stb.dma(gp.dma_start(out=scl_sc[r0:r0 + 128, 4:6],
                                             in_=sib[:, :]))
                        for t2 in range(2):
                            stb.dma(gp.dma_start(
                                out=int_sc[(b * 2 + t2) * 128:
                                           (b * 2 + t2 + 1) * 128, :],
                                in_=interb[t2][:, :]))
                        stb.wt(gp)

            if p5s < 4:
                continue
            with nc.Block() as blk:
                ldb0, ldd0, lda0, dva0, dvb0, peb0, gpa0, sta0 = (
                    ldb.v, ldd.v, lda.v, dva.v, dvb.v, peb.v, gpa.v, sta.v)

                @blk.sync
                def _(sync, b=b, dva0=dva0, gpa0=gpa0, dvb0=dvb0):
                    for oc in range(8):
                        for ic in range(16):
                            it = oc * 16 + ic
                            if it >= 2:
                                dva.wt(sync, dva0 + it - 1)
                            ldb.dma(sync.dma_start(
                                out=wdst[it % 2][:, :],
                                in_=wdT[ic * 128:(ic + 1) * 128,
                                        oc * 512:(oc + 1) * 512]))
                        if oc >= 2:
                            dvb.wt(sync, dvb0 + 2 * (oc - 1))
                        ldd.dma(sync.dma_start(
                            out=swdst[oc % 2][:, :],
                            in_=swd[:, oc * 512:(oc + 1) * 512]))
                        for t2 in range(2):
                            i2 = oc * 2 + t2
                            if i2 >= 4:
                                gpa.wt(sync, gpa0 + i2 - 1)
                            lda.dma(sync.dma_start(
                                out=x2r[t2][:, :],
                                in_=x2_sc[(b * 2 + t2) * 128:(b * 2 + t2 + 1) * 128,
                                          oc * 512:(oc + 1) * 512]))

                @blk.vector
                def _(v, ldb0=ldb0, peb0=peb0, lda0=lda0, ldd0=ldd0, gpa0=gpa0):
                    for oc in range(8):
                        for ic in range(16):
                            it = oc * 16 + ic
                            ldb.wt(v, ldb0 + (it + 1) * 16)
                            if it >= 2:
                                peb.wt(v, peb0 + it - 1)
                            dva.inc(v.tensor_copy(wdb[it % 2][:, :],
                                                  wdst[it % 2][:, :]))
                        for t2 in range(2):
                            i2 = oc * 2 + t2
                            peb.wt(v, peb0 + (oc + 1) * 16)
                            ldd.wt(v, ldd0 + (oc + 1) * 16)
                            if i2 >= 1:
                                gpa.wt(v, gpa0 + i2)
                            dvb.inc(v.scalar_tensor_tensor(
                                tb5[:, :], d_ps[t2][:, :], sib[:, t2:t2 + 1],
                                swdst[oc % 2][:, :], ALU.mult, ALU.mult))
                            dvb.wt(v)

                @blk.tensor
                def _(te, dva0=dva0, dvb0=dvb0):
                    for oc in range(8):
                        for ic in range(16):
                            it = oc * 16 + ic
                            dva.wt(te, dva0 + it + 1)
                            if ic == 0 and oc >= 1:
                                dvb.wt(te, dvb0 + 2 * oc)
                            st = (ic == 0)
                            sp = (ic == 15)
                            for t2 in range(2):
                                qi_t = qiT[:, t2 * 2048 + ic * 128:
                                           t2 * 2048 + (ic + 1) * 128]
                                ins = te.matmul(d_ps[t2][:, :], qi_t,
                                                wdb[it % 2][:, :], start=st, stop=sp)
                                if t2 == 1:
                                    peb.inc(ins)

                @blk.gpsimd
                def _(gp, b=b, dvb0=dvb0, lda0=lda0, sta0=sta0):
                    for oc in range(8):
                        for t2 in range(2):
                            i2 = oc * 2 + t2
                            dvb.wt(gp, dvb0 + i2 + 1)
                            lda.wt(gp, lda0 + (i2 + 1) * 16)
                            if i2 >= 2:
                                sta.wt(gp, sta0 + (i2 - 1) * 16)
                            gpa.inc(gp.tensor_tensor(yb[i2 % 2][:, :], tb5[:, :],
                                                     x2r[t2][:, :], ALU.add))
                            gpa.wt(gp)
                            sta.dma(gp.dma_start(
                                out=y[(b * 2 + t2) * 128:(b * 2 + t2 + 1) * 128,
                                      oc * 512:(oc + 1) * 512],
                                in_=yb[i2 % 2][:, :]))
                    sta.wt(gp)

    with nc.Block() as blk:
        @blk.gpsimd
        def _(gp):
            sta.wt(gp)

    return nc


_CONST_CACHE = {}


def _host_constants():
    if "c" in _CONST_CACHE:
        return _CONST_CACHE["c"]
    f32 = np.float32
    freqs = 1.0 / (10000.0 ** (np.arange(0, 128, 2, dtype=np.float64) / 128.0))
    t = np.arange(S, dtype=np.float64)
    ang = np.outer(t, freqs)
    cos = np.ascontiguousarray(np.cos(ang).astype(f32))
    sin = np.ascontiguousarray(np.sin(ang).astype(f32))
    cosq = np.ascontiguousarray(np.tile(cos, (1, 4)))
    sinq = np.ascontiguousarray(np.tile(sin, (1, 4)))
    lane = np.arange(128)[:, None]
    qf = np.arange(512)[None, :]
    m = np.zeros((128, 4 * 512), dtype=f32)
    for j in range(4):
        m[:, j * 512:(j + 1) * 512] = np.where(128 * j + lane <= qf, 0.0, -1.0e9)
    identb = np.eye(128, dtype=ml_dtypes.bfloat16)
    identf = np.eye(128, dtype=f32)
    _CONST_CACHE["c"] = (cosq, sinq, cos, sin, m, identb, identf)
    return _CONST_CACHE["c"]


def kernel(L0_attn_x, L0_attn_wq, L0_attn_wk, L0_attn_wv, L0_attn_wo,
           L0_ffn_w_gate, L0_ffn_s_wgate, L0_ffn_w_up, L0_ffn_s_wup,
           L0_ffn_w_down, L0_ffn_s_wdown, L0_ffn_inv_gate, L0_ffn_inv_up,
           L0_ffn_inv_inter):
    f32 = np.float32
    x = np.ascontiguousarray(np.asarray(L0_attn_x, dtype=f32).reshape(NTOK, D))
    wq = np.asarray(L0_attn_wq, dtype=f32)
    wk = np.asarray(L0_attn_wk, dtype=f32)
    wv = np.asarray(L0_attn_wv, dtype=f32)
    wo_ = np.ascontiguousarray(np.asarray(L0_attn_wo, dtype=f32))
    wgT = np.ascontiguousarray(np.asarray(L0_ffn_w_gate, dtype=f32).T)
    wuT = np.ascontiguousarray(np.asarray(L0_ffn_w_up, dtype=f32).T)
    wdT = np.ascontiguousarray(np.asarray(L0_ffn_w_down, dtype=f32).T)
    bcast = lambda a, n: np.ascontiguousarray(
        np.broadcast_to(np.asarray(a, dtype=f32)[None, :], (128, n)))
    swg_ = bcast(L0_ffn_s_wgate, IFF)
    swu_ = bcast(L0_ffn_s_wup, IFF)
    swd_ = bcast(L0_ffn_s_wdown, D)
    invg_ = bcast(L0_ffn_inv_gate, D)
    invu_ = bcast(L0_ffn_inv_up, D)
    invi_ = bcast(L0_ffn_inv_inter, IFF)
    cosq, sinq, cosk, sink, masks_, identb, identf = _host_constants()

    nc = build_program()
    in_maps = []
    for c in range(NCORE):
        wqkv = np.ascontiguousarray(np.concatenate(
            [wq[:, 512 * c:512 * (c + 1)],
             wk[:, 128 * c:128 * (c + 1)],
             wv[:, 128 * c:128 * (c + 1)]], axis=1))
        in_maps.append({
            "x_my": np.ascontiguousarray(x[TPC * c:TPC * (c + 1)]),
            "wqkv": wqkv, "wo": wo_, "wgT": wgT, "wuT": wuT, "wdT": wdT,
            "swg": swg_, "swu": swu_, "swd": swd_,
            "invg": invg_, "invu": invu_, "invi": invi_,
            "cosq": cosq, "sinq": sinq, "cosk": cosk, "sink": sink,
            "masks": masks_, "identb": identb, "identf": identf,
            "col0": np.array([[TPC * c]], dtype=np.int32),
        })
    res = run_bass_kernel_spmd(nc, in_maps, core_ids=list(range(NCORE)))
    out = np.concatenate([r["y"] for r in res.results], axis=0)
    return out.reshape(2, S, D)


# version 09
# speedup vs baseline: 1.0061x; 1.0061x over previous
import contextlib

import numpy as np
import ml_dtypes

import concourse.bass as bass
import concourse.mybir as mybir
from concourse.bass_utils import run_bass_kernel_spmd

F32 = mybir.dt.float32
BF16 = mybir.dt.bfloat16
I32 = mybir.dt.int32
AF = mybir.ActivationFunctionType
ALU = mybir.AluOpType
AX = mybir.AxisListType

D = 4096
S = 4096
NTOK = 8192
NCORE = 8
TPC = 1024
IFF = 2048
EPS = 1e-5
MAGIC = float(1.5 * 2.0 ** 23)
SCL = float(1.0 / np.float32(np.sqrt(np.float32(128.0))))
FBLK = 256
NFB = TPC // FBLK


class Ctr:
    def __init__(self, sem):
        self.sem = sem
        self.v = 0

    def inc(self, instr, n=1):
        self.v += n
        instr.then_inc(self.sem, n)
        return instr

    def dma(self, instr):
        return self.inc(instr, 16)

    def wt(self, eng, v=None):
        eng.wait_ge(self.sem, self.v if v is None else v)


def build_program(phases=6, p2u=32, p5b=NFB, p5s=9, p5q=9, dbg=False):
    nc = bass.Bass("TRN2", target_bir_lowering=False)
    dbgk = "ExternalOutput" if dbg else "Internal"

    apd = lambda name, shape, dt, kind: nc.dram_tensor(name, shape, dt, kind=kind).ap()
    x_my = apd("x_my", [TPC, D], F32, "ExternalInput")
    wqkv = apd("wqkv", [D, 768], F32, "ExternalInput")
    wo = apd("wo", [D, D], F32, "ExternalInput")
    wgT = apd("wgT", [D, IFF], F32, "ExternalInput")
    wuT = apd("wuT", [D, IFF], F32, "ExternalInput")
    wdT = apd("wdT", [IFF, D], F32, "ExternalInput")
    swg = apd("swg", [128, IFF], F32, "ExternalInput")
    swu = apd("swu", [128, IFF], F32, "ExternalInput")
    swd = apd("swd", [128, D], F32, "ExternalInput")
    invg = apd("invg", [128, D], F32, "ExternalInput")
    invu = apd("invu", [128, D], F32, "ExternalInput")
    invi = apd("invi", [128, IFF], F32, "ExternalInput")
    cosq = apd("cosq", [S, 256], F32, "ExternalInput")
    sinq = apd("sinq", [S, 256], F32, "ExternalInput")
    cosk = apd("cosk", [S, 64], F32, "ExternalInput")
    sink = apd("sink", [S, 64], F32, "ExternalInput")
    masks = apd("masks", [128, 2048], F32, "ExternalInput")
    identb = apd("identb", [128, 128], BF16, "ExternalInput")
    identf = apd("identf", [128, 128], F32, "ExternalInput")
    col0 = apd("col0", [1, 1], I32, "ExternalInput")
    y = apd("y", [TPC, D], F32, "ExternalOutput")

    ag1_in = apd("ag1_in", [64 * 128, TPC], BF16, "Internal")
    ag1_out = apd("ag1_out", [NCORE * 64 * 128, TPC], BF16, "Internal")
    qt_sc = apd("qt_sc", [16 * 128, S], BF16, dbgk)
    ag2_in = apd("ag2_in", [512, NTOK], F32, "Internal")
    if dbg:
        ag2_cp = apd("ag2_cp", [512, NTOK], F32, "ExternalOutput")
        kv_cp = apd("kv_cp", [4 * 128, 64 * 128], BF16, "ExternalOutput")
        ag1_cp = apd("ag1_cp", [NCORE * 64 * 128, TPC], BF16, "ExternalOutput")
        qkr_cp = apd("qkr_cp", [64 * 128, 640], F32, "ExternalOutput")
    ag2_out = apd("ag2_out", [D, NTOK], F32, "Internal")
    x2_sc = apd("x2_sc", [TPC, D], F32, dbgk)
    if dbg:
        qg_sc = apd("qg_sc", [NFB * 128, 2 * D], BF16, "ExternalOutput")
        qu_sc = apd("qu_sc", [NFB * 128, 2 * D], BF16, "ExternalOutput")
        qi_sc = apd("qi_sc", [NFB * 128, 2 * IFF], BF16, "ExternalOutput")
        int_sc = apd("int_sc", [8 * 128, IFF], BF16, "ExternalOutput")
        scl_sc = apd("scl_sc", [NFB * 128, 6], F32, "ExternalOutput")

    sbl = lambda name, shape, dt=F32: nc.alloc_sbuf_tensor(name, shape, dt).ap()
    identb_s = sbl("identb_s", [128, 128], BF16)
    identf_s = sbl("identf_s", [128, 128], F32)
    onesrow = sbl("onesrow", [1, 128], F32)
    onescol = sbl("onescol", [128, 1], BF16)
    epsap = sbl("epsap", [128, 1], F32)

    es0 = contextlib.ExitStack()
    SEM = {n: Ctr(es0.enter_context(nc.semaphore(n))) for n in
           ["lda", "ldb", "ldc", "ldd", "pea", "peb", "pec",
            "dva", "dvb", "dvc", "aca", "acb", "gpa", "gpb",
            "sta", "stb", "cca",
            "lda2", "ldb2", "ldc2", "ldd2", "sta2"]}
    lda, ldb, ldc, ldd = SEM["lda"], SEM["ldb"], SEM["ldc"], SEM["ldd"]
    lda2, ldb2, ldc2, ldd2 = SEM["lda2"], SEM["ldb2"], SEM["ldc2"], SEM["ldd2"]
    sta2 = SEM["sta2"]
    pea, peb, pec = SEM["pea"], SEM["peb"], SEM["pec"]
    dva, dvb, dvc = SEM["dva"], SEM["dvb"], SEM["dvc"]
    aca, acb = SEM["aca"], SEM["acb"]
    gpa, gpb = SEM["gpa"], SEM["gpb"]
    sta, stb, cca = SEM["sta"], SEM["stb"], SEM["cca"]

    with nc.Block() as blk:
        @blk.sync
        def _(sync):
            lda.dma(sync.dma_start(out=identb_s[:, :], in_=identb[:, :]))
            lda.dma(sync.dma_start(out=identf_s[:, :], in_=identf[:, :]))
            lda.wt(sync)

        @blk.vector
        def _(v):
            v.memset(onesrow[:, :], 1.0)
            v.memset(onescol[:, :], 1.0)
            dva.inc(v.memset(epsap[:, :], EPS))
            dva.wt(v)

    es123 = contextlib.ExitStack()
    kbuf_h = es123.enter_context(nc.sbuf_tensor("kbuf_h", [128, 64 * 128], BF16)).ap()
    kbuf_l = es123.enter_context(nc.sbuf_tensor("kbuf_l", [128, 64 * 128], BF16)).ap()
    vbuf_h = es123.enter_context(nc.sbuf_tensor("vbuf_h", [128, 64 * 128], BF16)).ap()
    vbuf_l = es123.enter_context(nc.sbuf_tensor("vbuf_l", [128, 64 * 128], BF16)).ap()

    with contextlib.ExitStack() as es:
        sb = lambda n_, s_, d_=F32: es.enter_context(nc.sbuf_tensor(n_, s_, d_)).ap()
        ps = lambda n_, s_, d_=F32: es.enter_context(nc.psum_tensor(n_, s_, d_)).ap()
        xt = sb("p1_x", [128, D])
        xc = sb("p1_xc", [128, D])
        h1f = sb("p1_h1f", [128, D])
        sqj = sb("p1_sqj", [128, D], BF16)
        h1h = sb("p1_h1h", [128, D], BF16)
        h1l = sb("p1_h1l", [128, D], BF16)
        ssum = sb("p1_sum", [128, 1])
        mmean = sb("p1_m", [128, 1])
        ssq = sb("p1_ssq", [128, 1])
        sstd = sb("p1_std", [128, 1])
        rstd = sb("p1_rstd", [128, 1])
        stg1 = [sb(f"p1_stg{i}", [128, 1024], BF16) for i in range(2)]
        pst = [ps(f"p1_ps{i}", [128, 1024], BF16) for i in range(2)]

        for tt in range(8):
            with nc.Block() as blk:
                lda0, dva0, aca0, acb0, gpa0, gpb0, pea0, sta0, sta20 = (
                    lda.v, dva.v, aca.v, acb.v, gpa.v, gpb.v, pea.v, sta.v,
                    sta2.v)

                @blk.sync
                def _(sync, tt=tt):
                    lda.dma(sync.dma_start(out=xt[:, :],
                                           in_=x_my[tt * 128:(tt + 1) * 128, :]))

                @blk.vector
                def _(v, lda0=lda0, aca0=aca0):
                    lda.wt(v, lda0 + 16)
                    dva.inc(v.reduce_sum(ssum[:, :], xt[:, :], axis=AX.X))
                    dva.wt(v)
                    dva.inc(v.tensor_scalar(mmean[:, :], ssum[:, :], float(1.0 / D),
                                            None, ALU.mult))
                    dva.wt(v)
                    dva.inc(v.tensor_scalar(xc[:, :], xt[:, :], mmean[:, 0:1],
                                            None, ALU.subtract))
                    aca.wt(v, aca0 + 2)
                    dva.inc(v.reciprocal(rstd[:, :], sstd[:, :]))
                    dva.wt(v)
                    dva.inc(v.tensor_scalar(h1f[:, :], xc[:, :], rstd[:, 0:1],
                                            None, ALU.mult))

                @blk.scalar
                def _(sc, dva0=dva0, pea0=pea0, sta0=sta0, sta20=sta20):
                    dva.wt(sc, dva0 + 3)
                    aca.inc(sc.activation(sqj[:, :], xc[:, :], AF.Square,
                                          accum_out=ssq[:, 0:1]))
                    aca.wt(sc)
                    aca.inc(sc.activation(sstd[:, :], ssq[:, 0:1], AF.Sqrt,
                                          bias=epsap[:, 0:1], scale=float(1.0 / D)))
                    dva.wt(sc, dva0 + 5)
                    acb.inc(sc.copy(h1h[:, :], h1f[:, :]))
                    for g in range(8):
                        pea.wt(sc, pea0 + g + 1)
                        if g >= 2:
                            if g % 2 == 0:
                                sta.wt(sc, sta0 + (g // 2) * 128)
                            else:
                                sta2.wt(sc, sta20 + (g // 2) * 128)
                        acb.inc(sc.copy(stg1[g % 2][:, :], pst[g % 2][:, :]))

                @blk.gpsimd
                def _(gp, acb0=acb0):
                    acb.wt(gp, acb0 + 1)
                    gpa.inc(gp.tensor_tensor(h1l[:, :], h1f[:, :], h1h[:, :],
                                             ALU.subtract))

                @blk.tensor
                def _(te, acb0=acb0, gpa0=gpa0):
                    g = 0
                    for hl in range(2):
                        if hl == 0:
                            acb.wt(te, acb0 + 1)
                        else:
                            gpa.wt(te, gpa0 + 1)
                        src = h1h if hl == 0 else h1l
                        for grp in range(4):
                            if g >= 2:
                                acb.wt(te, acb0 + 1 + g - 1)
                            pb = pst[g % 2]
                            for j in range(8):
                                dc = grp * 8 + j
                                ins = te.transpose(pb[:, j * 128:(j + 1) * 128],
                                                   src[:, dc * 128:(dc + 1) * 128],
                                                   identb_s[:, :])
                                if j == 7:
                                    pea.inc(ins)
                            g += 1

                @blk.gpsimd
                def _(gp, tt=tt, acb0=acb0):
                    for g in range(8):
                        hl, grp = divmod(g, 4)
                        acb.wt(gp, acb0 + 2 + g)
                        stg = stg1[g % 2]
                        stsem = sta if g % 2 == 0 else sta2
                        for j in range(8):
                            dc = grp * 8 + j
                            row0 = (hl * 32 + dc) * 128
                            stsem.dma(gp.dma_start(
                                out=ag1_in[row0:row0 + 128,
                                           tt * 128:(tt + 1) * 128],
                                in_=stg[:, j * 128:(j + 1) * 128]))
                    sta.wt(gp)
                    sta2.wt(gp)

        with nc.Block() as blk:
            @blk.gpsimd
            def _(gp):
                cca.inc(gp.collective_compute(
                    "AllGather", ALU.bypass,
                    replica_groups=[list(range(NCORE))],
                    ins=[ag1_in[:, :]],
                    outs=[ag1_out[:, :]]))
                cca.wt(gp)

    if dbg:
        with nc.Block() as blk:
            @blk.gpsimd
            def _(gp):
                for r in range(16):
                    stb.dma(gp.dma_start(
                        out=ag1_cp[r * 4096:(r + 1) * 4096, :],
                        in_=ag1_out[r * 4096:(r + 1) * 4096, :]))
                stb.wt(gp)

    if phases <= 1:
        return nc
    with contextlib.ExitStack() as es:
        sb = lambda n_, s_, d_=F32: es.enter_context(nc.sbuf_tensor(n_, s_, d_)).ap()
        ps = lambda n_, s_, d_=F32: es.enter_context(nc.psum_tensor(n_, s_, d_)).ap()
        wqh = sb("p2_wqh", [128, 32 * 768], BF16)
        wql = sb("p2_wql", [128, 32 * 768], BF16)
        wstage = [sb(f"p2_wst{i}", [128, 768]) for i in range(2)]
        xst = [[sb(f"p2_x{h}_{i}", [128, 256], BF16) for i in range(2)]
               for h in range(2)]
        rope_t = [sb(f"p2_rt{i}", [128, 640]) for i in range(2)]
        re1 = sb("p2_re1", [128, 320])
        re2 = sb("p2_re2", [128, 320])
        ro1 = sb("p2_ro1", [128, 320])
        ro2 = sb("p2_ro2", [128, 320])
        qkr = sb("p2_qkr", [128, 640])
        qkh = sb("p2_qkh", [128, 640], BF16)
        qkl = sb("p2_qkl", [128, 640], BF16)
        qkstg = [sb(f"p2_qkstg{i}", [128, 512], BF16) for i in range(2)]
        q_ps = [ps(f"p2_qps{i}", [128, 512]) for i in range(2)]
        kv_ps = [ps(f"p2_kvps{i}", [128, 256]) for i in range(2)]
        pst2 = [ps(f"p2_pst{i}", [128, 640], BF16) for i in range(2)]

        with nc.Block() as blk:
            ldb0, ldb20, dva0 = ldb.v, ldb2.v, dva.v

            @blk.sync
            def _(sync, dva0=dva0):
                for dc in range(32):
                    if dc >= 2:
                        dva.wt(sync, dva0 + 2 * (dc - 1))
                    sem = ldb if dc % 2 == 0 else ldb2
                    sem.dma(sync.dma_start(out=wstage[dc % 2][:, :],
                                           in_=wqkv[dc * 128:(dc + 1) * 128, :]))

            @blk.vector
            def _(v, ldb0=ldb0, ldb20=ldb20):
                for dc in range(32):
                    if dc % 2 == 0:
                        ldb.wt(v, ldb0 + (dc // 2 + 1) * 16)
                    else:
                        ldb2.wt(v, ldb20 + (dc // 2 + 1) * 16)
                    dva.inc(v.tensor_copy(wqh[:, dc * 768:(dc + 1) * 768],
                                          wstage[dc % 2][:, :]))
                    dva.wt(v)
                    dva.inc(v.tensor_tensor(wql[:, dc * 768:(dc + 1) * 768],
                                            wstage[dc % 2][:, :],
                                            wqh[:, dc * 768:(dc + 1) * 768],
                                            ALU.subtract))

        for u in range(p2u):
            s_, qr = divmod(u, 4)
            hf2 = qr
            with nc.Block() as blk:
                ldc0, ldc20, ldd0, peb0, pec0 = (
                    ldc.v, ldc2.v, ldd.v, peb.v, pec.v)
                dva0, dvb0, dvc0 = dva.v, dvb.v, dvc.v
                aca0, acb0, gpa0, gpb0, stb0 = aca.v, acb.v, gpa.v, gpb.v, stb.v

                @blk.sync
                def _(sync, u=u, s_=s_, hf2=hf2, peb0=peb0):
                    for ti in range(2):
                        tg = u * 2 + ti
                        trow = (tg * 128) % S
                        buf = rope_t[ti]
                        ldd.dma(sync.dma_start(out=buf[:, 0:256],
                                               in_=cosq[trow:trow + 128, :]))
                        ldd.dma(sync.dma_start(out=buf[:, 256:512],
                                               in_=sinq[trow:trow + 128, :]))
                        ldd.dma(sync.dma_start(out=buf[:, 512:576],
                                               in_=cosk[trow:trow + 128, :]))
                        ldd.dma(sync.dma_start(out=buf[:, 576:640],
                                               in_=sink[trow:trow + 128, :]))
                    for dc in range(32):
                        if dc >= 2:
                            peb.wt(sync, peb0 + dc - 1)
                        sem = ldc if dc % 2 == 0 else ldc2
                        for hl in range(2):
                            r0 = (s_ * 64 + hl * 32 + dc) * 128
                            sem.dma(sync.dma_start(
                                out=xst[hl][dc % 2][:, :],
                                in_=ag1_out[r0:r0 + 128,
                                            hf2 * 256:(hf2 + 1) * 256]))

                @blk.tensor
                def _(te, ldc0=ldc0, ldc20=ldc20):
                    for dc in range(32):
                        if dc % 2 == 0:
                            ldc.wt(te, ldc0 + (dc // 2 + 1) * 32)
                        else:
                            ldc2.wt(te, ldc20 + (dc // 2 + 1) * 32)
                        xh, xl = xst[0][dc % 2], xst[1][dc % 2]
                        wh = wqh[:, dc * 768:(dc + 1) * 768]
                        wl = wql[:, dc * 768:(dc + 1) * 768]
                        st = (dc == 0)
                        sp = (dc == 31)
                        for ti in range(2):
                            lwh = xh[:, ti * 128:(ti + 1) * 128]
                            lwl = xl[:, ti * 128:(ti + 1) * 128]
                            qp = q_ps[ti]
                            kp = kv_ps[ti]
                            te.matmul(qp[:, :], lwh, wh[:, 0:512], start=st, stop=False)
                            te.matmul(kp[:, :], lwh, wh[:, 512:768], start=st, stop=False)
                            te.matmul(qp[:, :], lwh, wl[:, 0:512], start=False, stop=False)
                            te.matmul(kp[:, :], lwh, wl[:, 512:768], start=False, stop=False)
                            te.matmul(qp[:, :], lwl, wh[:, 0:512], start=False, stop=sp)
                            ins = te.matmul(kp[:, :], lwl, wh[:, 512:768],
                                            start=False, stop=sp)
                            if ti == 1:
                                peb.inc(ins)

                @blk.vector
                def _(v, u=u, peb0=peb0, ldd0=ldd0, gpa0=gpa0, acb0=acb0,
                      pec0=pec0, stb0=stb0):
                    peb.wt(v, peb0 + 32)
                    ldd.wt(v, ldd0 + 8 * 16)
                    for ti in range(2):
                        tg = u * 2 + ti
                        if ti >= 1:
                            gpa.wt(v, gpa0 + 4 * ti)
                        buf = rope_t[ti]
                        qp = q_ps[ti]
                        kp = kv_ps[ti]
                        v.tensor_tensor(re1[:, 0:256], qp[:, 0:512:2], buf[:, 0:256], ALU.mult)
                        v.tensor_tensor(re2[:, 0:256], qp[:, 1:512:2], buf[:, 256:512], ALU.mult)
                        v.tensor_tensor(ro1[:, 0:256], qp[:, 0:512:2], buf[:, 256:512], ALU.mult)
                        v.tensor_tensor(ro2[:, 0:256], qp[:, 1:512:2], buf[:, 0:256], ALU.mult)
                        v.tensor_tensor(re1[:, 256:320], kp[:, 0:128:2],
                                        buf[:, 512:576], ALU.mult)
                        v.tensor_tensor(re2[:, 256:320], kp[:, 1:128:2],
                                        buf[:, 576:640], ALU.mult)
                        v.tensor_tensor(ro1[:, 256:320], kp[:, 0:128:2],
                                        buf[:, 576:640], ALU.mult)
                        dva.inc(v.tensor_tensor(ro2[:, 256:320], kp[:, 1:128:2],
                                                buf[:, 512:576], ALU.mult))
                        dvb.inc(v.tensor_copy(vbuf_h[:, tg * 128:(tg + 1) * 128],
                                              kp[:, 128:256]))
                        dvb.wt(v)
                        dvb.inc(v.tensor_tensor(
                            vbuf_l[:, tg * 128:(tg + 1) * 128],
                            kp[:, 128:256],
                            vbuf_h[:, tg * 128:(tg + 1) * 128], ALU.subtract))
                        acb.wt(v, acb0 + ti + 1)
                        dvc.inc(v.tensor_tensor(qkl[:, :], qkr[:, :], qkh[:, :],
                                                ALU.subtract))
                        pec.wt(v, pec0 + 2 * (ti + 1))
                        if ti >= 1:
                            stb.wt(v, stb0 + 8 * 16 * ti)
                        for hl in range(2):
                            pb = pst2[hl]
                            dvb.inc(v.tensor_copy(qkstg[hl][:, :], pb[:, 0:512]))
                            kdst = kbuf_h if hl == 0 else kbuf_l
                            dvb.inc(v.tensor_copy(
                                kdst[:, tg * 128:(tg + 1) * 128], pb[:, 512:640]))

                @blk.gpsimd
                def _(gp, u=u, dva0=dva0, acb0=acb0, dvc0=dvc0):
                    for ti in range(2):
                        dva.wt(gp, dva0 + ti + 1)
                        if ti >= 1:
                            acb.wt(gp, acb0 + ti)
                            dvc.wt(gp, dvc0 + ti)
                            if dbg:
                                stb.wt(gp)
                        gpa.inc(gp.tensor_tensor(qkr[:, 0:512:2], re1[:, 0:256],
                                                 re2[:, 0:256], ALU.subtract))
                        gpa.wt(gp)
                        gpa.inc(gp.tensor_tensor(qkr[:, 1:512:2], ro1[:, 0:256],
                                                 ro2[:, 0:256], ALU.add))
                        gpa.wt(gp)
                        gpa.inc(gp.tensor_tensor(qkr[:, 512:640:2], re1[:, 256:320],
                                                 re2[:, 256:320], ALU.subtract))
                        gpa.wt(gp)
                        gpa.inc(gp.tensor_tensor(qkr[:, 513:640:2], ro1[:, 256:320],
                                                 ro2[:, 256:320], ALU.add))
                        if dbg:
                            gpa.wt(gp)
                            tg = u * 2 + ti
                            stb.dma(gp.dma_start(
                                out=qkr_cp[tg * 128:(tg + 1) * 128, :],
                                in_=qkr[:, :]))
                            if ti == 1:
                                stb.wt(gp)

                @blk.scalar
                def _(sc, u=u, gpa0=gpa0, dvc0=dvc0, pec0=pec0):
                    for ti in range(2):
                        gpa.wt(sc, gpa0 + 4 * (ti + 1))
                        if ti >= 1:
                            dvc.wt(sc, dvc0 + ti)
                            pec.wt(sc, pec0 + 2 * ti)
                        acb.inc(sc.copy(qkh[:, :], qkr[:, :]))

                @blk.tensor
                def _(te, dvc0=dvc0, dvb0=dvb0):
                    for ti in range(2):
                        dvc.wt(te, dvc0 + ti + 1)
                        if ti >= 1:
                            te.wait_ge(SEM["dvb"].sem, dvb0 + 6 * ti)
                        for hl, srcb in ((0, qkh), (1, qkl)):
                            pb = pst2[hl]
                            for j in range(5):
                                ins = te.transpose(pb[:, j * 128:(j + 1) * 128],
                                                   srcb[:, j * 128:(j + 1) * 128],
                                                   identb_s[:, :])
                                if j == 4:
                                    pec.inc(ins)

                @blk.gpsimd
                def _(gp, u=u, dvb0=dvb0):
                    for ti in range(2):
                        tg = u * 2 + ti
                        p_ = tg // 32
                        tok0 = (tg % 32) * 128
                        for hl in range(2):
                            gp.wait_ge(SEM["dvb"].sem, dvb0 + 6 * ti + 3 + 2 * hl)
                            for h in range(4):
                                r0 = (p_ * 8 + h * 2 + hl) * 128
                                stb.dma(gp.dma_start(
                                    out=qt_sc[r0:r0 + 128, tok0:tok0 + 128],
                                    in_=qkstg[hl][:, h * 128:(h + 1) * 128]))
                    stb.wt(gp)

    if dbg:
        with nc.Block() as blk:
            @blk.gpsimd
            def _(gp):
                for i, src in enumerate((kbuf_h, kbuf_l, vbuf_h, vbuf_l)):
                    stb.dma(gp.dma_start(
                        out=kv_cp[i * 128:(i + 1) * 128, :], in_=src[:, :]))
                stb.wt(gp)

    with contextlib.ExitStack() as es:
        sb = lambda n_, s_, d_=F32: es.enter_context(nc.sbuf_tensor(n_, s_, d_)).ap()
        ps = lambda n_, s_, d_=F32: es.enter_context(nc.psum_tensor(n_, s_, d_)).ap()
        maskb = sb("p3_mask", [128, 2048])
        qgh = sb("p3_qh", [128, 512], BF16)
        qgl = sb("p3_ql", [128, 512], BF16)
        sbuf_s = sb("p3_s", [128, 32 * 512])
        mx8 = sb("p3_mx8", [128, 8])
        mcol = sb("p3_mcol", [128, 1])
        negm = sb("p3_negm", [1, 1])
        bias_s = sb("p3_bias", [128, 1])
        e32 = [sb(f"p3_e32{i}", [128, 512]) for i in range(2)]
        ehb = [sb(f"p3_eh{i}", [128, 512], BF16) for i in range(3)]
        elb = [sb(f"p3_el{i}", [128, 512], BF16) for i in range(3)]
        dn_row = sb("p3_dnr", [1, 512])
        bc_sb = sb("p3_bcsb", [128, 512])
        ao32 = sb("p3_ao", [128, 512])
        s_ps = [ps(f"p3_sps{i}", [128, 512]) for i in range(2)]
        av_ps = ps("p3_avps", [128, 512])
        dn_ps = ps("p3_dnps", [1, 512])
        mt_ps = ps("p3_mtps", [1, 128])
        nb_ps = ps("p3_nbps", [128, 1])
        bc_ps = ps("p3_bcps", [128, 512])

        with nc.Block() as blk:
            @blk.sync
            def _(sync):
                ldd.dma(sync.dma_start(out=maskb[:, :], in_=masks[:, :]))
                ldd.wt(sync)

        for p_ in range(2):
            for h in range(4):
                for g in range(8):
                    L = 4 * g + 4
                    with nc.Block() as blk:
                        lda0, pea0, peb0, pec0 = lda.v, pea.v, peb.v, pec.v
                        dva0, dvb0, dvc0 = dva.v, dvb.v, dvc.v
                        aca0, acb0, gpa0, gpb0 = aca.v, acb.v, gpa.v, gpb.v

                        @blk.sync
                        def _(sync, p_=p_, h=h, g=g):
                            r0 = (p_ * 8 + h * 2) * 128
                            c0 = g * 512
                            lda.dma(sync.dma_start(
                                out=qgh[:, :], in_=qt_sc[r0:r0 + 128, c0:c0 + 512]))
                            lda.dma(sync.dma_start(
                                out=qgl[:, :],
                                in_=qt_sc[r0 + 128:r0 + 256, c0:c0 + 512]))

                        @blk.tensor
                        def _(te, p_=p_, L=L, lda0=lda0, dva0=dva0):
                            lda.wt(te, lda0 + 32)
                            for kt in range(L):
                                tile = 32 * p_ + kt
                                pb = s_ps[kt % 2]
                                if kt >= 2:
                                    dva.wt(te, dva0 + kt - 1)
                                kh = kbuf_h[:, tile * 128:(tile + 1) * 128]
                                kl = kbuf_l[:, tile * 128:(tile + 1) * 128]
                                te.matmul(pb[:, :], kh, qgh[:, :], start=True, stop=False)
                                te.matmul(pb[:, :], kh, qgl[:, :], start=False, stop=False)
                                pea.inc(te.matmul(pb[:, :], kl, qgh[:, :],
                                                  start=False, stop=True))

                        @blk.vector
                        def _(v, g=g, L=L, pea0=pea0):
                            for kt in range(L):
                                pea.wt(v, pea0 + kt + 1)
                                pb = s_ps[kt % 2]
                                dst = sbuf_s[:, kt * 512:(kt + 1) * 512]
                                if kt >= 4 * g:
                                    jm = kt - 4 * g
                                    dva.inc(v.scalar_tensor_tensor(
                                        dst, pb[:, :], SCL,
                                        maskb[:, jm * 512:(jm + 1) * 512],
                                        ALU.mult, ALU.add))
                                else:
                                    dva.inc(v.tensor_scalar(dst, pb[:, :], SCL,
                                                            None, ALU.mult))
                            nch = (L * 512) // 2048
                            for j in range(nch):
                                dvb.inc(v.reduce_max(
                                    mx8[:, j:j + 1],
                                    sbuf_s[:, j * 2048:(j + 1) * 2048], axis=AX.X))
                            dvb.wt(v)
                            dvb.inc(v.reduce_max(mcol[:, :], mx8[:, 0:nch], axis=AX.X))

                        @blk.tensor
                        def _(te, dvb0=dvb0, L=L):
                            nch = (L * 512) // 2048
                            dvb.wt(te, dvb0 + nch + 1)
                            pec.inc(te.transpose(mt_ps[:, :], mcol[:, :],
                                                 identf_s[:, :]))

                        @blk.vector
                        def _(v, pec0=pec0):
                            pec.wt(v, pec0 + 1)
                            dvc.inc(v.reduce_max(negm[:, :], mt_ps[0:1, :], axis=AX.X,
                                                 negate=True))

                        @blk.tensor
                        def _(te, dvc0=dvc0):
                            dvc.wt(te, dvc0 + 1)
                            pec.inc(te.matmul(nb_ps[:, :], onesrow[:, :], negm[:, :],
                                              start=True, stop=True))

                        @blk.vector
                        def _(v, pec0=pec0):
                            pec.wt(v, pec0 + 2)
                            dvc.inc(v.tensor_copy(bias_s[:, :], nb_ps[:, :]))

                        @blk.scalar
                        def _(sc, L=L, dvc0=dvc0, dva0=dva0, gpb0=gpb0, peb0=peb0):
                            dvc.wt(sc, dvc0 + 2)
                            for kt in range(L):
                                dva.wt(sc, dva0 + kt + 1)
                                if kt >= 2:
                                    gpb.wt(sc, gpb0 + kt - 1)
                                if kt >= 3:
                                    peb.wt(sc, peb0 + kt - 2)
                                aca.inc(sc.activation(
                                    e32[kt % 2][:, :],
                                    sbuf_s[:, kt * 512:(kt + 1) * 512],
                                    AF.Exp, bias=bias_s[:, 0:1], scale=1.0))
                                aca.wt(sc)
                                acb.inc(sc.copy(ehb[kt % 3][:, :], e32[kt % 2][:, :]))

                        @blk.gpsimd
                        def _(gp, L=L, acb0=acb0, peb0=peb0):
                            for kt in range(L):
                                acb.wt(gp, acb0 + kt + 1)
                                if kt >= 3:
                                    peb.wt(gp, peb0 + kt - 2)
                                gpb.inc(gp.tensor_tensor(
                                    elb[kt % 3][:, :], e32[kt % 2][:, :],
                                    ehb[kt % 3][:, :], ALU.subtract))

                        @blk.tensor
                        def _(te, p_=p_, L=L, gpb0=gpb0):
                            for kt in range(L):
                                tile = 32 * p_ + kt
                                gpb.wt(te, gpb0 + kt + 1)
                                vh = vbuf_h[:, tile * 128:(tile + 1) * 128]
                                vl = vbuf_l[:, tile * 128:(tile + 1) * 128]
                                eh_, el_ = ehb[kt % 3], elb[kt % 3]
                                st = (kt == 0)
                                sp = (kt == L - 1)
                                te.matmul(av_ps[:, :], vh, eh_[:, :], start=st, stop=False)
                                te.matmul(av_ps[:, :], vh, el_[:, :], start=False,
                                          stop=False)
                                te.matmul(av_ps[:, :], vl, eh_[:, :], start=False, stop=sp)
                                te.matmul(dn_ps[:, :], onescol[:, :], eh_[:, :],
                                          start=st, stop=False)
                                peb.inc(te.matmul(dn_ps[:, :], onescol[:, :], el_[:, :],
                                                  start=False, stop=sp))

                        @blk.vector
                        def _(v, L=L, peb0=peb0):
                            peb.wt(v, peb0 + L)
                            dvc.inc(v.reciprocal(dn_row[:, :], dn_ps[0:1, :]))

                        @blk.tensor
                        def _(te, dvc0=dvc0):
                            dvc.wt(te, dvc0 + 3)
                            pec.inc(te.matmul(bc_ps[:, :], onesrow[:, :], dn_row[:, :],
                                              start=True, stop=True))

                        @blk.vector
                        def _(v, pec0=pec0):
                            pec.wt(v, pec0 + 3)
                            dvc.inc(v.tensor_copy(bc_sb[:, :], bc_ps[:, :]))
                            dvc.wt(v)
                            dvc.inc(v.tensor_tensor(ao32[:, :], av_ps[:, :],
                                                    bc_sb[:, :], ALU.mult))

                        @blk.gpsimd
                        def _(gp, p_=p_, h=h, g=g, dvc0=dvc0):
                            dvc.wt(gp, dvc0 + 5)
                            r0 = h * 128
                            c0 = p_ * S + g * 512
                            stb.dma(gp.dma_start(
                                out=ag2_in[r0:r0 + 128, c0:c0 + 512], in_=ao32[:, :]))
                            if dbg:
                                stb.dma(gp.dma_start(
                                    out=ag2_cp[r0:r0 + 128, c0:c0 + 512],
                                    in_=ao32[:, :]))
                            stb.wt(gp)

        with nc.Block() as blk:
            @blk.gpsimd
            def _(gp):
                cca.inc(gp.collective_compute(
                    "AllGather", ALU.bypass,
                    replica_groups=[list(range(NCORE))],
                    ins=[ag2_in[:, :]],
                    outs=[ag2_out[:, :]]))
                cca.wt(gp)

    es123.close()
    if phases <= 3:
        return nc

    with contextlib.ExitStack() as es:
        sb = lambda n_, s_, d_=F32: es.enter_context(nc.sbuf_tensor(n_, s_, d_)).ap()
        ps = lambda n_, s_, d_=F32: es.enter_context(nc.psum_tensor(n_, s_, d_)).ap()
        aoh = sb("p4_aoh", [128, 32 * 1024], BF16)
        aol = sb("p4_aol", [128, 32 * 1024], BF16)
        aost = [sb(f"p4_aost{i}", [128, 4096]) for i in range(2)]
        c0sb = sb("p4_c0", [1, 1], I32)
        wst = [sb(f"p4_wst{i}", [128, 512]) for i in range(2)]
        woh = [sb(f"p4_woh{i}", [128, 512], BF16) for i in range(2)]
        wol = [sb(f"p4_wol{i}", [128, 512], BF16) for i in range(2)]
        xres = [sb(f"p4_xr{i}", [128, 512]) for i in range(2)]
        x2st = [sb(f"p4_x2{i}", [128, 512]) for i in range(2)]
        wo_ps = [ps(f"p4_wops{i}", [128, 512]) for i in range(8)]

        with nc.Block() as blk:
            ldb0, ldb20, aca0, dva0 = ldb.v, ldb2.v, aca.v, dva.v

            @blk.sync
            def _(sync, ldb0=ldb0, dva0=dva0):
                ldb.dma(sync.dma_start(out=c0sb[:, :], in_=col0[:, :]))
                ldb.wt(sync, ldb0 + 16)
                ag2v = ag2_out.rearrange("(g a p) c -> g p a c", p=128, a=4)
                with sync.register("p4off") as off_reg:
                    sync.reg_load(off_reg, c0sb[0:1, 0:1])
                    off = sync.snap(off_reg)
                    for g4 in range(8):
                        if g4 >= 2:
                            dva.wt(sync, dva0 + 4 * (g4 - 1))
                        dst = aost[g4 % 2].rearrange("p (a b) -> p a b", a=4)
                        sem = ldb if g4 % 2 == 0 else ldb2
                        sem.dma(sync.dma_start(
                            out=dst, in_=ag2v[g4, :, :, bass.ds(off, 1024)]))

            @blk.scalar
            def _(sc, ldb0=ldb0, ldb20=ldb20):
                for g4 in range(8):
                    if g4 % 2 == 0:
                        ldb.wt(sc, ldb0 + 16 + (g4 // 2 + 1) * 16)
                    else:
                        ldb2.wt(sc, ldb20 + (g4 // 2 + 1) * 16)
                    for j in range(4):
                        dc = g4 * 4 + j
                        aca.inc(sc.copy(aoh[:, dc * 1024:(dc + 1) * 1024],
                                        aost[g4 % 2][:, j * 1024:(j + 1) * 1024]))

            @blk.vector
            def _(v, aca0=aca0):
                for g4 in range(8):
                    for j in range(4):
                        dc = g4 * 4 + j
                        aca.wt(v, aca0 + dc + 1)
                        dva.inc(v.tensor_tensor(aol[:, dc * 1024:(dc + 1) * 1024],
                                                aost[g4 % 2][:, j * 1024:(j + 1) * 1024],
                                                aoh[:, dc * 1024:(dc + 1) * 1024],
                                                ALU.subtract))

        for oc in range(8):
            with nc.Block() as blk:
                ldc0, ldc20, lda0, lda20, dva0, dvb0, peb0, sta0, sta20 = (
                    ldc.v, ldc2.v, lda.v, lda2.v, dva.v, dvb.v, peb.v, sta.v,
                    sta2.v)

                @blk.sync
                def _(sync, oc=oc, dva0=dva0, dvb0=dvb0):
                    for dc in range(32):
                        if dc >= 2:
                            dva.wt(sync, dva0 + 2 * (dc - 1))
                        sem = ldc if dc % 2 == 0 else ldc2
                        sem.dma(sync.dma_start(
                            out=wst[dc % 2][:, :],
                            in_=wo[dc * 128:(dc + 1) * 128,
                                   oc * 512:(oc + 1) * 512]))
                    for tt in range(8):
                        if tt >= 2:
                            dvb.wt(sync, dvb0 + tt - 1)
                        sem = lda if tt % 2 == 0 else lda2
                        sem.dma(sync.dma_start(
                            out=xres[tt % 2][:, :],
                            in_=x_my[tt * 128:(tt + 1) * 128,
                                     oc * 512:(oc + 1) * 512]))

                @blk.vector
                def _(v, ldc0=ldc0, ldc20=ldc20, peb0=peb0):
                    for dc in range(32):
                        if dc % 2 == 0:
                            ldc.wt(v, ldc0 + (dc // 2 + 1) * 16)
                        else:
                            ldc2.wt(v, ldc20 + (dc // 2 + 1) * 16)
                        if dc >= 2:
                            peb.wt(v, peb0 + dc - 1)
                        dva.inc(v.tensor_copy(woh[dc % 2][:, :], wst[dc % 2][:, :]))
                        dva.wt(v)
                        dva.inc(v.tensor_tensor(wol[dc % 2][:, :], wst[dc % 2][:, :],
                                                woh[dc % 2][:, :], ALU.subtract))

                @blk.tensor
                def _(te, dva0=dva0):
                    for dc in range(32):
                        dva.wt(te, dva0 + 2 * (dc + 1))
                        st = (dc == 0)
                        sp = (dc == 31)
                        for tt in range(8):
                            ah = aoh[:, dc * 1024 + tt * 128:dc * 1024 + (tt + 1) * 128]
                            al = aol[:, dc * 1024 + tt * 128:dc * 1024 + (tt + 1) * 128]
                            te.matmul(wo_ps[tt][:, :], ah, woh[dc % 2][:, :],
                                      start=st, stop=False)
                            te.matmul(wo_ps[tt][:, :], ah, wol[dc % 2][:, :],
                                      start=False, stop=False)
                            ins = te.matmul(wo_ps[tt][:, :], al, woh[dc % 2][:, :],
                                            start=False, stop=sp)
                            if tt == 7:
                                peb.inc(ins)

                @blk.vector
                def _(v, peb0=peb0, lda0=lda0, lda20=lda20,
                      sta0=sta0, sta20=sta20):
                    for tt in range(8):
                        peb.wt(v, peb0 + 32)
                        if tt % 2 == 0:
                            lda.wt(v, lda0 + (tt // 2 + 1) * 16)
                        else:
                            lda2.wt(v, lda20 + (tt // 2 + 1) * 16)
                        if tt >= 2:
                            if tt % 2 == 0:
                                sta.wt(v, sta0 + (tt // 2) * 16)
                            else:
                                sta2.wt(v, sta20 + (tt // 2) * 16)
                        dvb.inc(v.tensor_tensor(x2st[tt % 2][:, :], wo_ps[tt][:, :],
                                                xres[tt % 2][:, :], ALU.add))

                @blk.gpsimd
                def _(gp, oc=oc, dvb0=dvb0):
                    for tt in range(8):
                        dvb.wt(gp, dvb0 + tt + 1)
                        sem = sta if tt % 2 == 0 else sta2
                        sem.dma(gp.dma_start(
                            out=x2_sc[tt * 128:(tt + 1) * 128,
                                      oc * 512:(oc + 1) * 512],
                            in_=x2st[tt % 2][:, :]))
                    sta.wt(gp)
                    sta2.wt(gp)

    if phases <= 4:
        return nc
    with contextlib.ExitStack() as es:
        sb = lambda n_, s_, d_=F32: es.enter_context(nc.sbuf_tensor(n_, s_, d_)).ap()
        ps = lambda n_, s_, d_=F32: es.enter_context(nc.psum_tensor(n_, s_, d_)).ap()
        swg_s = sb("p5_swg", [128, IFF])
        swu_s = sb("p5_swu", [128, IFF])
        invg_s = sb("p5_invg", [128, D])
        invu_s = sb("p5_invu", [128, D])
        invi_s = sb("p5_invi", [128, IFF])
        xsg = sb("p5_xsg", [128, D])
        xsu = sb("p5_xsu", [128, D])
        h2 = sb("p5_h2", [128, D], BF16)
        st5 = {n_: sb(f"p5_{n_}", [128, 1]) for n_ in
               ["sum", "m", "ssq", "std", "rstd"]}
        qgb = sb("p5_qgb", [128, D], BF16)
        qub = sb("p5_qub", [128, D], BF16)
        rcpc = sb("p5_rcpc", [128, 1])
        sgb = sb("p5_sgb", [128, 2])
        sub = sb("p5_sub", [128, 2])
        sib = sb("p5_sib", [128, 2])
        qgT = sb("p5_qgT", [128, 32 * FBLK], BF16)
        quT = sb("p5_quT", [128, 32 * FBLK], BF16)
        qiT = sb("p5_qiT", [128, 16 * FBLK], BF16)
        interb = [sb(f"p5_int{i}", [128, IFF], BF16) for i in range(2)]
        work2048 = sb("p5_work", [128, IFF])
        xsi = work2048
        gbuf = work2048[:, 0:512]
        silb = work2048[:, 512:1024]
        ubuf = work2048[:, 1024:1536]
        ibuf = work2048[:, 1536:2048]
        qib = sb("p5_qib", [128, IFF], BF16)
        wst5 = [sb(f"p5_wst{i}", [128, 1024]) for i in range(2)]
        wgb = [sb(f"p5_wgb{i}", [128, 512], BF16) for i in range(2)]
        wub = [sb(f"p5_wub{i}", [128, 512], BF16) for i in range(2)]
        wdst = [wst5[i][:, 0:512] for i in range(2)]
        wdb = [sb(f"p5_wdb{i}", [128, 512], BF16) for i in range(2)]
        swdst = [sb(f"p5_swdst{i}", [128, 512]) for i in range(2)]
        x2r = [sb(f"p5_x2r{i}", [128, 512]) for i in range(2)]
        tb5 = sb("p5_tb", [128, 512])
        yb = [sb(f"p5_yb{i}", [128, 512]) for i in range(2)]
        gu_ps = [ps(f"p5_gups{i}", [128, 512]) for i in range(4)]
        d_ps = [ps(f"p5_dps{i}", [128, 512]) for i in range(2)]
        pst5 = [ps(f"p5_pst{i}", [128, 1024], BF16) for i in range(2)]

        with nc.Block() as blk:
            @blk.sync
            def _(sync):
                for dst, src in ((swg_s, swg), (swu_s, swu),
                                 (invg_s, invg), (invu_s, invu), (invi_s, invi)):
                    ldd.dma(sync.dma_start(out=dst[:, :], in_=src[:, :]))
                ldd.wt(sync)

        def quant(eng, ctr, src_bf, inv_s, xs_, sc_col, qb_, n):
            ctr.inc(eng.tensor_tensor(xs_[:, 0:n], src_bf[:, 0:n], inv_s[:, 0:n],
                                      ALU.mult))
            ctr.wt(eng)
            ctr.inc(eng.reduce_max(sc_col, xs_[:, 0:n], axis=AX.X,
                                   apply_absolute_value=True))
            ctr.wt(eng)
            ctr.inc(eng.tensor_scalar(sc_col, sc_col, float(1.0 / 127.0), 1e-8,
                                      ALU.mult, ALU.max))
            ctr.wt(eng)
            ctr.inc(eng.reciprocal(rcpc[:, :], sc_col))
            ctr.wt(eng)
            ctr.inc(eng.tensor_scalar(xs_[:, 0:n], xs_[:, 0:n], rcpc[:, 0:1], MAGIC,
                                      ALU.mult, ALU.add))
            ctr.wt(eng)
            ctr.inc(eng.tensor_scalar(xs_[:, 0:n], xs_[:, 0:n], MAGIC, 127.0,
                                      ALU.subtract, ALU.min))
            ctr.wt(eng)
            ctr.inc(eng.tensor_scalar(qb_[:, 0:n], xs_[:, 0:n], -127.0, None,
                                      ALU.max))
            ctr.wt(eng)

        for b in range(p5b):
            for t2 in range(2):
                tt = b * 2 + t2
                with nc.Block() as blk:
                    lda0, dva0, aca0, acb0, gpa0, pea0, gpb0 = (
                        lda.v, dva.v, aca.v, acb.v, gpa.v, pea.v, gpb.v)

                    @blk.sync
                    def _(sync, tt=tt):
                        lda.dma(sync.dma_start(
                            out=xsu[:, :], in_=x2_sc[tt * 128:(tt + 1) * 128, :]))

                    @blk.vector
                    def _(v, lda0=lda0, aca0=aca0, t2=t2):
                        lda.wt(v, lda0 + 16)
                        dva.inc(v.reduce_sum(st5["sum"][:, :], xsu[:, :], axis=AX.X))
                        dva.wt(v)
                        dva.inc(v.tensor_scalar(st5["m"][:, :], st5["sum"][:, :],
                                                float(1.0 / D), None, ALU.mult))
                        dva.wt(v)
                        dva.inc(v.tensor_scalar(xsg[:, :], xsu[:, :], st5["m"][:, 0:1],
                                                None, ALU.subtract))
                        aca.wt(v, aca0 + 2)
                        dva.inc(v.reciprocal(st5["rstd"][:, :], st5["std"][:, :]))
                        dva.wt(v)
                        dva.inc(v.tensor_scalar(h2[:, :], xsg[:, :],
                                                st5["rstd"][:, 0:1], None, ALU.mult))
                        dva.wt(v)
                        if p5q >= 1:
                            quant(v, dva, h2, invg_s, xsg, sgb[:, t2:t2 + 1], qgb, D)
                        if p5q >= 2:
                            quant(v, dva, h2, invu_s, xsu, sub[:, t2:t2 + 1], qub, D)

                    @blk.scalar
                    def _(sc, t2=t2, dva0=dva0, pea0=pea0):
                        dva.wt(sc, dva0 + 3)
                        aca.inc(sc.activation(h2[:, :], xsg[:, :], AF.Square,
                                              accum_out=st5["ssq"][:, 0:1]))
                        aca.wt(sc)
                        aca.inc(sc.activation(st5["std"][:, :], st5["ssq"][:, 0:1],
                                              AF.Sqrt, bias=epsap[:, 0:1],
                                              scale=float(1.0 / D)))
                        if p5q < 3:
                            return
                        gi = 0
                        for dstT in (qgT, quT):
                            for grp in range(4):
                                gi += 1
                                pea.wt(sc, pea0 + gi)
                                pb = pst5[(gi - 1) % 2]
                                c_ = t2 * 4096 + grp * 1024
                                acb.inc(sc.copy(dstT[:, c_:c_ + 1024], pb[:, :]))

                    @blk.tensor
                    def _(te, dva0=dva0, acb0=acb0):
                        if p5q < 3:
                            return
                        gi = 0
                        for src in (qgb, qub):
                            if src is qgb:
                                dva.wt(te, dva0 + 12)
                            else:
                                dva.wt(te, dva0 + 19)
                            for grp in range(4):
                                if gi >= 2:
                                    acb.wt(te, acb0 + gi - 1)
                                pb = pst5[gi % 2]
                                for j in range(8):
                                    dc = grp * 8 + j
                                    ins = te.transpose(
                                        pb[:, j * 128:(j + 1) * 128],
                                        src[:, dc * 128:(dc + 1) * 128],
                                        identb_s[:, :])
                                    if j == 7:
                                        pea.inc(ins)
                                gi += 1

            if p5s < 2:
                continue
            with nc.Block() as blk:
                ldb0, ldb20, dva0, dvb0, aca0, peb0 = (
                    ldb.v, ldb2.v, dva.v, dvb.v, aca.v, peb.v)

                @blk.sync
                def _(sync, dva0=dva0):
                    for ic in range(4):
                        for hc in range(32):
                            it = ic * 32 + hc
                            if it >= 2:
                                dva.wt(sync, dva0 + 2 * (it - 1))
                            sem = ldb if it % 2 == 0 else ldb2
                            sem.dma(sync.dma_start(
                                out=wst5[it % 2][:, 0:512],
                                in_=wgT[hc * 128:(hc + 1) * 128,
                                        ic * 512:(ic + 1) * 512]))
                            sem.dma(sync.dma_start(
                                out=wst5[it % 2][:, 512:1024],
                                in_=wuT[hc * 128:(hc + 1) * 128,
                                        ic * 512:(ic + 1) * 512]))

                @blk.vector
                def _(v, ldb0=ldb0, ldb20=ldb20, peb0=peb0, aca0=aca0):
                    for ic in range(4):
                        for hc in range(32):
                            it = ic * 32 + hc
                            if it % 2 == 0:
                                ldb.wt(v, ldb0 + (it // 2 + 1) * 32)
                            else:
                                ldb2.wt(v, ldb20 + (it // 2 + 1) * 32)
                            if it >= 2:
                                peb.wt(v, peb0 + it - 1)
                            dva.inc(v.tensor_copy(wgb[it % 2][:, :],
                                                  wst5[it % 2][:, 0:512]))
                            dva.wt(v)
                            dva.inc(v.tensor_copy(wub[it % 2][:, :],
                                                  wst5[it % 2][:, 512:1024]))
                        for t2 in range(2):
                            idx = ic * 2 + t2
                            peb.wt(v, peb0 + (ic + 1) * 32)
                            aca.wt(v, aca0 + idx)
                            dvb.inc(v.scalar_tensor_tensor(
                                gbuf[:, :], gu_ps[t2][:, :], sgb[:, t2:t2 + 1],
                                swg_s[:, ic * 512:(ic + 1) * 512],
                                ALU.mult, ALU.mult))
                            dvb.inc(v.scalar_tensor_tensor(
                                ubuf[:, :], gu_ps[2 + t2][:, :], sub[:, t2:t2 + 1],
                                swu_s[:, ic * 512:(ic + 1) * 512],
                                ALU.mult, ALU.mult))
                            aca.wt(v, aca0 + idx + 1)
                            dvb.inc(v.tensor_tensor(ibuf[:, :], silb[:, :],
                                                    ubuf[:, :], ALU.mult))
                            dvb.wt(v)
                            dvb.inc(v.tensor_scalar(
                                interb[t2][:, ic * 512:(ic + 1) * 512],
                                ibuf[:, :], 10.0, -10.0, ALU.min, ALU.max))

                @blk.tensor
                def _(te, dva0=dva0, dvb0=dvb0):
                    for ic in range(4):
                        for hc in range(32):
                            it = ic * 32 + hc
                            dva.wt(te, dva0 + 2 * (it + 1))
                            if hc == 0 and ic >= 1:
                                dvb.wt(te, dvb0 + 8 * ic)
                            st = (hc == 0)
                            sp = (hc == 31)
                            for t2 in range(2):
                                qg_t = qgT[:, t2 * 4096 + hc * 128:
                                           t2 * 4096 + (hc + 1) * 128]
                                qu_t = quT[:, t2 * 4096 + hc * 128:
                                           t2 * 4096 + (hc + 1) * 128]
                                te.matmul(gu_ps[t2][:, :], qg_t, wgb[it % 2][:, :],
                                          start=st, stop=sp)
                                ins = te.matmul(gu_ps[2 + t2][:, :], qu_t,
                                                wub[it % 2][:, :], start=st, stop=sp)
                                if t2 == 1:
                                    peb.inc(ins)

                @blk.scalar
                def _(sc, dvb0=dvb0):
                    for ic in range(4):
                        for t2 in range(2):
                            idx = ic * 2 + t2
                            dvb.wt(sc, dvb0 + idx * 4 + 1)
                            aca.inc(sc.activation(silb[:, :], gbuf[:, :], AF.Silu))

            if p5s < 3:
                continue
            for t2 in range(2):
                with nc.Block() as blk:
                    dva0, pea0 = dva.v, pea.v

                    @blk.vector
                    def _(v, t2=t2):
                        quant(v, dva, interb[t2], invi_s, xsi, sib[:, t2:t2 + 1],
                              qib, IFF)

                    @blk.tensor
                    def _(te, dva0=dva0):
                        dva.wt(te, dva0 + 7)
                        for grp in range(2):
                            pb = pst5[grp % 2]
                            for j in range(8):
                                dc = grp * 8 + j
                                ins = te.transpose(pb[:, j * 128:(j + 1) * 128],
                                                   qib[:, dc * 128:(dc + 1) * 128],
                                                   identb_s[:, :])
                                if j == 7:
                                    pea.inc(ins)

                    @blk.scalar
                    def _(sc, t2=t2, pea0=pea0):
                        for grp in range(2):
                            pea.wt(sc, pea0 + grp + 1)
                            pb = pst5[grp % 2]
                            c_ = t2 * 2048 + grp * 1024
                            acb.inc(sc.copy(qiT[:, c_:c_ + 1024], pb[:, :]))

            if dbg:
                with nc.Block() as blk:
                    @blk.gpsimd
                    def _(gp, b=b):
                        r0 = b * 128
                        stb.dma(gp.dma_start(out=qg_sc[r0:r0 + 128, :],
                                             in_=qgT[:, :]))
                        stb.dma(gp.dma_start(out=qu_sc[r0:r0 + 128, :],
                                             in_=quT[:, :]))
                        stb.dma(gp.dma_start(out=qi_sc[r0:r0 + 128, :],
                                             in_=qiT[:, :]))
                        stb.dma(gp.dma_start(out=scl_sc[r0:r0 + 128, 0:2],
                                             in_=sgb[:, :]))
                        stb.dma(gp.dma_start(out=scl_sc[r0:r0 + 128, 2:4],
                                             in_=sub[:, :]))
                        stb.dma(gp.dma_start(out=scl_sc[r0:r0 + 128, 4:6],
                                             in_=sib[:, :]))
                        for t2 in range(2):
                            stb.dma(gp.dma_start(
                                out=int_sc[(b * 2 + t2) * 128:
                                           (b * 2 + t2 + 1) * 128, :],
                                in_=interb[t2][:, :]))
                        stb.wt(gp)

            if p5s < 4:
                continue
            with nc.Block() as blk:
                (ldb0, ldb20, ldd0, ldd20, lda0, lda20, dva0, dvb0, peb0,
                 gpa0, sta0, sta20) = (
                    ldb.v, ldb2.v, ldd.v, ldd2.v, lda.v, lda2.v, dva.v,
                    dvb.v, peb.v, gpa.v, sta.v, sta2.v)

                @blk.sync
                def _(sync, b=b, dva0=dva0, gpa0=gpa0, dvb0=dvb0):
                    for oc in range(8):
                        for ic in range(16):
                            it = oc * 16 + ic
                            if it >= 2:
                                dva.wt(sync, dva0 + it - 1)
                            sem = ldb if it % 2 == 0 else ldb2
                            sem.dma(sync.dma_start(
                                out=wdst[it % 2][:, :],
                                in_=wdT[ic * 128:(ic + 1) * 128,
                                        oc * 512:(oc + 1) * 512]))
                        if oc >= 2:
                            dvb.wt(sync, dvb0 + 2 * (oc - 1))
                        sem = ldd if oc % 2 == 0 else ldd2
                        sem.dma(sync.dma_start(
                            out=swdst[oc % 2][:, :],
                            in_=swd[:, oc * 512:(oc + 1) * 512]))
                        for t2 in range(2):
                            i2 = oc * 2 + t2
                            if i2 >= 4:
                                gpa.wt(sync, gpa0 + i2 - 1)
                            sem = lda if t2 == 0 else lda2
                            sem.dma(sync.dma_start(
                                out=x2r[t2][:, :],
                                in_=x2_sc[(b * 2 + t2) * 128:(b * 2 + t2 + 1) * 128,
                                          oc * 512:(oc + 1) * 512]))

                @blk.vector
                def _(v, ldb0=ldb0, ldb20=ldb20, peb0=peb0,
                      ldd0=ldd0, ldd20=ldd20, gpa0=gpa0):
                    for oc in range(8):
                        for ic in range(16):
                            it = oc * 16 + ic
                            if it % 2 == 0:
                                ldb.wt(v, ldb0 + (it // 2 + 1) * 16)
                            else:
                                ldb2.wt(v, ldb20 + (it // 2 + 1) * 16)
                            if it >= 2:
                                peb.wt(v, peb0 + it - 1)
                            dva.inc(v.tensor_copy(wdb[it % 2][:, :],
                                                  wdst[it % 2][:, :]))
                        for t2 in range(2):
                            i2 = oc * 2 + t2
                            peb.wt(v, peb0 + (oc + 1) * 16)
                            if oc % 2 == 0:
                                ldd.wt(v, ldd0 + (oc // 2 + 1) * 16)
                            else:
                                ldd2.wt(v, ldd20 + (oc // 2 + 1) * 16)
                            if i2 >= 1:
                                gpa.wt(v, gpa0 + i2)
                            dvb.inc(v.scalar_tensor_tensor(
                                tb5[:, :], d_ps[t2][:, :], sib[:, t2:t2 + 1],
                                swdst[oc % 2][:, :], ALU.mult, ALU.mult))
                            dvb.wt(v)

                @blk.tensor
                def _(te, dva0=dva0, dvb0=dvb0):
                    for oc in range(8):
                        for ic in range(16):
                            it = oc * 16 + ic
                            dva.wt(te, dva0 + it + 1)
                            if ic == 0 and oc >= 1:
                                dvb.wt(te, dvb0 + 2 * oc)
                            st = (ic == 0)
                            sp = (ic == 15)
                            for t2 in range(2):
                                qi_t = qiT[:, t2 * 2048 + ic * 128:
                                           t2 * 2048 + (ic + 1) * 128]
                                ins = te.matmul(d_ps[t2][:, :], qi_t,
                                                wdb[it % 2][:, :], start=st, stop=sp)
                                if t2 == 1:
                                    peb.inc(ins)

                @blk.gpsimd
                def _(gp, b=b, dvb0=dvb0, lda0=lda0, lda20=lda20,
                      sta0=sta0, sta20=sta20):
                    for oc in range(8):
                        for t2 in range(2):
                            i2 = oc * 2 + t2
                            dvb.wt(gp, dvb0 + i2 + 1)
                            if t2 == 0:
                                lda.wt(gp, lda0 + (oc + 1) * 16)
                            else:
                                lda2.wt(gp, lda20 + (oc + 1) * 16)
                            if i2 >= 2:
                                if i2 % 2 == 0:
                                    sta.wt(gp, sta0 + (i2 // 2) * 16)
                                else:
                                    sta2.wt(gp, sta20 + (i2 // 2) * 16)
                            gpa.inc(gp.tensor_tensor(yb[i2 % 2][:, :], tb5[:, :],
                                                     x2r[t2][:, :], ALU.add))
                            gpa.wt(gp)
                            sem = sta if i2 % 2 == 0 else sta2
                            sem.dma(gp.dma_start(
                                out=y[(b * 2 + t2) * 128:(b * 2 + t2 + 1) * 128,
                                      oc * 512:(oc + 1) * 512],
                                in_=yb[i2 % 2][:, :]))
                    sta.wt(gp)
                    sta2.wt(gp)

    with nc.Block() as blk:
        @blk.gpsimd
        def _(gp):
            sta.wt(gp)
            sta2.wt(gp)

    return nc


_CONST_CACHE = {}


def _host_constants():
    if "c" in _CONST_CACHE:
        return _CONST_CACHE["c"]
    f32 = np.float32
    freqs = 1.0 / (10000.0 ** (np.arange(0, 128, 2, dtype=np.float64) / 128.0))
    t = np.arange(S, dtype=np.float64)
    ang = np.outer(t, freqs)
    cos = np.ascontiguousarray(np.cos(ang).astype(f32))
    sin = np.ascontiguousarray(np.sin(ang).astype(f32))
    cosq = np.ascontiguousarray(np.tile(cos, (1, 4)))
    sinq = np.ascontiguousarray(np.tile(sin, (1, 4)))
    lane = np.arange(128)[:, None]
    qf = np.arange(512)[None, :]
    m = np.zeros((128, 4 * 512), dtype=f32)
    for j in range(4):
        m[:, j * 512:(j + 1) * 512] = np.where(128 * j + lane <= qf, 0.0, -1.0e9)
    identb = np.eye(128, dtype=ml_dtypes.bfloat16)
    identf = np.eye(128, dtype=f32)
    _CONST_CACHE["c"] = (cosq, sinq, cos, sin, m, identb, identf)
    return _CONST_CACHE["c"]


def kernel(L0_attn_x, L0_attn_wq, L0_attn_wk, L0_attn_wv, L0_attn_wo,
           L0_ffn_w_gate, L0_ffn_s_wgate, L0_ffn_w_up, L0_ffn_s_wup,
           L0_ffn_w_down, L0_ffn_s_wdown, L0_ffn_inv_gate, L0_ffn_inv_up,
           L0_ffn_inv_inter):
    f32 = np.float32
    x = np.ascontiguousarray(np.asarray(L0_attn_x, dtype=f32).reshape(NTOK, D))
    wq = np.asarray(L0_attn_wq, dtype=f32)
    wk = np.asarray(L0_attn_wk, dtype=f32)
    wv = np.asarray(L0_attn_wv, dtype=f32)
    wo_ = np.ascontiguousarray(np.asarray(L0_attn_wo, dtype=f32))
    wgT = np.ascontiguousarray(np.asarray(L0_ffn_w_gate, dtype=f32).T)
    wuT = np.ascontiguousarray(np.asarray(L0_ffn_w_up, dtype=f32).T)
    wdT = np.ascontiguousarray(np.asarray(L0_ffn_w_down, dtype=f32).T)
    bcast = lambda a, n: np.ascontiguousarray(
        np.broadcast_to(np.asarray(a, dtype=f32)[None, :], (128, n)))
    swg_ = bcast(L0_ffn_s_wgate, IFF)
    swu_ = bcast(L0_ffn_s_wup, IFF)
    swd_ = bcast(L0_ffn_s_wdown, D)
    invg_ = bcast(L0_ffn_inv_gate, D)
    invu_ = bcast(L0_ffn_inv_up, D)
    invi_ = bcast(L0_ffn_inv_inter, IFF)
    cosq, sinq, cosk, sink, masks_, identb, identf = _host_constants()

    nc = build_program()
    in_maps = []
    for c in range(NCORE):
        wqkv = np.ascontiguousarray(np.concatenate(
            [wq[:, 512 * c:512 * (c + 1)],
             wk[:, 128 * c:128 * (c + 1)],
             wv[:, 128 * c:128 * (c + 1)]], axis=1))
        in_maps.append({
            "x_my": np.ascontiguousarray(x[TPC * c:TPC * (c + 1)]),
            "wqkv": wqkv, "wo": wo_, "wgT": wgT, "wuT": wuT, "wdT": wdT,
            "swg": swg_, "swu": swu_, "swd": swd_,
            "invg": invg_, "invu": invu_, "invi": invi_,
            "cosq": cosq, "sinq": sinq, "cosk": cosk, "sink": sink,
            "masks": masks_, "identb": identb, "identf": identf,
            "col0": np.array([[TPC * c]], dtype=np.int32),
        })
    res = run_bass_kernel_spmd(nc, in_maps, core_ids=list(range(NCORE)))
    out = np.concatenate([r["y"] for r in res.results], axis=0)
    return out.reshape(2, S, D)


# version 11 (final)
# speedup vs baseline: 1.0538x; 1.0474x over previous
import contextlib

import numpy as np
import ml_dtypes

import concourse.bass as bass
import concourse.mybir as mybir
from concourse.bass_utils import run_bass_kernel_spmd

F32 = mybir.dt.float32
BF16 = mybir.dt.bfloat16
I32 = mybir.dt.int32
AF = mybir.ActivationFunctionType
ALU = mybir.AluOpType
AX = mybir.AxisListType

D = 4096
S = 4096
NTOK = 8192
NCORE = 8
TPC = 1024
IFF = 2048
EPS = 1e-5
MAGIC = float(1.5 * 2.0 ** 23)
SCL = float(1.0 / np.float32(np.sqrt(np.float32(128.0))))
FBLK = 256
NFB = TPC // FBLK


class Ctr:
    def __init__(self, sem):
        self.sem = sem
        self.v = 0

    def inc(self, instr, n=1):
        self.v += n
        instr.then_inc(self.sem, n)
        return instr

    def dma(self, instr):
        return self.inc(instr, 16)

    def wt(self, eng, v=None):
        eng.wait_ge(self.sem, self.v if v is None else v)


def build_program(phases=6, p2u=32, p5b=NFB, p5s=9, p5q=9, dbg=False):
    nc = bass.Bass("TRN2", target_bir_lowering=False)
    dbgk = "ExternalOutput" if dbg else "Internal"

    apd = lambda name, shape, dt, kind: nc.dram_tensor(name, shape, dt, kind=kind).ap()
    x_my = apd("x_my", [TPC, D], F32, "ExternalInput")
    wqkv = apd("wqkv", [D, 768], F32, "ExternalInput")
    wo = apd("wo", [D, D], F32, "ExternalInput")
    wgT = apd("wgT", [D, IFF], F32, "ExternalInput")
    wuT = apd("wuT", [D, IFF], F32, "ExternalInput")
    wdT = apd("wdT", [IFF, D], F32, "ExternalInput")
    swg = apd("swg", [128, IFF], F32, "ExternalInput")
    swu = apd("swu", [128, IFF], F32, "ExternalInput")
    swd = apd("swd", [128, D], F32, "ExternalInput")
    invg = apd("invg", [128, D], F32, "ExternalInput")
    invu = apd("invu", [128, D], F32, "ExternalInput")
    invi = apd("invi", [128, IFF], F32, "ExternalInput")
    cosq = apd("cosq", [S, 256], F32, "ExternalInput")
    sinq = apd("sinq", [S, 256], F32, "ExternalInput")
    cosk = apd("cosk", [S, 64], F32, "ExternalInput")
    sink = apd("sink", [S, 64], F32, "ExternalInput")
    masks = apd("masks", [128, 2048], F32, "ExternalInput")
    identb = apd("identb", [128, 128], BF16, "ExternalInput")
    identf = apd("identf", [128, 128], F32, "ExternalInput")
    col0 = apd("col0", [1, 1], I32, "ExternalInput")
    y = apd("y", [TPC, D], F32, "ExternalOutput")

    ag1_i = [apd(f"ag1_i{q}", [64 * 128, 256], BF16, "Internal")
             for q in range(4)]
    ag1_o = [apd(f"ag1_o{q}", [NCORE * 64 * 128, 256], BF16, "Internal")
             for q in range(4)]
    qt_sc = apd("qt_sc", [16 * 128, S], BF16, dbgk)
    ag2_in = apd("ag2_in", [512, NTOK], F32, "Internal")
    if dbg:
        ag2_cp = apd("ag2_cp", [512, NTOK], F32, "ExternalOutput")
        kv_cp = apd("kv_cp", [4 * 128, 64 * 128], BF16, "ExternalOutput")
        qkr_cp = apd("qkr_cp", [64 * 128, 640], F32, "ExternalOutput")
    ag2_out = apd("ag2_out", [D, NTOK], F32, "Internal")
    x2_sc = apd("x2_sc", [TPC, D], F32, dbgk)
    if dbg:
        qg_sc = apd("qg_sc", [NFB * 128, 2 * D], BF16, "ExternalOutput")
        qu_sc = apd("qu_sc", [NFB * 128, 2 * D], BF16, "ExternalOutput")
        qi_sc = apd("qi_sc", [NFB * 128, 2 * IFF], BF16, "ExternalOutput")
        int_sc = apd("int_sc", [8 * 128, IFF], BF16, "ExternalOutput")
        scl_sc = apd("scl_sc", [NFB * 128, 6], F32, "ExternalOutput")

    sbl = lambda name, shape, dt=F32: nc.alloc_sbuf_tensor(name, shape, dt).ap()
    identb_s = sbl("identb_s", [128, 128], BF16)
    identf_s = sbl("identf_s", [128, 128], F32)
    onesrow = sbl("onesrow", [1, 128], F32)
    onescol = sbl("onescol", [128, 1], BF16)
    epsap = sbl("epsap", [128, 1], F32)

    es0 = contextlib.ExitStack()
    SEM = {n: Ctr(es0.enter_context(nc.semaphore(n))) for n in
           ["lda", "ldb", "ldc", "ldd", "pea", "peb", "pec",
            "dva", "dvb", "dvc", "aca", "acb", "gpa", "gpb",
            "sta", "stb", "cca",
            "lda2", "ldb2", "ldc2", "ldd2", "sta2"]}
    lda, ldb, ldc, ldd = SEM["lda"], SEM["ldb"], SEM["ldc"], SEM["ldd"]
    lda2, ldb2, ldc2, ldd2 = SEM["lda2"], SEM["ldb2"], SEM["ldc2"], SEM["ldd2"]
    sta2 = SEM["sta2"]
    pea, peb, pec = SEM["pea"], SEM["peb"], SEM["pec"]
    dva, dvb, dvc = SEM["dva"], SEM["dvb"], SEM["dvc"]
    aca, acb = SEM["aca"], SEM["acb"]
    gpa, gpb = SEM["gpa"], SEM["gpb"]
    sta, stb, cca = SEM["sta"], SEM["stb"], SEM["cca"]

    with nc.Block() as blk:
        @blk.sync
        def _(sync):
            lda.dma(sync.dma_start(out=identb_s[:, :], in_=identb[:, :]))
            lda.dma(sync.dma_start(out=identf_s[:, :], in_=identf[:, :]))
            lda.wt(sync)

        @blk.vector
        def _(v):
            v.memset(onesrow[:, :], 1.0)
            v.memset(onescol[:, :], 1.0)
            dva.inc(v.memset(epsap[:, :], EPS))
            dva.wt(v)

    es123 = contextlib.ExitStack()
    kbuf_h = es123.enter_context(nc.sbuf_tensor("kbuf_h", [128, 64 * 128], BF16)).ap()
    kbuf_l = es123.enter_context(nc.sbuf_tensor("kbuf_l", [128, 64 * 128], BF16)).ap()
    vbuf_h = es123.enter_context(nc.sbuf_tensor("vbuf_h", [128, 64 * 128], BF16)).ap()
    vbuf_l = es123.enter_context(nc.sbuf_tensor("vbuf_l", [128, 64 * 128], BF16)).ap()

    with contextlib.ExitStack() as es:
        sb = lambda n_, s_, d_=F32: es.enter_context(nc.sbuf_tensor(n_, s_, d_)).ap()
        ps = lambda n_, s_, d_=F32: es.enter_context(nc.psum_tensor(n_, s_, d_)).ap()
        xt = sb("p1_x", [128, D])
        xc = sb("p1_xc", [128, D])
        h1f = sb("p1_h1f", [128, D])
        sqj = sb("p1_sqj", [128, D], BF16)
        h1h = sb("p1_h1h", [128, D], BF16)
        h1l = sb("p1_h1l", [128, D], BF16)
        ssum = sb("p1_sum", [128, 1])
        mmean = sb("p1_m", [128, 1])
        ssq = sb("p1_ssq", [128, 1])
        sstd = sb("p1_std", [128, 1])
        rstd = sb("p1_rstd", [128, 1])
        stg1 = [sb(f"p1_stg{i}", [128, 1024], BF16) for i in range(2)]
        pst = [ps(f"p1_ps{i}", [128, 1024], BF16) for i in range(2)]

        for tt in range(8):
            with nc.Block() as blk:
                lda0, dva0, aca0, acb0, gpa0, gpb0, pea0, sta0, sta20 = (
                    lda.v, dva.v, aca.v, acb.v, gpa.v, gpb.v, pea.v, sta.v,
                    sta2.v)

                @blk.sync
                def _(sync, tt=tt):
                    lda.dma(sync.dma_start(out=xt[:, :],
                                           in_=x_my[tt * 128:(tt + 1) * 128, :]))

                @blk.vector
                def _(v, lda0=lda0, aca0=aca0):
                    lda.wt(v, lda0 + 16)
                    dva.inc(v.reduce_sum(ssum[:, :], xt[:, :], axis=AX.X))
                    dva.wt(v)
                    dva.inc(v.tensor_scalar(mmean[:, :], ssum[:, :], float(1.0 / D),
                                            None, ALU.mult))
                    dva.wt(v)
                    dva.inc(v.tensor_scalar(xc[:, :], xt[:, :], mmean[:, 0:1],
                                            None, ALU.subtract))
                    aca.wt(v, aca0 + 2)
                    dva.inc(v.reciprocal(rstd[:, :], sstd[:, :]))
                    dva.wt(v)
                    dva.inc(v.tensor_scalar(h1f[:, :], xc[:, :], rstd[:, 0:1],
                                            None, ALU.mult))

                @blk.scalar
                def _(sc, dva0=dva0, pea0=pea0, sta0=sta0, sta20=sta20):
                    dva.wt(sc, dva0 + 3)
                    aca.inc(sc.activation(sqj[:, :], xc[:, :], AF.Square,
                                          accum_out=ssq[:, 0:1]))
                    aca.wt(sc)
                    aca.inc(sc.activation(sstd[:, :], ssq[:, 0:1], AF.Sqrt,
                                          bias=epsap[:, 0:1], scale=float(1.0 / D)))
                    dva.wt(sc, dva0 + 5)
                    acb.inc(sc.copy(h1h[:, :], h1f[:, :]))
                    for g in range(8):
                        pea.wt(sc, pea0 + g + 1)
                        if g >= 2:
                            if g % 2 == 0:
                                sta.wt(sc, sta0 + (g // 2) * 128)
                            else:
                                sta2.wt(sc, sta20 + (g // 2) * 128)
                        acb.inc(sc.copy(stg1[g % 2][:, :], pst[g % 2][:, :]))

                @blk.gpsimd
                def _(gp, acb0=acb0):
                    acb.wt(gp, acb0 + 1)
                    gpa.inc(gp.tensor_tensor(h1l[:, :], h1f[:, :], h1h[:, :],
                                             ALU.subtract))

                @blk.tensor
                def _(te, acb0=acb0, gpa0=gpa0):
                    g = 0
                    for hl in range(2):
                        if hl == 0:
                            acb.wt(te, acb0 + 1)
                        else:
                            gpa.wt(te, gpa0 + 1)
                        src = h1h if hl == 0 else h1l
                        for grp in range(4):
                            if g >= 2:
                                acb.wt(te, acb0 + 1 + g - 1)
                            pb = pst[g % 2]
                            for j in range(8):
                                dc = grp * 8 + j
                                ins = te.transpose(pb[:, j * 128:(j + 1) * 128],
                                                   src[:, dc * 128:(dc + 1) * 128],
                                                   identb_s[:, :])
                                if j == 7:
                                    pea.inc(ins)
                            g += 1

                @blk.gpsimd
                def _(gp, tt=tt, acb0=acb0):
                    for g in range(8):
                        hl, grp = divmod(g, 4)
                        acb.wt(gp, acb0 + 2 + g)
                        stg = stg1[g % 2]
                        stsem = sta if g % 2 == 0 else sta2
                        for j in range(8):
                            dc = grp * 8 + j
                            row0 = (hl * 32 + dc) * 128
                            stsem.dma(gp.dma_start(
                                out=ag1_i[tt // 2][row0:row0 + 128,
                                                   (tt % 2) * 128:
                                                   (tt % 2) * 128 + 128],
                                in_=stg[:, j * 128:(j + 1) * 128]))
                    sta.wt(gp)
                    sta2.wt(gp)
                    if tt % 2 == 1:
                        q = tt // 2
                        cca.inc(gp.collective_compute(
                            "AllGather", ALU.bypass,
                            replica_groups=[list(range(NCORE))],
                            ins=[ag1_i[q][:, :]],
                            outs=[ag1_o[q][:, :]]))

    if phases <= 1:
        return nc
    with contextlib.ExitStack() as es:
        sb = lambda n_, s_, d_=F32: es.enter_context(nc.sbuf_tensor(n_, s_, d_)).ap()
        ps = lambda n_, s_, d_=F32: es.enter_context(nc.psum_tensor(n_, s_, d_)).ap()
        wqh = sb("p2_wqh", [128, 32 * 768], BF16)
        wql = sb("p2_wql", [128, 32 * 768], BF16)
        wstage = [sb(f"p2_wst{i}", [128, 768]) for i in range(2)]
        xst = [[sb(f"p2_x{h}_{i}", [128, 256], BF16) for i in range(2)]
               for h in range(2)]
        rope_t = [sb(f"p2_rt{i}", [128, 640]) for i in range(2)]
        re1 = sb("p2_re1", [128, 320])
        re2 = sb("p2_re2", [128, 320])
        ro1 = sb("p2_ro1", [128, 320])
        ro2 = sb("p2_ro2", [128, 320])
        qkr = sb("p2_qkr", [128, 640])
        qkh = sb("p2_qkh", [128, 640], BF16)
        qkl = sb("p2_qkl", [128, 640], BF16)
        qkstg = [sb(f"p2_qkstg{i}", [128, 512], BF16) for i in range(2)]
        q_ps = [ps(f"p2_qps{i}", [128, 512]) for i in range(2)]
        kv_ps = [ps(f"p2_kvps{i}", [128, 256]) for i in range(2)]
        pst2 = [ps(f"p2_pst{i}", [128, 640], BF16) for i in range(2)]

        with nc.Block() as blk:
            ldb0, ldb20, dva0 = ldb.v, ldb2.v, dva.v

            @blk.sync
            def _(sync, dva0=dva0):
                for dc in range(32):
                    if dc >= 2:
                        dva.wt(sync, dva0 + 2 * (dc - 1))
                    sem = ldb if dc % 2 == 0 else ldb2
                    sem.dma(sync.dma_start(out=wstage[dc % 2][:, :],
                                           in_=wqkv[dc * 128:(dc + 1) * 128, :]))

            @blk.vector
            def _(v, ldb0=ldb0, ldb20=ldb20):
                for dc in range(32):
                    if dc % 2 == 0:
                        ldb.wt(v, ldb0 + (dc // 2 + 1) * 16)
                    else:
                        ldb2.wt(v, ldb20 + (dc // 2 + 1) * 16)
                    dva.inc(v.tensor_copy(wqh[:, dc * 768:(dc + 1) * 768],
                                          wstage[dc % 2][:, :]))
                    dva.wt(v)
                    dva.inc(v.tensor_tensor(wql[:, dc * 768:(dc + 1) * 768],
                                            wstage[dc % 2][:, :],
                                            wqh[:, dc * 768:(dc + 1) * 768],
                                            ALU.subtract))

        for u in range(p2u):
            qr, s_ = divmod(u, 8)
            hf2 = qr
            tg0 = s_ * 8 + qr * 2
            with nc.Block() as blk:
                ldc0, ldc20, ldd0, peb0, pec0 = (
                    ldc.v, ldc2.v, ldd.v, peb.v, pec.v)
                dva0, dvb0, dvc0 = dva.v, dvb.v, dvc.v
                aca0, acb0, gpa0, gpb0, stb0 = aca.v, acb.v, gpa.v, gpb.v, stb.v

                @blk.sync
                def _(sync, s_=s_, qr=qr, tg0=tg0, peb0=peb0):
                    for ti in range(2):
                        tg = tg0 + ti
                        trow = (tg * 128) % S
                        buf = rope_t[ti]
                        ldd.dma(sync.dma_start(out=buf[:, 0:256],
                                               in_=cosq[trow:trow + 128, :]))
                        ldd.dma(sync.dma_start(out=buf[:, 256:512],
                                               in_=sinq[trow:trow + 128, :]))
                        ldd.dma(sync.dma_start(out=buf[:, 512:576],
                                               in_=cosk[trow:trow + 128, :]))
                        ldd.dma(sync.dma_start(out=buf[:, 576:640],
                                               in_=sink[trow:trow + 128, :]))
                    sync.wait_ge(cca.sem, qr + 1)
                    for dc in range(32):
                        if dc >= 2:
                            peb.wt(sync, peb0 + dc - 1)
                        sem = ldc if dc % 2 == 0 else ldc2
                        for hl in range(2):
                            r0 = (s_ * 64 + hl * 32 + dc) * 128
                            sem.dma(sync.dma_start(
                                out=xst[hl][dc % 2][:, :],
                                in_=ag1_o[qr][r0:r0 + 128, :]))

                @blk.tensor
                def _(te, ldc0=ldc0, ldc20=ldc20):
                    for dc in range(32):
                        if dc % 2 == 0:
                            ldc.wt(te, ldc0 + (dc // 2 + 1) * 32)
                        else:
                            ldc2.wt(te, ldc20 + (dc // 2 + 1) * 32)
                        xh, xl = xst[0][dc % 2], xst[1][dc % 2]
                        wh = wqh[:, dc * 768:(dc + 1) * 768]
                        wl = wql[:, dc * 768:(dc + 1) * 768]
                        st = (dc == 0)
                        sp = (dc == 31)
                        for ti in range(2):
                            lwh = xh[:, ti * 128:(ti + 1) * 128]
                            lwl = xl[:, ti * 128:(ti + 1) * 128]
                            qp = q_ps[ti]
                            kp = kv_ps[ti]
                            te.matmul(qp[:, :], lwh, wh[:, 0:512], start=st, stop=False)
                            te.matmul(kp[:, :], lwh, wh[:, 512:768], start=st, stop=False)
                            te.matmul(qp[:, :], lwh, wl[:, 0:512], start=False, stop=False)
                            te.matmul(kp[:, :], lwh, wl[:, 512:768], start=False, stop=False)
                            te.matmul(qp[:, :], lwl, wh[:, 0:512], start=False, stop=sp)
                            ins = te.matmul(kp[:, :], lwl, wh[:, 512:768],
                                            start=False, stop=sp)
                            if ti == 1:
                                peb.inc(ins)

                @blk.vector
                def _(v, tg0=tg0, peb0=peb0, ldd0=ldd0, gpa0=gpa0, acb0=acb0,
                      pec0=pec0, stb0=stb0):
                    peb.wt(v, peb0 + 32)
                    ldd.wt(v, ldd0 + 8 * 16)
                    for ti in range(2):
                        tg = tg0 + ti
                        if ti >= 1:
                            gpa.wt(v, gpa0 + 4 * ti)
                        buf = rope_t[ti]
                        qp = q_ps[ti]
                        kp = kv_ps[ti]
                        v.tensor_tensor(re1[:, 0:256], qp[:, 0:512:2], buf[:, 0:256], ALU.mult)
                        v.tensor_tensor(re2[:, 0:256], qp[:, 1:512:2], buf[:, 256:512], ALU.mult)
                        v.tensor_tensor(ro1[:, 0:256], qp[:, 0:512:2], buf[:, 256:512], ALU.mult)
                        v.tensor_tensor(ro2[:, 0:256], qp[:, 1:512:2], buf[:, 0:256], ALU.mult)
                        v.tensor_tensor(re1[:, 256:320], kp[:, 0:128:2],
                                        buf[:, 512:576], ALU.mult)
                        v.tensor_tensor(re2[:, 256:320], kp[:, 1:128:2],
                                        buf[:, 576:640], ALU.mult)
                        v.tensor_tensor(ro1[:, 256:320], kp[:, 0:128:2],
                                        buf[:, 576:640], ALU.mult)
                        dva.inc(v.tensor_tensor(ro2[:, 256:320], kp[:, 1:128:2],
                                                buf[:, 512:576], ALU.mult))
                        dvb.inc(v.tensor_copy(vbuf_h[:, tg * 128:(tg + 1) * 128],
                                              kp[:, 128:256]))
                        dvb.wt(v)
                        dvb.inc(v.tensor_tensor(
                            vbuf_l[:, tg * 128:(tg + 1) * 128],
                            kp[:, 128:256],
                            vbuf_h[:, tg * 128:(tg + 1) * 128], ALU.subtract))
                        acb.wt(v, acb0 + ti + 1)
                        dvc.inc(v.tensor_tensor(qkl[:, :], qkr[:, :], qkh[:, :],
                                                ALU.subtract))
                        pec.wt(v, pec0 + 2 * (ti + 1))
                        if ti >= 1:
                            stb.wt(v, stb0 + 8 * 16 * ti)
                        for hl in range(2):
                            pb = pst2[hl]
                            dvb.inc(v.tensor_copy(qkstg[hl][:, :], pb[:, 0:512]))
                            kdst = kbuf_h if hl == 0 else kbuf_l
                            dvb.inc(v.tensor_copy(
                                kdst[:, tg * 128:(tg + 1) * 128], pb[:, 512:640]))

                @blk.gpsimd
                def _(gp, tg0=tg0, dva0=dva0, acb0=acb0, dvc0=dvc0):
                    for ti in range(2):
                        dva.wt(gp, dva0 + ti + 1)
                        if ti >= 1:
                            acb.wt(gp, acb0 + ti)
                            dvc.wt(gp, dvc0 + ti)
                            if dbg:
                                stb.wt(gp)
                        gpa.inc(gp.tensor_tensor(qkr[:, 0:512:2], re1[:, 0:256],
                                                 re2[:, 0:256], ALU.subtract))
                        gpa.wt(gp)
                        gpa.inc(gp.tensor_tensor(qkr[:, 1:512:2], ro1[:, 0:256],
                                                 ro2[:, 0:256], ALU.add))
                        gpa.wt(gp)
                        gpa.inc(gp.tensor_tensor(qkr[:, 512:640:2], re1[:, 256:320],
                                                 re2[:, 256:320], ALU.subtract))
                        gpa.wt(gp)
                        gpa.inc(gp.tensor_tensor(qkr[:, 513:640:2], ro1[:, 256:320],
                                                 ro2[:, 256:320], ALU.add))
                        if dbg:
                            gpa.wt(gp)
                            tg = tg0 + ti
                            stb.dma(gp.dma_start(
                                out=qkr_cp[tg * 128:(tg + 1) * 128, :],
                                in_=qkr[:, :]))
                            if ti == 1:
                                stb.wt(gp)

                @blk.scalar
                def _(sc, gpa0=gpa0, dvc0=dvc0, pec0=pec0):
                    for ti in range(2):
                        gpa.wt(sc, gpa0 + 4 * (ti + 1))
                        if ti >= 1:
                            dvc.wt(sc, dvc0 + ti)
                            pec.wt(sc, pec0 + 2 * ti)
                        acb.inc(sc.copy(qkh[:, :], qkr[:, :]))

                @blk.tensor
                def _(te, dvc0=dvc0, dvb0=dvb0):
                    for ti in range(2):
                        dvc.wt(te, dvc0 + ti + 1)
                        if ti >= 1:
                            te.wait_ge(SEM["dvb"].sem, dvb0 + 6 * ti)
                        for hl, srcb in ((0, qkh), (1, qkl)):
                            pb = pst2[hl]
                            for j in range(5):
                                ins = te.transpose(pb[:, j * 128:(j + 1) * 128],
                                                   srcb[:, j * 128:(j + 1) * 128],
                                                   identb_s[:, :])
                                if j == 4:
                                    pec.inc(ins)

                @blk.gpsimd
                def _(gp, tg0=tg0, dvb0=dvb0):
                    for ti in range(2):
                        tg = tg0 + ti
                        p_ = tg // 32
                        tok0 = (tg % 32) * 128
                        for hl in range(2):
                            gp.wait_ge(SEM["dvb"].sem, dvb0 + 6 * ti + 3 + 2 * hl)
                            for h in range(4):
                                r0 = (p_ * 8 + h * 2 + hl) * 128
                                stb.dma(gp.dma_start(
                                    out=qt_sc[r0:r0 + 128, tok0:tok0 + 128],
                                    in_=qkstg[hl][:, h * 128:(h + 1) * 128]))
                    stb.wt(gp)

    if dbg:
        with nc.Block() as blk:
            @blk.gpsimd
            def _(gp):
                for i, src in enumerate((kbuf_h, kbuf_l, vbuf_h, vbuf_l)):
                    stb.dma(gp.dma_start(
                        out=kv_cp[i * 128:(i + 1) * 128, :], in_=src[:, :]))
                stb.wt(gp)

    with contextlib.ExitStack() as es:
        sb = lambda n_, s_, d_=F32: es.enter_context(nc.sbuf_tensor(n_, s_, d_)).ap()
        ps = lambda n_, s_, d_=F32: es.enter_context(nc.psum_tensor(n_, s_, d_)).ap()
        maskb = sb("p3_mask", [128, 2048])
        qgh = sb("p3_qh", [128, 512], BF16)
        qgl = sb("p3_ql", [128, 512], BF16)
        sbuf_s = sb("p3_s", [128, 32 * 512])
        mx8 = sb("p3_mx8", [128, 8])
        mcol = sb("p3_mcol", [128, 1])
        negm = sb("p3_negm", [1, 1])
        bias_s = sb("p3_bias", [128, 1])
        e32 = [sb(f"p3_e32{i}", [128, 512]) for i in range(2)]
        ehb = [sb(f"p3_eh{i}", [128, 512], BF16) for i in range(3)]
        elb = [sb(f"p3_el{i}", [128, 512], BF16) for i in range(3)]
        dn_row = sb("p3_dnr", [1, 512])
        bc_sb = sb("p3_bcsb", [128, 512])
        ao32 = sb("p3_ao", [128, 512])
        s_ps = [ps(f"p3_sps{i}", [128, 512]) for i in range(2)]
        av_ps = ps("p3_avps", [128, 512])
        dn_ps = ps("p3_dnps", [1, 512])
        mt_ps = ps("p3_mtps", [1, 128])
        nb_ps = ps("p3_nbps", [128, 1])
        bc_ps = ps("p3_bcps", [128, 512])

        with nc.Block() as blk:
            @blk.sync
            def _(sync):
                ldd.dma(sync.dma_start(out=maskb[:, :], in_=masks[:, :]))
                ldd.wt(sync)

        for p_ in range(2):
            for h in range(4):
                for g in range(8):
                    L = 4 * g + 4
                    with nc.Block() as blk:
                        lda0, pea0, peb0, pec0 = lda.v, pea.v, peb.v, pec.v
                        dva0, dvb0, dvc0 = dva.v, dvb.v, dvc.v
                        aca0, acb0, gpa0, gpb0 = aca.v, acb.v, gpa.v, gpb.v

                        @blk.sync
                        def _(sync, p_=p_, h=h, g=g):
                            r0 = (p_ * 8 + h * 2) * 128
                            c0 = g * 512
                            lda.dma(sync.dma_start(
                                out=qgh[:, :], in_=qt_sc[r0:r0 + 128, c0:c0 + 512]))
                            lda.dma(sync.dma_start(
                                out=qgl[:, :],
                                in_=qt_sc[r0 + 128:r0 + 256, c0:c0 + 512]))

                        @blk.tensor
                        def _(te, p_=p_, L=L, lda0=lda0, dva0=dva0):
                            lda.wt(te, lda0 + 32)
                            for kt in range(L):
                                tile = 32 * p_ + kt
                                pb = s_ps[kt % 2]
                                if kt >= 2:
                                    dva.wt(te, dva0 + kt - 1)
                                kh = kbuf_h[:, tile * 128:(tile + 1) * 128]
                                kl = kbuf_l[:, tile * 128:(tile + 1) * 128]
                                te.matmul(pb[:, :], kh, qgh[:, :], start=True, stop=False)
                                te.matmul(pb[:, :], kh, qgl[:, :], start=False, stop=False)
                                pea.inc(te.matmul(pb[:, :], kl, qgh[:, :],
                                                  start=False, stop=True))

                        @blk.vector
                        def _(v, g=g, L=L, pea0=pea0):
                            for kt in range(L):
                                pea.wt(v, pea0 + kt + 1)
                                pb = s_ps[kt % 2]
                                dst = sbuf_s[:, kt * 512:(kt + 1) * 512]
                                if kt >= 4 * g:
                                    jm = kt - 4 * g
                                    dva.inc(v.scalar_tensor_tensor(
                                        dst, pb[:, :], SCL,
                                        maskb[:, jm * 512:(jm + 1) * 512],
                                        ALU.mult, ALU.add))
                                else:
                                    dva.inc(v.tensor_scalar(dst, pb[:, :], SCL,
                                                            None, ALU.mult))
                            nch = (L * 512) // 2048
                            for j in range(nch):
                                dvb.inc(v.reduce_max(
                                    mx8[:, j:j + 1],
                                    sbuf_s[:, j * 2048:(j + 1) * 2048], axis=AX.X))
                            dvb.wt(v)
                            dvb.inc(v.reduce_max(mcol[:, :], mx8[:, 0:nch], axis=AX.X))

                        @blk.tensor
                        def _(te, dvb0=dvb0, L=L):
                            nch = (L * 512) // 2048
                            dvb.wt(te, dvb0 + nch + 1)
                            pec.inc(te.transpose(mt_ps[:, :], mcol[:, :],
                                                 identf_s[:, :]))

                        @blk.vector
                        def _(v, pec0=pec0):
                            pec.wt(v, pec0 + 1)
                            dvc.inc(v.reduce_max(negm[:, :], mt_ps[0:1, :], axis=AX.X,
                                                 negate=True))

                        @blk.tensor
                        def _(te, dvc0=dvc0):
                            dvc.wt(te, dvc0 + 1)
                            pec.inc(te.matmul(nb_ps[:, :], onesrow[:, :], negm[:, :],
                                              start=True, stop=True))

                        @blk.vector
                        def _(v, pec0=pec0):
                            pec.wt(v, pec0 + 2)
                            dvc.inc(v.tensor_copy(bias_s[:, :], nb_ps[:, :]))

                        @blk.scalar
                        def _(sc, L=L, dvc0=dvc0, dva0=dva0, gpb0=gpb0, peb0=peb0):
                            dvc.wt(sc, dvc0 + 2)
                            for kt in range(L):
                                dva.wt(sc, dva0 + kt + 1)
                                if kt >= 2:
                                    gpb.wt(sc, gpb0 + kt - 1)
                                if kt >= 3:
                                    peb.wt(sc, peb0 + kt - 2)
                                aca.inc(sc.activation(
                                    e32[kt % 2][:, :],
                                    sbuf_s[:, kt * 512:(kt + 1) * 512],
                                    AF.Exp, bias=bias_s[:, 0:1], scale=1.0))
                                aca.wt(sc)
                                acb.inc(sc.copy(ehb[kt % 3][:, :], e32[kt % 2][:, :]))

                        @blk.gpsimd
                        def _(gp, L=L, acb0=acb0, peb0=peb0):
                            for kt in range(L):
                                acb.wt(gp, acb0 + kt + 1)
                                if kt >= 3:
                                    peb.wt(gp, peb0 + kt - 2)
                                gpb.inc(gp.tensor_tensor(
                                    elb[kt % 3][:, :], e32[kt % 2][:, :],
                                    ehb[kt % 3][:, :], ALU.subtract))

                        @blk.tensor
                        def _(te, p_=p_, L=L, gpb0=gpb0):
                            for kt in range(L):
                                tile = 32 * p_ + kt
                                gpb.wt(te, gpb0 + kt + 1)
                                vh = vbuf_h[:, tile * 128:(tile + 1) * 128]
                                vl = vbuf_l[:, tile * 128:(tile + 1) * 128]
                                eh_, el_ = ehb[kt % 3], elb[kt % 3]
                                st = (kt == 0)
                                sp = (kt == L - 1)
                                te.matmul(av_ps[:, :], vh, eh_[:, :], start=st, stop=False)
                                te.matmul(av_ps[:, :], vh, el_[:, :], start=False,
                                          stop=False)
                                te.matmul(av_ps[:, :], vl, eh_[:, :], start=False, stop=sp)
                                te.matmul(dn_ps[:, :], onescol[:, :], eh_[:, :],
                                          start=st, stop=False)
                                peb.inc(te.matmul(dn_ps[:, :], onescol[:, :], el_[:, :],
                                                  start=False, stop=sp))

                        @blk.vector
                        def _(v, L=L, peb0=peb0):
                            peb.wt(v, peb0 + L)
                            dvc.inc(v.reciprocal(dn_row[:, :], dn_ps[0:1, :]))

                        @blk.tensor
                        def _(te, dvc0=dvc0):
                            dvc.wt(te, dvc0 + 3)
                            pec.inc(te.matmul(bc_ps[:, :], onesrow[:, :], dn_row[:, :],
                                              start=True, stop=True))

                        @blk.vector
                        def _(v, pec0=pec0):
                            pec.wt(v, pec0 + 3)
                            dvc.inc(v.tensor_copy(bc_sb[:, :], bc_ps[:, :]))
                            dvc.wt(v)
                            dvc.inc(v.tensor_tensor(ao32[:, :], av_ps[:, :],
                                                    bc_sb[:, :], ALU.mult))

                        @blk.gpsimd
                        def _(gp, p_=p_, h=h, g=g, dvc0=dvc0):
                            dvc.wt(gp, dvc0 + 5)
                            r0 = h * 128
                            c0 = p_ * S + g * 512
                            stb.dma(gp.dma_start(
                                out=ag2_in[r0:r0 + 128, c0:c0 + 512], in_=ao32[:, :]))
                            if dbg:
                                stb.dma(gp.dma_start(
                                    out=ag2_cp[r0:r0 + 128, c0:c0 + 512],
                                    in_=ao32[:, :]))
                            stb.wt(gp)

        with nc.Block() as blk:
            @blk.gpsimd
            def _(gp):
                cca.inc(gp.collective_compute(
                    "AllGather", ALU.bypass,
                    replica_groups=[list(range(NCORE))],
                    ins=[ag2_in[:, :]],
                    outs=[ag2_out[:, :]]))

    es123.close()
    if phases <= 3:
        return nc

    with contextlib.ExitStack() as es:
        sb = lambda n_, s_, d_=F32: es.enter_context(nc.sbuf_tensor(n_, s_, d_)).ap()
        ps = lambda n_, s_, d_=F32: es.enter_context(nc.psum_tensor(n_, s_, d_)).ap()
        aoh = sb("p4_aoh", [128, 32 * 1024], BF16)
        aol = sb("p4_aol", [128, 32 * 1024], BF16)
        aost = [sb(f"p4_aost{i}", [128, 4096]) for i in range(2)]
        c0sb = sb("p4_c0", [1, 1], I32)
        wst = [sb(f"p4_wst{i}", [128, 512]) for i in range(2)]
        woh = [sb(f"p4_woh{i}", [128, 512], BF16) for i in range(2)]
        wol = [sb(f"p4_wol{i}", [128, 512], BF16) for i in range(2)]
        xres = [sb(f"p4_xr{i}", [128, 512]) for i in range(2)]
        x2st = [sb(f"p4_x2{i}", [128, 512]) for i in range(2)]
        wo_ps = [ps(f"p4_wops{i}", [128, 512]) for i in range(8)]

        with nc.Block() as blk:
            ldb0, ldb20, aca0, dva0 = ldb.v, ldb2.v, aca.v, dva.v

            @blk.sync
            def _(sync, ldb0=ldb0, dva0=dva0):
                ldb.dma(sync.dma_start(out=c0sb[:, :], in_=col0[:, :]))
                ldb.wt(sync, ldb0 + 16)
                cca.wt(sync)
                ag2v = ag2_out.rearrange("(g a p) c -> g p a c", p=128, a=4)
                with sync.register("p4off") as off_reg:
                    sync.reg_load(off_reg, c0sb[0:1, 0:1])
                    off = sync.snap(off_reg)
                    for g4 in range(8):
                        if g4 >= 2:
                            dva.wt(sync, dva0 + 4 * (g4 - 1))
                        dst = aost[g4 % 2].rearrange("p (a b) -> p a b", a=4)
                        sem = ldb if g4 % 2 == 0 else ldb2
                        sem.dma(sync.dma_start(
                            out=dst, in_=ag2v[g4, :, :, bass.ds(off, 1024)]))

            @blk.scalar
            def _(sc, ldb0=ldb0, ldb20=ldb20):
                for g4 in range(8):
                    if g4 % 2 == 0:
                        ldb.wt(sc, ldb0 + 16 + (g4 // 2 + 1) * 16)
                    else:
                        ldb2.wt(sc, ldb20 + (g4 // 2 + 1) * 16)
                    for j in range(4):
                        dc = g4 * 4 + j
                        aca.inc(sc.copy(aoh[:, dc * 1024:(dc + 1) * 1024],
                                        aost[g4 % 2][:, j * 1024:(j + 1) * 1024]))

            @blk.vector
            def _(v, aca0=aca0):
                for g4 in range(8):
                    for j in range(4):
                        dc = g4 * 4 + j
                        aca.wt(v, aca0 + dc + 1)
                        dva.inc(v.tensor_tensor(aol[:, dc * 1024:(dc + 1) * 1024],
                                                aost[g4 % 2][:, j * 1024:(j + 1) * 1024],
                                                aoh[:, dc * 1024:(dc + 1) * 1024],
                                                ALU.subtract))

        for oc in range(8):
            with nc.Block() as blk:
                ldc0, ldc20, lda0, lda20, dva0, dvb0, peb0, sta0, sta20 = (
                    ldc.v, ldc2.v, lda.v, lda2.v, dva.v, dvb.v, peb.v, sta.v,
                    sta2.v)

                @blk.sync
                def _(sync, oc=oc, dva0=dva0, dvb0=dvb0):
                    for dc in range(32):
                        if dc >= 2:
                            dva.wt(sync, dva0 + 2 * (dc - 1))
                        sem = ldc if dc % 2 == 0 else ldc2
                        sem.dma(sync.dma_start(
                            out=wst[dc % 2][:, :],
                            in_=wo[dc * 128:(dc + 1) * 128,
                                   oc * 512:(oc + 1) * 512]))
                    for tt in range(8):
                        if tt >= 2:
                            dvb.wt(sync, dvb0 + tt - 1)
                        sem = lda if tt % 2 == 0 else lda2
                        sem.dma(sync.dma_start(
                            out=xres[tt % 2][:, :],
                            in_=x_my[tt * 128:(tt + 1) * 128,
                                     oc * 512:(oc + 1) * 512]))

                @blk.vector
                def _(v, ldc0=ldc0, ldc20=ldc20, peb0=peb0):
                    for dc in range(32):
                        if dc % 2 == 0:
                            ldc.wt(v, ldc0 + (dc // 2 + 1) * 16)
                        else:
                            ldc2.wt(v, ldc20 + (dc // 2 + 1) * 16)
                        if dc >= 2:
                            peb.wt(v, peb0 + dc - 1)
                        dva.inc(v.tensor_copy(woh[dc % 2][:, :], wst[dc % 2][:, :]))
                        dva.wt(v)
                        dva.inc(v.tensor_tensor(wol[dc % 2][:, :], wst[dc % 2][:, :],
                                                woh[dc % 2][:, :], ALU.subtract))

                @blk.tensor
                def _(te, dva0=dva0):
                    for dc in range(32):
                        dva.wt(te, dva0 + 2 * (dc + 1))
                        st = (dc == 0)
                        sp = (dc == 31)
                        for tt in range(8):
                            ah = aoh[:, dc * 1024 + tt * 128:dc * 1024 + (tt + 1) * 128]
                            al = aol[:, dc * 1024 + tt * 128:dc * 1024 + (tt + 1) * 128]
                            te.matmul(wo_ps[tt][:, :], ah, woh[dc % 2][:, :],
                                      start=st, stop=False)
                            te.matmul(wo_ps[tt][:, :], ah, wol[dc % 2][:, :],
                                      start=False, stop=False)
                            ins = te.matmul(wo_ps[tt][:, :], al, woh[dc % 2][:, :],
                                            start=False, stop=sp)
                            if tt == 7:
                                peb.inc(ins)

                @blk.vector
                def _(v, peb0=peb0, lda0=lda0, lda20=lda20,
                      sta0=sta0, sta20=sta20):
                    for tt in range(8):
                        peb.wt(v, peb0 + 32)
                        if tt % 2 == 0:
                            lda.wt(v, lda0 + (tt // 2 + 1) * 16)
                        else:
                            lda2.wt(v, lda20 + (tt // 2 + 1) * 16)
                        if tt >= 2:
                            if tt % 2 == 0:
                                sta.wt(v, sta0 + (tt // 2) * 16)
                            else:
                                sta2.wt(v, sta20 + (tt // 2) * 16)
                        dvb.inc(v.tensor_tensor(x2st[tt % 2][:, :], wo_ps[tt][:, :],
                                                xres[tt % 2][:, :], ALU.add))

                @blk.gpsimd
                def _(gp, oc=oc, dvb0=dvb0):
                    for tt in range(8):
                        dvb.wt(gp, dvb0 + tt + 1)
                        sem = sta if tt % 2 == 0 else sta2
                        sem.dma(gp.dma_start(
                            out=x2_sc[tt * 128:(tt + 1) * 128,
                                      oc * 512:(oc + 1) * 512],
                            in_=x2st[tt % 2][:, :]))
                    sta.wt(gp)
                    sta2.wt(gp)

    if phases <= 4:
        return nc
    with contextlib.ExitStack() as es:
        sb = lambda n_, s_, d_=F32: es.enter_context(nc.sbuf_tensor(n_, s_, d_)).ap()
        ps = lambda n_, s_, d_=F32: es.enter_context(nc.psum_tensor(n_, s_, d_)).ap()
        swg_s = sb("p5_swg", [128, IFF])
        swu_s = sb("p5_swu", [128, IFF])
        invg_s = sb("p5_invg", [128, D])
        invu_s = sb("p5_invu", [128, D])
        invi_s = sb("p5_invi", [128, IFF])
        xsg = sb("p5_xsg", [128, D])
        xsu = sb("p5_xsu", [128, D])
        h2 = sb("p5_h2", [128, D], BF16)
        st5 = {n_: sb(f"p5_{n_}", [128, 1]) for n_ in
               ["sum", "m", "ssq", "std", "rstd"]}
        qgb = sb("p5_qgb", [128, D], BF16)
        qub = sb("p5_qub", [128, D], BF16)
        rcpc = sb("p5_rcpc", [128, 1])
        sgb = sb("p5_sgb", [128, 2])
        sub = sb("p5_sub", [128, 2])
        sib = sb("p5_sib", [128, 2])
        qgT = sb("p5_qgT", [128, 32 * FBLK], BF16)
        quT = sb("p5_quT", [128, 32 * FBLK], BF16)
        qiT = sb("p5_qiT", [128, 16 * FBLK], BF16)
        interb = [sb(f"p5_int{i}", [128, IFF], BF16) for i in range(2)]
        work2048 = sb("p5_work", [128, IFF])
        xsi = work2048
        gbuf = work2048[:, 0:512]
        silb = work2048[:, 512:1024]
        ubuf = work2048[:, 1024:1536]
        ibuf = work2048[:, 1536:2048]
        qib = sb("p5_qib", [128, IFF], BF16)
        wst5 = [sb(f"p5_wst{i}", [128, 1024]) for i in range(2)]
        wgb = [sb(f"p5_wgb{i}", [128, 512], BF16) for i in range(2)]
        wub = [sb(f"p5_wub{i}", [128, 512], BF16) for i in range(2)]
        wdst = [wst5[i][:, 0:512] for i in range(2)]
        wdb = [sb(f"p5_wdb{i}", [128, 512], BF16) for i in range(2)]
        swdst = [sb(f"p5_swdst{i}", [128, 512]) for i in range(2)]
        x2r = [sb(f"p5_x2r{i}", [128, 512]) for i in range(2)]
        tb5 = sb("p5_tb", [128, 512])
        yb = [sb(f"p5_yb{i}", [128, 512]) for i in range(2)]
        gu_ps = [ps(f"p5_gups{i}", [128, 512]) for i in range(4)]
        d_ps = [ps(f"p5_dps{i}", [128, 512]) for i in range(2)]
        pst5 = [ps(f"p5_pst{i}", [128, 1024], BF16) for i in range(2)]

        with nc.Block() as blk:
            @blk.sync
            def _(sync):
                for dst, src in ((swg_s, swg), (swu_s, swu),
                                 (invg_s, invg), (invu_s, invu), (invi_s, invi)):
                    ldd.dma(sync.dma_start(out=dst[:, :], in_=src[:, :]))
                ldd.wt(sync)

        def quant(eng, ctr, src_bf, inv_s, xs_, sc_col, qb_, n):
            ctr.inc(eng.tensor_tensor(xs_[:, 0:n], src_bf[:, 0:n], inv_s[:, 0:n],
                                      ALU.mult))
            ctr.wt(eng)
            ctr.inc(eng.reduce_max(sc_col, xs_[:, 0:n], axis=AX.X,
                                   apply_absolute_value=True))
            ctr.wt(eng)
            ctr.inc(eng.tensor_scalar(sc_col, sc_col, float(1.0 / 127.0), 1e-8,
                                      ALU.mult, ALU.max))
            ctr.wt(eng)
            ctr.inc(eng.reciprocal(rcpc[:, :], sc_col))
            ctr.wt(eng)
            ctr.inc(eng.tensor_scalar(xs_[:, 0:n], xs_[:, 0:n], rcpc[:, 0:1], MAGIC,
                                      ALU.mult, ALU.add))
            ctr.wt(eng)
            ctr.inc(eng.tensor_scalar(xs_[:, 0:n], xs_[:, 0:n], MAGIC, 127.0,
                                      ALU.subtract, ALU.min))
            ctr.wt(eng)
            ctr.inc(eng.tensor_scalar(qb_[:, 0:n], xs_[:, 0:n], -127.0, None,
                                      ALU.max))
            ctr.wt(eng)

        for b in range(p5b):
            for t2 in range(2):
                tt = b * 2 + t2
                with nc.Block() as blk:
                    lda0, dva0, aca0, acb0, gpa0, pea0, gpb0 = (
                        lda.v, dva.v, aca.v, acb.v, gpa.v, pea.v, gpb.v)

                    @blk.sync
                    def _(sync, tt=tt):
                        lda.dma(sync.dma_start(
                            out=xsu[:, :], in_=x2_sc[tt * 128:(tt + 1) * 128, :]))

                    @blk.vector
                    def _(v, lda0=lda0, aca0=aca0, t2=t2):
                        lda.wt(v, lda0 + 16)
                        dva.inc(v.reduce_sum(st5["sum"][:, :], xsu[:, :], axis=AX.X))
                        dva.wt(v)
                        dva.inc(v.tensor_scalar(st5["m"][:, :], st5["sum"][:, :],
                                                float(1.0 / D), None, ALU.mult))
                        dva.wt(v)
                        dva.inc(v.tensor_scalar(xsg[:, :], xsu[:, :], st5["m"][:, 0:1],
                                                None, ALU.subtract))
                        aca.wt(v, aca0 + 2)
                        dva.inc(v.reciprocal(st5["rstd"][:, :], st5["std"][:, :]))
                        dva.wt(v)
                        dva.inc(v.tensor_scalar(h2[:, :], xsg[:, :],
                                                st5["rstd"][:, 0:1], None, ALU.mult))
                        dva.wt(v)
                        if p5q >= 1:
                            quant(v, dva, h2, invg_s, xsg, sgb[:, t2:t2 + 1], qgb, D)
                        if p5q >= 2:
                            quant(v, dva, h2, invu_s, xsu, sub[:, t2:t2 + 1], qub, D)

                    @blk.scalar
                    def _(sc, t2=t2, dva0=dva0, pea0=pea0):
                        dva.wt(sc, dva0 + 3)
                        aca.inc(sc.activation(h2[:, :], xsg[:, :], AF.Square,
                                              accum_out=st5["ssq"][:, 0:1]))
                        aca.wt(sc)
                        aca.inc(sc.activation(st5["std"][:, :], st5["ssq"][:, 0:1],
                                              AF.Sqrt, bias=epsap[:, 0:1],
                                              scale=float(1.0 / D)))
                        if p5q < 3:
                            return
                        gi = 0
                        for dstT in (qgT, quT):
                            for grp in range(4):
                                gi += 1
                                pea.wt(sc, pea0 + gi)
                                pb = pst5[(gi - 1) % 2]
                                c_ = t2 * 4096 + grp * 1024
                                acb.inc(sc.copy(dstT[:, c_:c_ + 1024], pb[:, :]))

                    @blk.tensor
                    def _(te, dva0=dva0, acb0=acb0):
                        if p5q < 3:
                            return
                        gi = 0
                        for src in (qgb, qub):
                            if src is qgb:
                                dva.wt(te, dva0 + 12)
                            else:
                                dva.wt(te, dva0 + 19)
                            for grp in range(4):
                                if gi >= 2:
                                    acb.wt(te, acb0 + gi - 1)
                                pb = pst5[gi % 2]
                                for j in range(8):
                                    dc = grp * 8 + j
                                    ins = te.transpose(
                                        pb[:, j * 128:(j + 1) * 128],
                                        src[:, dc * 128:(dc + 1) * 128],
                                        identb_s[:, :])
                                    if j == 7:
                                        pea.inc(ins)
                                gi += 1

            if p5s < 2:
                continue
            with nc.Block() as blk:
                ldb0, ldb20, dva0, dvb0, aca0, peb0 = (
                    ldb.v, ldb2.v, dva.v, dvb.v, aca.v, peb.v)

                @blk.sync
                def _(sync, dva0=dva0):
                    for ic in range(4):
                        for hc in range(32):
                            it = ic * 32 + hc
                            if it >= 2:
                                dva.wt(sync, dva0 + 2 * (it - 1))
                            sem = ldb if it % 2 == 0 else ldb2
                            sem.dma(sync.dma_start(
                                out=wst5[it % 2][:, 0:512],
                                in_=wgT[hc * 128:(hc + 1) * 128,
                                        ic * 512:(ic + 1) * 512]))
                            sem.dma(sync.dma_start(
                                out=wst5[it % 2][:, 512:1024],
                                in_=wuT[hc * 128:(hc + 1) * 128,
                                        ic * 512:(ic + 1) * 512]))

                @blk.vector
                def _(v, ldb0=ldb0, ldb20=ldb20, peb0=peb0, aca0=aca0):
                    for ic in range(4):
                        for hc in range(32):
                            it = ic * 32 + hc
                            if it % 2 == 0:
                                ldb.wt(v, ldb0 + (it // 2 + 1) * 32)
                            else:
                                ldb2.wt(v, ldb20 + (it // 2 + 1) * 32)
                            if it >= 2:
                                peb.wt(v, peb0 + it - 1)
                            dva.inc(v.tensor_copy(wgb[it % 2][:, :],
                                                  wst5[it % 2][:, 0:512]))
                            dva.wt(v)
                            dva.inc(v.tensor_copy(wub[it % 2][:, :],
                                                  wst5[it % 2][:, 512:1024]))
                        for t2 in range(2):
                            idx = ic * 2 + t2
                            peb.wt(v, peb0 + (ic + 1) * 32)
                            aca.wt(v, aca0 + idx)
                            dvb.inc(v.scalar_tensor_tensor(
                                gbuf[:, :], gu_ps[t2][:, :], sgb[:, t2:t2 + 1],
                                swg_s[:, ic * 512:(ic + 1) * 512],
                                ALU.mult, ALU.mult))
                            dvb.inc(v.scalar_tensor_tensor(
                                ubuf[:, :], gu_ps[2 + t2][:, :], sub[:, t2:t2 + 1],
                                swu_s[:, ic * 512:(ic + 1) * 512],
                                ALU.mult, ALU.mult))
                            aca.wt(v, aca0 + idx + 1)
                            dvb.inc(v.tensor_tensor(ibuf[:, :], silb[:, :],
                                                    ubuf[:, :], ALU.mult))
                            dvb.wt(v)
                            dvb.inc(v.tensor_scalar(
                                interb[t2][:, ic * 512:(ic + 1) * 512],
                                ibuf[:, :], 10.0, -10.0, ALU.min, ALU.max))

                @blk.tensor
                def _(te, dva0=dva0, dvb0=dvb0):
                    for ic in range(4):
                        for hc in range(32):
                            it = ic * 32 + hc
                            dva.wt(te, dva0 + 2 * (it + 1))
                            if hc == 0 and ic >= 1:
                                dvb.wt(te, dvb0 + 8 * ic)
                            st = (hc == 0)
                            sp = (hc == 31)
                            for t2 in range(2):
                                qg_t = qgT[:, t2 * 4096 + hc * 128:
                                           t2 * 4096 + (hc + 1) * 128]
                                qu_t = quT[:, t2 * 4096 + hc * 128:
                                           t2 * 4096 + (hc + 1) * 128]
                                te.matmul(gu_ps[t2][:, :], qg_t, wgb[it % 2][:, :],
                                          start=st, stop=sp)
                                ins = te.matmul(gu_ps[2 + t2][:, :], qu_t,
                                                wub[it % 2][:, :], start=st, stop=sp)
                                if t2 == 1:
                                    peb.inc(ins)

                @blk.scalar
                def _(sc, dvb0=dvb0):
                    for ic in range(4):
                        for t2 in range(2):
                            idx = ic * 2 + t2
                            dvb.wt(sc, dvb0 + idx * 4 + 1)
                            aca.inc(sc.activation(silb[:, :], gbuf[:, :], AF.Silu))

            if p5s < 3:
                continue
            for t2 in range(2):
                with nc.Block() as blk:
                    dva0, pea0 = dva.v, pea.v

                    @blk.vector
                    def _(v, t2=t2):
                        quant(v, dva, interb[t2], invi_s, xsi, sib[:, t2:t2 + 1],
                              qib, IFF)

                    @blk.tensor
                    def _(te, dva0=dva0):
                        dva.wt(te, dva0 + 7)
                        for grp in range(2):
                            pb = pst5[grp % 2]
                            for j in range(8):
                                dc = grp * 8 + j
                                ins = te.transpose(pb[:, j * 128:(j + 1) * 128],
                                                   qib[:, dc * 128:(dc + 1) * 128],
                                                   identb_s[:, :])
                                if j == 7:
                                    pea.inc(ins)

                    @blk.scalar
                    def _(sc, t2=t2, pea0=pea0):
                        for grp in range(2):
                            pea.wt(sc, pea0 + grp + 1)
                            pb = pst5[grp % 2]
                            c_ = t2 * 2048 + grp * 1024
                            acb.inc(sc.copy(qiT[:, c_:c_ + 1024], pb[:, :]))

            if dbg:
                with nc.Block() as blk:
                    @blk.gpsimd
                    def _(gp, b=b):
                        r0 = b * 128
                        stb.dma(gp.dma_start(out=qg_sc[r0:r0 + 128, :],
                                             in_=qgT[:, :]))
                        stb.dma(gp.dma_start(out=qu_sc[r0:r0 + 128, :],
                                             in_=quT[:, :]))
                        stb.dma(gp.dma_start(out=qi_sc[r0:r0 + 128, :],
                                             in_=qiT[:, :]))
                        stb.dma(gp.dma_start(out=scl_sc[r0:r0 + 128, 0:2],
                                             in_=sgb[:, :]))
                        stb.dma(gp.dma_start(out=scl_sc[r0:r0 + 128, 2:4],
                                             in_=sub[:, :]))
                        stb.dma(gp.dma_start(out=scl_sc[r0:r0 + 128, 4:6],
                                             in_=sib[:, :]))
                        for t2 in range(2):
                            stb.dma(gp.dma_start(
                                out=int_sc[(b * 2 + t2) * 128:
                                           (b * 2 + t2 + 1) * 128, :],
                                in_=interb[t2][:, :]))
                        stb.wt(gp)

            if p5s < 4:
                continue
            with nc.Block() as blk:
                (ldb0, ldb20, ldd0, ldd20, lda0, lda20, dva0, dvb0, peb0,
                 gpa0, sta0, sta20) = (
                    ldb.v, ldb2.v, ldd.v, ldd2.v, lda.v, lda2.v, dva.v,
                    dvb.v, peb.v, gpa.v, sta.v, sta2.v)

                @blk.sync
                def _(sync, b=b, dva0=dva0, gpa0=gpa0, dvb0=dvb0):
                    for oc in range(8):
                        for ic in range(16):
                            it = oc * 16 + ic
                            if it >= 2:
                                dva.wt(sync, dva0 + it - 1)
                            sem = ldb if it % 2 == 0 else ldb2
                            sem.dma(sync.dma_start(
                                out=wdst[it % 2][:, :],
                                in_=wdT[ic * 128:(ic + 1) * 128,
                                        oc * 512:(oc + 1) * 512]))
                        if oc >= 2:
                            dvb.wt(sync, dvb0 + 2 * (oc - 1))
                        sem = ldd if oc % 2 == 0 else ldd2
                        sem.dma(sync.dma_start(
                            out=swdst[oc % 2][:, :],
                            in_=swd[:, oc * 512:(oc + 1) * 512]))
                        for t2 in range(2):
                            i2 = oc * 2 + t2
                            if i2 >= 4:
                                gpa.wt(sync, gpa0 + i2 - 1)
                            sem = lda if t2 == 0 else lda2
                            sem.dma(sync.dma_start(
                                out=x2r[t2][:, :],
                                in_=x2_sc[(b * 2 + t2) * 128:(b * 2 + t2 + 1) * 128,
                                          oc * 512:(oc + 1) * 512]))

                @blk.vector
                def _(v, ldb0=ldb0, ldb20=ldb20, peb0=peb0,
                      ldd0=ldd0, ldd20=ldd20, gpa0=gpa0):
                    for oc in range(8):
                        for ic in range(16):
                            it = oc * 16 + ic
                            if it % 2 == 0:
                                ldb.wt(v, ldb0 + (it // 2 + 1) * 16)
                            else:
                                ldb2.wt(v, ldb20 + (it // 2 + 1) * 16)
                            if it >= 2:
                                peb.wt(v, peb0 + it - 1)
                            dva.inc(v.tensor_copy(wdb[it % 2][:, :],
                                                  wdst[it % 2][:, :]))
                        for t2 in range(2):
                            i2 = oc * 2 + t2
                            peb.wt(v, peb0 + (oc + 1) * 16)
                            if oc % 2 == 0:
                                ldd.wt(v, ldd0 + (oc // 2 + 1) * 16)
                            else:
                                ldd2.wt(v, ldd20 + (oc // 2 + 1) * 16)
                            if i2 >= 1:
                                gpa.wt(v, gpa0 + i2)
                            dvb.inc(v.scalar_tensor_tensor(
                                tb5[:, :], d_ps[t2][:, :], sib[:, t2:t2 + 1],
                                swdst[oc % 2][:, :], ALU.mult, ALU.mult))
                            dvb.wt(v)

                @blk.tensor
                def _(te, dva0=dva0, dvb0=dvb0):
                    for oc in range(8):
                        for ic in range(16):
                            it = oc * 16 + ic
                            dva.wt(te, dva0 + it + 1)
                            if ic == 0 and oc >= 1:
                                dvb.wt(te, dvb0 + 2 * oc)
                            st = (ic == 0)
                            sp = (ic == 15)
                            for t2 in range(2):
                                qi_t = qiT[:, t2 * 2048 + ic * 128:
                                           t2 * 2048 + (ic + 1) * 128]
                                ins = te.matmul(d_ps[t2][:, :], qi_t,
                                                wdb[it % 2][:, :], start=st, stop=sp)
                                if t2 == 1:
                                    peb.inc(ins)

                @blk.gpsimd
                def _(gp, b=b, dvb0=dvb0, lda0=lda0, lda20=lda20,
                      sta0=sta0, sta20=sta20):
                    for oc in range(8):
                        for t2 in range(2):
                            i2 = oc * 2 + t2
                            dvb.wt(gp, dvb0 + i2 + 1)
                            if t2 == 0:
                                lda.wt(gp, lda0 + (oc + 1) * 16)
                            else:
                                lda2.wt(gp, lda20 + (oc + 1) * 16)
                            if i2 >= 2:
                                if i2 % 2 == 0:
                                    sta.wt(gp, sta0 + (i2 // 2) * 16)
                                else:
                                    sta2.wt(gp, sta20 + (i2 // 2) * 16)
                            gpa.inc(gp.tensor_tensor(yb[i2 % 2][:, :], tb5[:, :],
                                                     x2r[t2][:, :], ALU.add))
                            gpa.wt(gp)
                            sem = sta if i2 % 2 == 0 else sta2
                            sem.dma(gp.dma_start(
                                out=y[(b * 2 + t2) * 128:(b * 2 + t2 + 1) * 128,
                                      oc * 512:(oc + 1) * 512],
                                in_=yb[i2 % 2][:, :]))
                    sta.wt(gp)
                    sta2.wt(gp)

    with nc.Block() as blk:
        @blk.gpsimd
        def _(gp):
            sta.wt(gp)
            sta2.wt(gp)

    return nc


_CONST_CACHE = {}


def _host_constants():
    if "c" in _CONST_CACHE:
        return _CONST_CACHE["c"]
    f32 = np.float32
    freqs = 1.0 / (10000.0 ** (np.arange(0, 128, 2, dtype=np.float64) / 128.0))
    t = np.arange(S, dtype=np.float64)
    ang = np.outer(t, freqs)
    cos = np.ascontiguousarray(np.cos(ang).astype(f32))
    sin = np.ascontiguousarray(np.sin(ang).astype(f32))
    cosq = np.ascontiguousarray(np.tile(cos, (1, 4)))
    sinq = np.ascontiguousarray(np.tile(sin, (1, 4)))
    lane = np.arange(128)[:, None]
    qf = np.arange(512)[None, :]
    m = np.zeros((128, 4 * 512), dtype=f32)
    for j in range(4):
        m[:, j * 512:(j + 1) * 512] = np.where(128 * j + lane <= qf, 0.0, -1.0e9)
    identb = np.eye(128, dtype=ml_dtypes.bfloat16)
    identf = np.eye(128, dtype=f32)
    _CONST_CACHE["c"] = (cosq, sinq, cos, sin, m, identb, identf)
    return _CONST_CACHE["c"]


def kernel(L0_attn_x, L0_attn_wq, L0_attn_wk, L0_attn_wv, L0_attn_wo,
           L0_ffn_w_gate, L0_ffn_s_wgate, L0_ffn_w_up, L0_ffn_s_wup,
           L0_ffn_w_down, L0_ffn_s_wdown, L0_ffn_inv_gate, L0_ffn_inv_up,
           L0_ffn_inv_inter):
    f32 = np.float32
    x = np.ascontiguousarray(np.asarray(L0_attn_x, dtype=f32).reshape(NTOK, D))
    wq = np.asarray(L0_attn_wq, dtype=f32)
    wk = np.asarray(L0_attn_wk, dtype=f32)
    wv = np.asarray(L0_attn_wv, dtype=f32)
    wo_ = np.ascontiguousarray(np.asarray(L0_attn_wo, dtype=f32))
    wgT = np.ascontiguousarray(np.asarray(L0_ffn_w_gate, dtype=f32).T)
    wuT = np.ascontiguousarray(np.asarray(L0_ffn_w_up, dtype=f32).T)
    wdT = np.ascontiguousarray(np.asarray(L0_ffn_w_down, dtype=f32).T)
    bcast = lambda a, n: np.ascontiguousarray(
        np.broadcast_to(np.asarray(a, dtype=f32)[None, :], (128, n)))
    swg_ = bcast(L0_ffn_s_wgate, IFF)
    swu_ = bcast(L0_ffn_s_wup, IFF)
    swd_ = bcast(L0_ffn_s_wdown, D)
    invg_ = bcast(L0_ffn_inv_gate, D)
    invu_ = bcast(L0_ffn_inv_up, D)
    invi_ = bcast(L0_ffn_inv_inter, IFF)
    cosq, sinq, cosk, sink, masks_, identb, identf = _host_constants()

    nc = build_program()
    in_maps = []
    for c in range(NCORE):
        wqkv = np.ascontiguousarray(np.concatenate(
            [wq[:, 512 * c:512 * (c + 1)],
             wk[:, 128 * c:128 * (c + 1)],
             wv[:, 128 * c:128 * (c + 1)]], axis=1))
        in_maps.append({
            "x_my": np.ascontiguousarray(x[TPC * c:TPC * (c + 1)]),
            "wqkv": wqkv, "wo": wo_, "wgT": wgT, "wuT": wuT, "wdT": wdT,
            "swg": swg_, "swu": swu_, "swd": swd_,
            "invg": invg_, "invu": invu_, "invi": invi_,
            "cosq": cosq, "sinq": sinq, "cosk": cosk, "sink": sink,
            "masks": masks_, "identb": identb, "identf": identf,
            "col0": np.array([[TPC * c]], dtype=np.int32),
        })
    res = run_bass_kernel_spmd(nc, in_maps, core_ids=list(range(NCORE)))
    out = np.concatenate([r["y"] for r in res.results], axis=0)
    return out.reshape(2, S, D)


# version 12
# speedup vs baseline: 1.0643x; 1.0100x over previous
import contextlib

import numpy as np
import ml_dtypes

import concourse.bass as bass
import concourse.mybir as mybir
from concourse.bass_utils import run_bass_kernel_spmd

F32 = mybir.dt.float32
BF16 = mybir.dt.bfloat16
I32 = mybir.dt.int32
AF = mybir.ActivationFunctionType
ALU = mybir.AluOpType
AX = mybir.AxisListType

D = 4096
S = 4096
NTOK = 8192
NCORE = 8
TPC = 1024
IFF = 2048
EPS = 1e-5
MAGIC = float(1.5 * 2.0 ** 23)
SCL = float(1.0 / np.float32(np.sqrt(np.float32(128.0))))
FBLK = 256
NFB = TPC // FBLK


class Ctr:
    def __init__(self, sem):
        self.sem = sem
        self.v = 0

    def inc(self, instr, n=1):
        self.v += n
        instr.then_inc(self.sem, n)
        return instr

    def dma(self, instr):
        return self.inc(instr, 16)

    def wt(self, eng, v=None):
        eng.wait_ge(self.sem, self.v if v is None else v)


def build_program(phases=6, p2u=32, p5b=NFB, p5s=9, p5q=9, dbg=False):
    nc = bass.Bass("TRN2", target_bir_lowering=False)
    dbgk = "ExternalOutput" if dbg else "Internal"

    apd = lambda name, shape, dt, kind: nc.dram_tensor(name, shape, dt, kind=kind).ap()
    x_my = apd("x_my", [TPC, D], F32, "ExternalInput")
    wqkv = apd("wqkv", [D, 768], F32, "ExternalInput")
    wo = apd("wo", [D, D], F32, "ExternalInput")
    wgT = apd("wgT", [D, IFF], F32, "ExternalInput")
    wuT = apd("wuT", [D, IFF], F32, "ExternalInput")
    wdT = apd("wdT", [IFF, D], F32, "ExternalInput")
    swg = apd("swg", [128, IFF], F32, "ExternalInput")
    swu = apd("swu", [128, IFF], F32, "ExternalInput")
    swd = apd("swd", [128, D], F32, "ExternalInput")
    invg = apd("invg", [128, D], F32, "ExternalInput")
    invu = apd("invu", [128, D], F32, "ExternalInput")
    invi = apd("invi", [128, IFF], F32, "ExternalInput")
    cosq = apd("cosq", [S, 256], F32, "ExternalInput")
    sinq = apd("sinq", [S, 256], F32, "ExternalInput")
    cosk = apd("cosk", [S, 64], F32, "ExternalInput")
    sink = apd("sink", [S, 64], F32, "ExternalInput")
    masks = apd("masks", [128, 2048], F32, "ExternalInput")
    identb = apd("identb", [128, 128], BF16, "ExternalInput")
    identf = apd("identf", [128, 128], F32, "ExternalInput")
    col0 = apd("col0", [1, 1], I32, "ExternalInput")
    y = apd("y", [TPC, D], F32, "ExternalOutput")

    ag1_i = [apd(f"ag1_i{q}", [64 * 128, 256], BF16, "Internal")
             for q in range(4)]
    ag1_o = [apd(f"ag1_o{q}", [NCORE * 64 * 128, 256], BF16, "Internal")
             for q in range(4)]
    qt_sc = apd("qt_sc", [16 * 128, S], BF16, dbgk)
    ag2_in = apd("ag2_in", [512, NTOK], F32, "Internal")
    if dbg:
        ag2_cp = apd("ag2_cp", [512, NTOK], F32, "ExternalOutput")
        kv_cp = apd("kv_cp", [4 * 128, 64 * 128], BF16, "ExternalOutput")
        qkr_cp = apd("qkr_cp", [64 * 128, 640], F32, "ExternalOutput")
    ag2_out = apd("ag2_out", [D, NTOK], F32, "Internal")
    x2_sc = apd("x2_sc", [TPC, D], F32, dbgk)
    if dbg:
        qg_sc = apd("qg_sc", [NFB * 128, 2 * D], BF16, "ExternalOutput")
        qu_sc = apd("qu_sc", [NFB * 128, 2 * D], BF16, "ExternalOutput")
        qi_sc = apd("qi_sc", [NFB * 128, 2 * IFF], BF16, "ExternalOutput")
        int_sc = apd("int_sc", [8 * 128, IFF], BF16, "ExternalOutput")
        scl_sc = apd("scl_sc", [NFB * 128, 6], F32, "ExternalOutput")

    sbl = lambda name, shape, dt=F32: nc.alloc_sbuf_tensor(name, shape, dt).ap()
    identb_s = sbl("identb_s", [128, 128], BF16)
    identf_s = sbl("identf_s", [128, 128], F32)
    onesrow = sbl("onesrow", [1, 128], F32)
    onescol = sbl("onescol", [128, 1], BF16)
    epsap = sbl("epsap", [128, 1], F32)

    es0 = contextlib.ExitStack()
    SEM = {n: Ctr(es0.enter_context(nc.semaphore(n))) for n in
           ["lda", "ldb", "ldc", "ldd", "pea", "peb", "pec",
            "dva", "dvb", "dvc", "aca", "acb", "gpa", "gpb",
            "sta", "stb", "cca",
            "lda2", "ldb2", "ldc2", "ldd2", "sta2"]}
    lda, ldb, ldc, ldd = SEM["lda"], SEM["ldb"], SEM["ldc"], SEM["ldd"]
    lda2, ldb2, ldc2, ldd2 = SEM["lda2"], SEM["ldb2"], SEM["ldc2"], SEM["ldd2"]
    sta2 = SEM["sta2"]
    pea, peb, pec = SEM["pea"], SEM["peb"], SEM["pec"]
    dva, dvb, dvc = SEM["dva"], SEM["dvb"], SEM["dvc"]
    aca, acb = SEM["aca"], SEM["acb"]
    gpa, gpb = SEM["gpa"], SEM["gpb"]
    sta, stb, cca = SEM["sta"], SEM["stb"], SEM["cca"]

    with nc.Block() as blk:
        @blk.sync
        def _(sync):
            lda.dma(sync.dma_start(out=identb_s[:, :], in_=identb[:, :]))
            lda.dma(sync.dma_start(out=identf_s[:, :], in_=identf[:, :]))
            lda.wt(sync)

        @blk.vector
        def _(v):
            v.memset(onesrow[:, :], 1.0)
            v.memset(onescol[:, :], 1.0)
            dva.inc(v.memset(epsap[:, :], EPS))
            dva.wt(v)

    es123 = contextlib.ExitStack()
    kbuf_h = es123.enter_context(nc.sbuf_tensor("kbuf_h", [128, 64 * 128], BF16)).ap()
    kbuf_l = es123.enter_context(nc.sbuf_tensor("kbuf_l", [128, 64 * 128], BF16)).ap()
    vbuf_h = es123.enter_context(nc.sbuf_tensor("vbuf_h", [128, 64 * 128], BF16)).ap()
    vbuf_l = es123.enter_context(nc.sbuf_tensor("vbuf_l", [128, 64 * 128], BF16)).ap()

    with contextlib.ExitStack() as es:
        sb = lambda n_, s_, d_=F32: es.enter_context(nc.sbuf_tensor(n_, s_, d_)).ap()
        ps = lambda n_, s_, d_=F32: es.enter_context(nc.psum_tensor(n_, s_, d_)).ap()
        xt = sb("p1_x", [128, D])
        xc = sb("p1_xc", [128, D])
        h1f = sb("p1_h1f", [128, D])
        sqj = sb("p1_sqj", [128, D], BF16)
        h1h = sb("p1_h1h", [128, D], BF16)
        h1l = sb("p1_h1l", [128, D], BF16)
        ssum = sb("p1_sum", [128, 1])
        mmean = sb("p1_m", [128, 1])
        ssq = sb("p1_ssq", [128, 1])
        sstd = sb("p1_std", [128, 1])
        rstd = sb("p1_rstd", [128, 1])
        stg1 = [sb(f"p1_stg{i}", [128, 1024], BF16) for i in range(2)]
        pst = [ps(f"p1_ps{i}", [128, 1024], BF16) for i in range(2)]

        for tt in range(8):
            with nc.Block() as blk:
                lda0, dva0, aca0, acb0, gpa0, gpb0, pea0, sta0, sta20 = (
                    lda.v, dva.v, aca.v, acb.v, gpa.v, gpb.v, pea.v, sta.v,
                    sta2.v)

                @blk.sync
                def _(sync, tt=tt):
                    lda.dma(sync.dma_start(out=xt[:, :],
                                           in_=x_my[tt * 128:(tt + 1) * 128, :]))

                @blk.vector
                def _(v, lda0=lda0, aca0=aca0):
                    lda.wt(v, lda0 + 16)
                    dva.inc(v.reduce_sum(ssum[:, :], xt[:, :], axis=AX.X))
                    dva.wt(v)
                    dva.inc(v.tensor_scalar(mmean[:, :], ssum[:, :], float(1.0 / D),
                                            None, ALU.mult))
                    dva.wt(v)
                    dva.inc(v.tensor_scalar(xc[:, :], xt[:, :], mmean[:, 0:1],
                                            None, ALU.subtract))
                    aca.wt(v, aca0 + 2)
                    dva.inc(v.reciprocal(rstd[:, :], sstd[:, :]))
                    dva.wt(v)
                    dva.inc(v.tensor_scalar(h1f[:, :], xc[:, :], rstd[:, 0:1],
                                            None, ALU.mult))

                @blk.scalar
                def _(sc, dva0=dva0, pea0=pea0, sta0=sta0, sta20=sta20):
                    dva.wt(sc, dva0 + 3)
                    aca.inc(sc.activation(sqj[:, :], xc[:, :], AF.Square,
                                          accum_out=ssq[:, 0:1]))
                    aca.wt(sc)
                    aca.inc(sc.activation(sstd[:, :], ssq[:, 0:1], AF.Sqrt,
                                          bias=epsap[:, 0:1], scale=float(1.0 / D)))
                    dva.wt(sc, dva0 + 5)
                    acb.inc(sc.copy(h1h[:, :], h1f[:, :]))
                    for g in range(8):
                        pea.wt(sc, pea0 + g + 1)
                        if g >= 2:
                            if g % 2 == 0:
                                sta.wt(sc, sta0 + (g // 2) * 128)
                            else:
                                sta2.wt(sc, sta20 + (g // 2) * 128)
                        acb.inc(sc.copy(stg1[g % 2][:, :], pst[g % 2][:, :]))

                @blk.gpsimd
                def _(gp, acb0=acb0):
                    acb.wt(gp, acb0 + 1)
                    gpa.inc(gp.tensor_tensor(h1l[:, :], h1f[:, :], h1h[:, :],
                                             ALU.subtract))

                @blk.tensor
                def _(te, acb0=acb0, gpa0=gpa0):
                    g = 0
                    for hl in range(2):
                        if hl == 0:
                            acb.wt(te, acb0 + 1)
                        else:
                            gpa.wt(te, gpa0 + 1)
                        src = h1h if hl == 0 else h1l
                        for grp in range(4):
                            if g >= 2:
                                acb.wt(te, acb0 + 1 + g - 1)
                            pb = pst[g % 2]
                            for j in range(8):
                                dc = grp * 8 + j
                                ins = te.transpose(pb[:, j * 128:(j + 1) * 128],
                                                   src[:, dc * 128:(dc + 1) * 128],
                                                   identb_s[:, :])
                                if j == 7:
                                    pea.inc(ins)
                            g += 1

                @blk.gpsimd
                def _(gp, tt=tt, acb0=acb0):
                    for g in range(8):
                        hl, grp = divmod(g, 4)
                        acb.wt(gp, acb0 + 2 + g)
                        stg = stg1[g % 2]
                        stsem = sta if g % 2 == 0 else sta2
                        for j in range(8):
                            dc = grp * 8 + j
                            row0 = (hl * 32 + dc) * 128
                            stsem.dma(gp.dma_start(
                                out=ag1_i[tt // 2][row0:row0 + 128,
                                                   (tt % 2) * 128:
                                                   (tt % 2) * 128 + 128],
                                in_=stg[:, j * 128:(j + 1) * 128]))
                    sta.wt(gp)
                    sta2.wt(gp)
                    if tt % 2 == 1:
                        q = tt // 2
                        cca.inc(gp.collective_compute(
                            "AllGather", ALU.bypass,
                            replica_groups=[list(range(NCORE))],
                            ins=[ag1_i[q][:, :]],
                            outs=[ag1_o[q][:, :]]))

    if phases <= 1:
        return nc
    with contextlib.ExitStack() as es:
        sb = lambda n_, s_, d_=F32: es.enter_context(nc.sbuf_tensor(n_, s_, d_)).ap()
        ps = lambda n_, s_, d_=F32: es.enter_context(nc.psum_tensor(n_, s_, d_)).ap()
        wqh = sb("p2_wqh", [128, 32 * 768], BF16)
        wql = sb("p2_wql", [128, 32 * 768], BF16)
        wstage = [sb(f"p2_wst{i}", [128, 768]) for i in range(2)]
        xst = [[sb(f"p2_x{h}_{i}", [128, 256], BF16) for i in range(2)]
               for h in range(2)]
        rope_t = [sb(f"p2_rt{i}", [128, 640]) for i in range(2)]
        re1 = sb("p2_re1", [128, 320])
        re2 = sb("p2_re2", [128, 320])
        ro1 = sb("p2_ro1", [128, 320])
        ro2 = sb("p2_ro2", [128, 320])
        qkr = sb("p2_qkr", [128, 640])
        qkh = sb("p2_qkh", [128, 640], BF16)
        qkl = sb("p2_qkl", [128, 640], BF16)
        qkstg = [sb(f"p2_qkstg{i}", [128, 512], BF16) for i in range(2)]
        q_ps = [ps(f"p2_qps{i}", [128, 512]) for i in range(2)]
        kv_ps = [ps(f"p2_kvps{i}", [128, 256]) for i in range(2)]
        pst2 = [ps(f"p2_pst{i}", [128, 640], BF16) for i in range(2)]

        with nc.Block() as blk:
            ldb0, ldb20, dva0 = ldb.v, ldb2.v, dva.v

            @blk.sync
            def _(sync, dva0=dva0):
                for dc in range(32):
                    if dc >= 2:
                        dva.wt(sync, dva0 + 2 * (dc - 1))
                    sem = ldb if dc % 2 == 0 else ldb2
                    sem.dma(sync.dma_start(out=wstage[dc % 2][:, :],
                                           in_=wqkv[dc * 128:(dc + 1) * 128, :]))

            @blk.vector
            def _(v, ldb0=ldb0, ldb20=ldb20):
                for dc in range(32):
                    if dc % 2 == 0:
                        ldb.wt(v, ldb0 + (dc // 2 + 1) * 16)
                    else:
                        ldb2.wt(v, ldb20 + (dc // 2 + 1) * 16)
                    dva.inc(v.tensor_copy(wqh[:, dc * 768:(dc + 1) * 768],
                                          wstage[dc % 2][:, :]))
                    dva.wt(v)
                    dva.inc(v.tensor_tensor(wql[:, dc * 768:(dc + 1) * 768],
                                            wstage[dc % 2][:, :],
                                            wqh[:, dc * 768:(dc + 1) * 768],
                                            ALU.subtract))

        for u in range(p2u):
            qr, s_ = divmod(u, 8)
            hf2 = qr
            tg0 = s_ * 8 + qr * 2
            with nc.Block() as blk:
                ldc0, ldc20, ldd0, peb0, pec0 = (
                    ldc.v, ldc2.v, ldd.v, peb.v, pec.v)
                dva0, dvb0, dvc0 = dva.v, dvb.v, dvc.v
                aca0, acb0, gpa0, gpb0, stb0 = aca.v, acb.v, gpa.v, gpb.v, stb.v

                @blk.sync
                def _(sync, s_=s_, qr=qr, tg0=tg0, peb0=peb0):
                    for ti in range(2):
                        tg = tg0 + ti
                        trow = (tg * 128) % S
                        buf = rope_t[ti]
                        ldd.dma(sync.dma_start(out=buf[:, 0:256],
                                               in_=cosq[trow:trow + 128, :]))
                        ldd.dma(sync.dma_start(out=buf[:, 256:512],
                                               in_=sinq[trow:trow + 128, :]))
                        ldd.dma(sync.dma_start(out=buf[:, 512:576],
                                               in_=cosk[trow:trow + 128, :]))
                        ldd.dma(sync.dma_start(out=buf[:, 576:640],
                                               in_=sink[trow:trow + 128, :]))
                    sync.wait_ge(cca.sem, qr + 1)
                    for dc in range(32):
                        if dc >= 2:
                            peb.wt(sync, peb0 + dc - 1)
                        sem = ldc if dc % 2 == 0 else ldc2
                        for hl in range(2):
                            r0 = (s_ * 64 + hl * 32 + dc) * 128
                            sem.dma(sync.dma_start(
                                out=xst[hl][dc % 2][:, :],
                                in_=ag1_o[qr][r0:r0 + 128, :]))

                @blk.tensor
                def _(te, ldc0=ldc0, ldc20=ldc20):
                    for dc in range(32):
                        if dc % 2 == 0:
                            ldc.wt(te, ldc0 + (dc // 2 + 1) * 32)
                        else:
                            ldc2.wt(te, ldc20 + (dc // 2 + 1) * 32)
                        xh, xl = xst[0][dc % 2], xst[1][dc % 2]
                        wh = wqh[:, dc * 768:(dc + 1) * 768]
                        wl = wql[:, dc * 768:(dc + 1) * 768]
                        st = (dc == 0)
                        sp = (dc == 31)
                        for ti in range(2):
                            lwh = xh[:, ti * 128:(ti + 1) * 128]
                            lwl = xl[:, ti * 128:(ti + 1) * 128]
                            qp = q_ps[ti]
                            kp = kv_ps[ti]
                            te.matmul(qp[:, :], lwh, wh[:, 0:512], start=st, stop=False)
                            te.matmul(kp[:, :], lwh, wh[:, 512:768], start=st, stop=False)
                            te.matmul(qp[:, :], lwh, wl[:, 0:512], start=False, stop=False)
                            te.matmul(kp[:, :], lwh, wl[:, 512:768], start=False, stop=False)
                            te.matmul(qp[:, :], lwl, wh[:, 0:512], start=False, stop=sp)
                            ins = te.matmul(kp[:, :], lwl, wh[:, 512:768],
                                            start=False, stop=sp)
                            if ti == 1:
                                peb.inc(ins)

                @blk.vector
                def _(v, tg0=tg0, peb0=peb0, ldd0=ldd0, gpa0=gpa0, acb0=acb0,
                      pec0=pec0, stb0=stb0):
                    peb.wt(v, peb0 + 32)
                    ldd.wt(v, ldd0 + 8 * 16)
                    for ti in range(2):
                        tg = tg0 + ti
                        if ti >= 1:
                            gpa.wt(v, gpa0 + 4 * ti)
                        buf = rope_t[ti]
                        qp = q_ps[ti]
                        kp = kv_ps[ti]
                        v.tensor_tensor(re1[:, 0:256], qp[:, 0:512:2], buf[:, 0:256], ALU.mult)
                        v.tensor_tensor(re2[:, 0:256], qp[:, 1:512:2], buf[:, 256:512], ALU.mult)
                        v.tensor_tensor(ro1[:, 0:256], qp[:, 0:512:2], buf[:, 256:512], ALU.mult)
                        v.tensor_tensor(ro2[:, 0:256], qp[:, 1:512:2], buf[:, 0:256], ALU.mult)
                        v.tensor_tensor(re1[:, 256:320], kp[:, 0:128:2],
                                        buf[:, 512:576], ALU.mult)
                        v.tensor_tensor(re2[:, 256:320], kp[:, 1:128:2],
                                        buf[:, 576:640], ALU.mult)
                        v.tensor_tensor(ro1[:, 256:320], kp[:, 0:128:2],
                                        buf[:, 576:640], ALU.mult)
                        dva.inc(v.tensor_tensor(ro2[:, 256:320], kp[:, 1:128:2],
                                                buf[:, 512:576], ALU.mult))
                        dvb.inc(v.tensor_copy(vbuf_h[:, tg * 128:(tg + 1) * 128],
                                              kp[:, 128:256]))
                        dvb.wt(v)
                        dvb.inc(v.tensor_tensor(
                            vbuf_l[:, tg * 128:(tg + 1) * 128],
                            kp[:, 128:256],
                            vbuf_h[:, tg * 128:(tg + 1) * 128], ALU.subtract))
                        acb.wt(v, acb0 + ti + 1)
                        dvc.inc(v.tensor_tensor(qkl[:, :], qkr[:, :], qkh[:, :],
                                                ALU.subtract))
                        pec.wt(v, pec0 + 2 * (ti + 1))
                        if ti >= 1:
                            stb.wt(v, stb0 + 8 * 16 * ti)
                        for hl in range(2):
                            pb = pst2[hl]
                            dvb.inc(v.tensor_copy(qkstg[hl][:, :], pb[:, 0:512]))
                            kdst = kbuf_h if hl == 0 else kbuf_l
                            dvb.inc(v.tensor_copy(
                                kdst[:, tg * 128:(tg + 1) * 128], pb[:, 512:640]))

                @blk.gpsimd
                def _(gp, tg0=tg0, dva0=dva0, acb0=acb0, dvc0=dvc0):
                    for ti in range(2):
                        dva.wt(gp, dva0 + ti + 1)
                        if ti >= 1:
                            acb.wt(gp, acb0 + ti)
                            dvc.wt(gp, dvc0 + ti)
                            if dbg:
                                stb.wt(gp)
                        gpa.inc(gp.tensor_tensor(qkr[:, 0:512:2], re1[:, 0:256],
                                                 re2[:, 0:256], ALU.subtract))
                        gpa.wt(gp)
                        gpa.inc(gp.tensor_tensor(qkr[:, 1:512:2], ro1[:, 0:256],
                                                 ro2[:, 0:256], ALU.add))
                        gpa.wt(gp)
                        gpa.inc(gp.tensor_tensor(qkr[:, 512:640:2], re1[:, 256:320],
                                                 re2[:, 256:320], ALU.subtract))
                        gpa.wt(gp)
                        gpa.inc(gp.tensor_tensor(qkr[:, 513:640:2], ro1[:, 256:320],
                                                 ro2[:, 256:320], ALU.add))
                        if dbg:
                            gpa.wt(gp)
                            tg = tg0 + ti
                            stb.dma(gp.dma_start(
                                out=qkr_cp[tg * 128:(tg + 1) * 128, :],
                                in_=qkr[:, :]))
                            if ti == 1:
                                stb.wt(gp)

                @blk.scalar
                def _(sc, gpa0=gpa0, dvc0=dvc0, pec0=pec0):
                    for ti in range(2):
                        gpa.wt(sc, gpa0 + 4 * (ti + 1))
                        if ti >= 1:
                            dvc.wt(sc, dvc0 + ti)
                            pec.wt(sc, pec0 + 2 * ti)
                        acb.inc(sc.copy(qkh[:, :], qkr[:, :]))

                @blk.tensor
                def _(te, dvc0=dvc0, dvb0=dvb0):
                    for ti in range(2):
                        dvc.wt(te, dvc0 + ti + 1)
                        if ti >= 1:
                            te.wait_ge(SEM["dvb"].sem, dvb0 + 6 * ti)
                        for hl, srcb in ((0, qkh), (1, qkl)):
                            pb = pst2[hl]
                            for j in range(5):
                                ins = te.transpose(pb[:, j * 128:(j + 1) * 128],
                                                   srcb[:, j * 128:(j + 1) * 128],
                                                   identb_s[:, :])
                                if j == 4:
                                    pec.inc(ins)

                @blk.gpsimd
                def _(gp, tg0=tg0, dvb0=dvb0):
                    for ti in range(2):
                        tg = tg0 + ti
                        p_ = tg // 32
                        tok0 = (tg % 32) * 128
                        for hl in range(2):
                            gp.wait_ge(SEM["dvb"].sem, dvb0 + 6 * ti + 3 + 2 * hl)
                            for h in range(4):
                                r0 = (p_ * 8 + h * 2 + hl) * 128
                                stb.dma(gp.dma_start(
                                    out=qt_sc[r0:r0 + 128, tok0:tok0 + 128],
                                    in_=qkstg[hl][:, h * 128:(h + 1) * 128]))
                    stb.wt(gp)

    if dbg:
        with nc.Block() as blk:
            @blk.gpsimd
            def _(gp):
                for i, src in enumerate((kbuf_h, kbuf_l, vbuf_h, vbuf_l)):
                    stb.dma(gp.dma_start(
                        out=kv_cp[i * 128:(i + 1) * 128, :], in_=src[:, :]))
                stb.wt(gp)

    with contextlib.ExitStack() as es:
        sb = lambda n_, s_, d_=F32: es.enter_context(nc.sbuf_tensor(n_, s_, d_)).ap()
        ps = lambda n_, s_, d_=F32: es.enter_context(nc.psum_tensor(n_, s_, d_)).ap()
        maskb = sb("p3_mask", [128, 2048])
        qgh = sb("p3_qh", [128, 512], BF16)
        qgl = sb("p3_ql", [128, 512], BF16)
        sbuf_s = sb("p3_s", [128, 32 * 512])
        mx8 = sb("p3_mx8", [128, 32])
        mcol = sb("p3_mcol", [128, 1])
        negm = sb("p3_negm", [1, 1])
        bias_s = sb("p3_bias", [128, 1])
        e32 = [sb(f"p3_e32{i}", [128, 512]) for i in range(2)]
        ehb = [sb(f"p3_eh{i}", [128, 512], BF16) for i in range(3)]
        elb = [sb(f"p3_el{i}", [128, 512], BF16) for i in range(3)]
        dn_row = sb("p3_dnr", [1, 512])
        bc_sb = sb("p3_bcsb", [128, 512])
        ao32 = sb("p3_ao", [128, 512])
        s_ps = [ps(f"p3_sps{i}", [128, 512]) for i in range(2)]
        av_ps = ps("p3_avps", [128, 512])
        dn_ps = ps("p3_dnps", [1, 512])
        mt_ps = ps("p3_mtps", [1, 128])
        nb_ps = ps("p3_nbps", [128, 1])
        bc_ps = ps("p3_bcps", [128, 512])

        with nc.Block() as blk:
            @blk.sync
            def _(sync):
                ldd.dma(sync.dma_start(out=maskb[:, :], in_=masks[:, :]))
                ldd.wt(sync)

        for p_ in range(2):
            for h in range(4):
                for g in range(8):
                    L = 4 * g + 4
                    with nc.Block() as blk:
                        lda0, pea0, peb0, pec0 = lda.v, pea.v, peb.v, pec.v
                        dva0, dvb0, dvc0 = dva.v, dvb.v, dvc.v
                        aca0, acb0, gpa0, gpb0 = aca.v, acb.v, gpa.v, gpb.v

                        @blk.sync
                        def _(sync, p_=p_, h=h, g=g):
                            r0 = (p_ * 8 + h * 2) * 128
                            c0 = g * 512
                            lda.dma(sync.dma_start(
                                out=qgh[:, :], in_=qt_sc[r0:r0 + 128, c0:c0 + 512]))
                            lda.dma(sync.dma_start(
                                out=qgl[:, :],
                                in_=qt_sc[r0 + 128:r0 + 256, c0:c0 + 512]))

                        @blk.tensor
                        def _(te, p_=p_, L=L, lda0=lda0, dva0=dva0):
                            lda.wt(te, lda0 + 32)
                            for kt in range(L):
                                tile = 32 * p_ + kt
                                pb = s_ps[kt % 2]
                                if kt >= 2:
                                    dva.wt(te, dva0 + kt - 1)
                                kh = kbuf_h[:, tile * 128:(tile + 1) * 128]
                                kl = kbuf_l[:, tile * 128:(tile + 1) * 128]
                                te.matmul(pb[:, :], kh, qgh[:, :], start=True, stop=False)
                                te.matmul(pb[:, :], kh, qgl[:, :], start=False, stop=False)
                                pea.inc(te.matmul(pb[:, :], kl, qgh[:, :],
                                                  start=False, stop=True))

                        @blk.vector
                        def _(v, g=g, L=L, pea0=pea0):
                            for kt in range(L):
                                pea.wt(v, pea0 + kt + 1)
                                pb = s_ps[kt % 2]
                                dst = sbuf_s[:, kt * 512:(kt + 1) * 512]
                                if kt >= 4 * g:
                                    jm = kt - 4 * g
                                    dva.inc(v.scalar_tensor_tensor(
                                        dst, pb[:, :], SCL,
                                        maskb[:, jm * 512:(jm + 1) * 512],
                                        ALU.mult, ALU.add))
                                else:
                                    dva.inc(v.tensor_scalar(dst, pb[:, :], SCL,
                                                            None, ALU.mult))
                                v.reduce_max(mx8[:, kt:kt + 1], dst, axis=AX.X)
                            dvb.inc(v.reduce_max(mcol[:, :], mx8[:, 0:L], axis=AX.X))

                        @blk.tensor
                        def _(te, dvb0=dvb0, L=L):
                            dvb.wt(te, dvb0 + 1)
                            pec.inc(te.transpose(mt_ps[:, :], mcol[:, :],
                                                 identf_s[:, :]))

                        @blk.vector
                        def _(v, pec0=pec0):
                            pec.wt(v, pec0 + 1)
                            dvc.inc(v.reduce_max(negm[:, :], mt_ps[0:1, :], axis=AX.X,
                                                 negate=True))

                        @blk.tensor
                        def _(te, dvc0=dvc0):
                            dvc.wt(te, dvc0 + 1)
                            pec.inc(te.matmul(nb_ps[:, :], onesrow[:, :], negm[:, :],
                                              start=True, stop=True))

                        @blk.vector
                        def _(v, pec0=pec0):
                            pec.wt(v, pec0 + 2)
                            dvc.inc(v.tensor_copy(bias_s[:, :], nb_ps[:, :]))

                        @blk.scalar
                        def _(sc, L=L, dvc0=dvc0, dva0=dva0, gpb0=gpb0, peb0=peb0):
                            dvc.wt(sc, dvc0 + 2)
                            for kt in range(L):
                                dva.wt(sc, dva0 + kt + 1)
                                if kt >= 2:
                                    gpb.wt(sc, gpb0 + kt - 1)
                                if kt >= 3:
                                    peb.wt(sc, peb0 + kt - 2)
                                aca.inc(sc.activation(
                                    e32[kt % 2][:, :],
                                    sbuf_s[:, kt * 512:(kt + 1) * 512],
                                    AF.Exp, bias=bias_s[:, 0:1], scale=1.0))
                                aca.wt(sc)
                                acb.inc(sc.copy(ehb[kt % 3][:, :], e32[kt % 2][:, :]))

                        @blk.gpsimd
                        def _(gp, L=L, acb0=acb0, peb0=peb0):
                            for kt in range(L):
                                acb.wt(gp, acb0 + kt + 1)
                                if kt >= 3:
                                    peb.wt(gp, peb0 + kt - 2)
                                gpb.inc(gp.tensor_tensor(
                                    elb[kt % 3][:, :], e32[kt % 2][:, :],
                                    ehb[kt % 3][:, :], ALU.subtract))

                        @blk.tensor
                        def _(te, p_=p_, L=L, gpb0=gpb0):
                            for kt in range(L):
                                tile = 32 * p_ + kt
                                gpb.wt(te, gpb0 + kt + 1)
                                vh = vbuf_h[:, tile * 128:(tile + 1) * 128]
                                vl = vbuf_l[:, tile * 128:(tile + 1) * 128]
                                eh_, el_ = ehb[kt % 3], elb[kt % 3]
                                st = (kt == 0)
                                sp = (kt == L - 1)
                                te.matmul(av_ps[:, :], vh, eh_[:, :], start=st, stop=False)
                                te.matmul(av_ps[:, :], vh, el_[:, :], start=False,
                                          stop=False)
                                te.matmul(av_ps[:, :], vl, eh_[:, :], start=False, stop=sp)
                                te.matmul(dn_ps[:, :], onescol[:, :], eh_[:, :],
                                          start=st, stop=False)
                                peb.inc(te.matmul(dn_ps[:, :], onescol[:, :], el_[:, :],
                                                  start=False, stop=sp))

                        @blk.vector
                        def _(v, L=L, peb0=peb0):
                            peb.wt(v, peb0 + L)
                            dvc.inc(v.reciprocal(dn_row[:, :], dn_ps[0:1, :]))

                        @blk.tensor
                        def _(te, dvc0=dvc0):
                            dvc.wt(te, dvc0 + 3)
                            pec.inc(te.matmul(bc_ps[:, :], onesrow[:, :], dn_row[:, :],
                                              start=True, stop=True))

                        @blk.vector
                        def _(v, pec0=pec0):
                            pec.wt(v, pec0 + 3)
                            dvc.inc(v.tensor_copy(bc_sb[:, :], bc_ps[:, :]))
                            dvc.wt(v)
                            dvc.inc(v.tensor_tensor(ao32[:, :], av_ps[:, :],
                                                    bc_sb[:, :], ALU.mult))

                        @blk.gpsimd
                        def _(gp, p_=p_, h=h, g=g, dvc0=dvc0):
                            dvc.wt(gp, dvc0 + 5)
                            r0 = h * 128
                            c0 = p_ * S + g * 512
                            stb.dma(gp.dma_start(
                                out=ag2_in[r0:r0 + 128, c0:c0 + 512], in_=ao32[:, :]))
                            if dbg:
                                stb.dma(gp.dma_start(
                                    out=ag2_cp[r0:r0 + 128, c0:c0 + 512],
                                    in_=ao32[:, :]))
                            stb.wt(gp)

        with nc.Block() as blk:
            @blk.gpsimd
            def _(gp):
                cca.inc(gp.collective_compute(
                    "AllGather", ALU.bypass,
                    replica_groups=[list(range(NCORE))],
                    ins=[ag2_in[:, :]],
                    outs=[ag2_out[:, :]]))

    es123.close()
    if phases <= 3:
        return nc

    with contextlib.ExitStack() as es:
        sb = lambda n_, s_, d_=F32: es.enter_context(nc.sbuf_tensor(n_, s_, d_)).ap()
        ps = lambda n_, s_, d_=F32: es.enter_context(nc.psum_tensor(n_, s_, d_)).ap()
        aoh = sb("p4_aoh", [128, 32 * 1024], BF16)
        aol = sb("p4_aol", [128, 32 * 1024], BF16)
        aost = [sb(f"p4_aost{i}", [128, 4096]) for i in range(2)]
        c0sb = sb("p4_c0", [1, 1], I32)
        wst = [sb(f"p4_wst{i}", [128, 512]) for i in range(2)]
        woh = [sb(f"p4_woh{i}", [128, 512], BF16) for i in range(2)]
        wol = [sb(f"p4_wol{i}", [128, 512], BF16) for i in range(2)]
        xres = [sb(f"p4_xr{i}", [128, 512]) for i in range(2)]
        x2st = [sb(f"p4_x2{i}", [128, 512]) for i in range(2)]
        wo_ps = [ps(f"p4_wops{i}", [128, 512]) for i in range(8)]

        with nc.Block() as blk:
            ldb0, ldb20, aca0, dva0 = ldb.v, ldb2.v, aca.v, dva.v

            @blk.sync
            def _(sync, ldb0=ldb0, dva0=dva0):
                ldb.dma(sync.dma_start(out=c0sb[:, :], in_=col0[:, :]))
                ldb.wt(sync, ldb0 + 16)
                cca.wt(sync)
                ag2v = ag2_out.rearrange("(g a p) c -> g p a c", p=128, a=4)
                with sync.register("p4off") as off_reg:
                    sync.reg_load(off_reg, c0sb[0:1, 0:1])
                    off = sync.snap(off_reg)
                    for g4 in range(8):
                        if g4 >= 2:
                            dva.wt(sync, dva0 + 4 * (g4 - 1))
                        dst = aost[g4 % 2].rearrange("p (a b) -> p a b", a=4)
                        sem = ldb if g4 % 2 == 0 else ldb2
                        sem.dma(sync.dma_start(
                            out=dst, in_=ag2v[g4, :, :, bass.ds(off, 1024)]))

            @blk.scalar
            def _(sc, ldb0=ldb0, ldb20=ldb20):
                for g4 in range(8):
                    if g4 % 2 == 0:
                        ldb.wt(sc, ldb0 + 16 + (g4 // 2 + 1) * 16)
                    else:
                        ldb2.wt(sc, ldb20 + (g4 // 2 + 1) * 16)
                    for j in range(4):
                        dc = g4 * 4 + j
                        aca.inc(sc.copy(aoh[:, dc * 1024:(dc + 1) * 1024],
                                        aost[g4 % 2][:, j * 1024:(j + 1) * 1024]))

            @blk.vector
            def _(v, aca0=aca0):
                for g4 in range(8):
                    for j in range(4):
                        dc = g4 * 4 + j
                        aca.wt(v, aca0 + dc + 1)
                        dva.inc(v.tensor_tensor(aol[:, dc * 1024:(dc + 1) * 1024],
                                                aost[g4 % 2][:, j * 1024:(j + 1) * 1024],
                                                aoh[:, dc * 1024:(dc + 1) * 1024],
                                                ALU.subtract))

        for oc in range(8):
            with nc.Block() as blk:
                ldc0, ldc20, lda0, lda20, dva0, dvb0, peb0, sta0, sta20 = (
                    ldc.v, ldc2.v, lda.v, lda2.v, dva.v, dvb.v, peb.v, sta.v,
                    sta2.v)

                @blk.sync
                def _(sync, oc=oc, dva0=dva0, dvb0=dvb0):
                    for dc in range(32):
                        if dc >= 2:
                            dva.wt(sync, dva0 + 2 * (dc - 1))
                        sem = ldc if dc % 2 == 0 else ldc2
                        sem.dma(sync.dma_start(
                            out=wst[dc % 2][:, :],
                            in_=wo[dc * 128:(dc + 1) * 128,
                                   oc * 512:(oc + 1) * 512]))
                    for tt in range(8):
                        if tt >= 2:
                            dvb.wt(sync, dvb0 + tt - 1)
                        sem = lda if tt % 2 == 0 else lda2
                        sem.dma(sync.dma_start(
                            out=xres[tt % 2][:, :],
                            in_=x_my[tt * 128:(tt + 1) * 128,
                                     oc * 512:(oc + 1) * 512]))

                @blk.vector
                def _(v, ldc0=ldc0, ldc20=ldc20, peb0=peb0):
                    for dc in range(32):
                        if dc % 2 == 0:
                            ldc.wt(v, ldc0 + (dc // 2 + 1) * 16)
                        else:
                            ldc2.wt(v, ldc20 + (dc // 2 + 1) * 16)
                        if dc >= 2:
                            peb.wt(v, peb0 + dc - 1)
                        dva.inc(v.tensor_copy(woh[dc % 2][:, :], wst[dc % 2][:, :]))
                        dva.wt(v)
                        dva.inc(v.tensor_tensor(wol[dc % 2][:, :], wst[dc % 2][:, :],
                                                woh[dc % 2][:, :], ALU.subtract))

                @blk.tensor
                def _(te, dva0=dva0):
                    for dc in range(32):
                        dva.wt(te, dva0 + 2 * (dc + 1))
                        st = (dc == 0)
                        sp = (dc == 31)
                        for tt in range(8):
                            ah = aoh[:, dc * 1024 + tt * 128:dc * 1024 + (tt + 1) * 128]
                            al = aol[:, dc * 1024 + tt * 128:dc * 1024 + (tt + 1) * 128]
                            te.matmul(wo_ps[tt][:, :], ah, woh[dc % 2][:, :],
                                      start=st, stop=False)
                            te.matmul(wo_ps[tt][:, :], ah, wol[dc % 2][:, :],
                                      start=False, stop=False)
                            ins = te.matmul(wo_ps[tt][:, :], al, woh[dc % 2][:, :],
                                            start=False, stop=sp)
                            if tt == 7:
                                peb.inc(ins)

                @blk.vector
                def _(v, peb0=peb0, lda0=lda0, lda20=lda20,
                      sta0=sta0, sta20=sta20):
                    for tt in range(8):
                        peb.wt(v, peb0 + 32)
                        if tt % 2 == 0:
                            lda.wt(v, lda0 + (tt // 2 + 1) * 16)
                        else:
                            lda2.wt(v, lda20 + (tt // 2 + 1) * 16)
                        if tt >= 2:
                            if tt % 2 == 0:
                                sta.wt(v, sta0 + (tt // 2) * 16)
                            else:
                                sta2.wt(v, sta20 + (tt // 2) * 16)
                        dvb.inc(v.tensor_tensor(x2st[tt % 2][:, :], wo_ps[tt][:, :],
                                                xres[tt % 2][:, :], ALU.add))

                @blk.gpsimd
                def _(gp, oc=oc, dvb0=dvb0):
                    for tt in range(8):
                        dvb.wt(gp, dvb0 + tt + 1)
                        sem = sta if tt % 2 == 0 else sta2
                        sem.dma(gp.dma_start(
                            out=x2_sc[tt * 128:(tt + 1) * 128,
                                      oc * 512:(oc + 1) * 512],
                            in_=x2st[tt % 2][:, :]))
                    sta.wt(gp)
                    sta2.wt(gp)

    if phases <= 4:
        return nc
    with contextlib.ExitStack() as es:
        sb = lambda n_, s_, d_=F32: es.enter_context(nc.sbuf_tensor(n_, s_, d_)).ap()
        ps = lambda n_, s_, d_=F32: es.enter_context(nc.psum_tensor(n_, s_, d_)).ap()
        swg_s = sb("p5_swg", [128, IFF])
        swu_s = sb("p5_swu", [128, IFF])
        invg_s = sb("p5_invg", [128, D])
        invu_s = sb("p5_invu", [128, D])
        invi_s = sb("p5_invi", [128, IFF])
        xsg = sb("p5_xsg", [128, D])
        xsu = sb("p5_xsu", [128, D])
        h2 = sb("p5_h2", [128, D], BF16)
        st5 = {n_: sb(f"p5_{n_}", [128, 1]) for n_ in
               ["sum", "m", "ssq", "std", "rstd"]}
        qgb = sb("p5_qgb", [128, D], BF16)
        qub = sb("p5_qub", [128, D], BF16)
        rcpc = sb("p5_rcpc", [128, 1])
        sgb = sb("p5_sgb", [128, 2])
        sub = sb("p5_sub", [128, 2])
        sib = sb("p5_sib", [128, 2])
        qgT = sb("p5_qgT", [128, 32 * FBLK], BF16)
        quT = sb("p5_quT", [128, 32 * FBLK], BF16)
        qiT = sb("p5_qiT", [128, 16 * FBLK], BF16)
        interb = [sb(f"p5_int{i}", [128, IFF], BF16) for i in range(2)]
        work2048 = sb("p5_work", [128, IFF])
        xsi = work2048
        gbuf = work2048[:, 0:512]
        silb = work2048[:, 512:1024]
        ubuf = work2048[:, 1024:1536]
        ibuf = work2048[:, 1536:2048]
        qib = sb("p5_qib", [128, IFF], BF16)
        wst5 = [sb(f"p5_wst{i}", [128, 1024]) for i in range(2)]
        wgb = [sb(f"p5_wgb{i}", [128, 512], BF16) for i in range(2)]
        wub = [sb(f"p5_wub{i}", [128, 512], BF16) for i in range(2)]
        wdst = [wst5[i][:, 0:512] for i in range(2)]
        wdb = [sb(f"p5_wdb{i}", [128, 512], BF16) for i in range(2)]
        swdst = [sb(f"p5_swdst{i}", [128, 512]) for i in range(2)]
        x2r = [sb(f"p5_x2r{i}", [128, 512]) for i in range(2)]
        tb5 = sb("p5_tb", [128, 512])
        yb = [sb(f"p5_yb{i}", [128, 512]) for i in range(2)]
        gu_ps = [ps(f"p5_gups{i}", [128, 512]) for i in range(4)]
        d_ps = [ps(f"p5_dps{i}", [128, 512]) for i in range(2)]
        pst5 = [ps(f"p5_pst{i}", [128, 1024], BF16) for i in range(2)]

        with nc.Block() as blk:
            @blk.sync
            def _(sync):
                for dst, src in ((swg_s, swg), (swu_s, swu),
                                 (invg_s, invg), (invu_s, invu), (invi_s, invi)):
                    ldd.dma(sync.dma_start(out=dst[:, :], in_=src[:, :]))
                ldd.wt(sync)

        def quant(eng, ctr, src_bf, inv_s, xs_, sc_col, qb_, n):
            ctr.inc(eng.tensor_tensor(xs_[:, 0:n], src_bf[:, 0:n], inv_s[:, 0:n],
                                      ALU.mult))
            ctr.wt(eng)
            ctr.inc(eng.reduce_max(sc_col, xs_[:, 0:n], axis=AX.X,
                                   apply_absolute_value=True))
            ctr.wt(eng)
            ctr.inc(eng.tensor_scalar(sc_col, sc_col, float(1.0 / 127.0), 1e-8,
                                      ALU.mult, ALU.max))
            ctr.wt(eng)
            ctr.inc(eng.reciprocal(rcpc[:, :], sc_col))
            ctr.wt(eng)
            ctr.inc(eng.tensor_scalar(xs_[:, 0:n], xs_[:, 0:n], rcpc[:, 0:1], MAGIC,
                                      ALU.mult, ALU.add))
            ctr.wt(eng)
            ctr.inc(eng.tensor_scalar(xs_[:, 0:n], xs_[:, 0:n], MAGIC, 127.0,
                                      ALU.subtract, ALU.min))
            ctr.wt(eng)
            ctr.inc(eng.tensor_scalar(qb_[:, 0:n], xs_[:, 0:n], -127.0, None,
                                      ALU.max))
            ctr.wt(eng)

        for b in range(p5b):
            for t2 in range(2):
                tt = b * 2 + t2
                with nc.Block() as blk:
                    lda0, dva0, aca0, acb0, gpa0, pea0, gpb0 = (
                        lda.v, dva.v, aca.v, acb.v, gpa.v, pea.v, gpb.v)

                    @blk.sync
                    def _(sync, tt=tt):
                        lda.dma(sync.dma_start(
                            out=xsu[:, :], in_=x2_sc[tt * 128:(tt + 1) * 128, :]))

                    @blk.vector
                    def _(v, lda0=lda0, aca0=aca0, t2=t2):
                        lda.wt(v, lda0 + 16)
                        dva.inc(v.reduce_sum(st5["sum"][:, :], xsu[:, :], axis=AX.X))
                        dva.wt(v)
                        dva.inc(v.tensor_scalar(st5["m"][:, :], st5["sum"][:, :],
                                                float(1.0 / D), None, ALU.mult))
                        dva.wt(v)
                        dva.inc(v.tensor_scalar(xsg[:, :], xsu[:, :], st5["m"][:, 0:1],
                                                None, ALU.subtract))
                        aca.wt(v, aca0 + 2)
                        dva.inc(v.reciprocal(st5["rstd"][:, :], st5["std"][:, :]))
                        dva.wt(v)
                        dva.inc(v.tensor_scalar(h2[:, :], xsg[:, :],
                                                st5["rstd"][:, 0:1], None, ALU.mult))
                        dva.wt(v)
                        if p5q >= 1:
                            quant(v, dva, h2, invg_s, xsg, sgb[:, t2:t2 + 1], qgb, D)
                        if p5q >= 2:
                            quant(v, dva, h2, invu_s, xsu, sub[:, t2:t2 + 1], qub, D)

                    @blk.scalar
                    def _(sc, t2=t2, dva0=dva0, pea0=pea0):
                        dva.wt(sc, dva0 + 3)
                        aca.inc(sc.activation(h2[:, :], xsg[:, :], AF.Square,
                                              accum_out=st5["ssq"][:, 0:1]))
                        aca.wt(sc)
                        aca.inc(sc.activation(st5["std"][:, :], st5["ssq"][:, 0:1],
                                              AF.Sqrt, bias=epsap[:, 0:1],
                                              scale=float(1.0 / D)))
                        if p5q < 3:
                            return
                        gi = 0
                        for dstT in (qgT, quT):
                            for grp in range(4):
                                gi += 1
                                pea.wt(sc, pea0 + gi)
                                pb = pst5[(gi - 1) % 2]
                                c_ = t2 * 4096 + grp * 1024
                                acb.inc(sc.copy(dstT[:, c_:c_ + 1024], pb[:, :]))

                    @blk.tensor
                    def _(te, dva0=dva0, acb0=acb0):
                        if p5q < 3:
                            return
                        gi = 0
                        for src in (qgb, qub):
                            if src is qgb:
                                dva.wt(te, dva0 + 12)
                            else:
                                dva.wt(te, dva0 + 19)
                            for grp in range(4):
                                if gi >= 2:
                                    acb.wt(te, acb0 + gi - 1)
                                pb = pst5[gi % 2]
                                for j in range(8):
                                    dc = grp * 8 + j
                                    ins = te.transpose(
                                        pb[:, j * 128:(j + 1) * 128],
                                        src[:, dc * 128:(dc + 1) * 128],
                                        identb_s[:, :])
                                    if j == 7:
                                        pea.inc(ins)
                                gi += 1

            if p5s < 2:
                continue
            with nc.Block() as blk:
                ldb0, ldb20, dva0, dvb0, aca0, peb0 = (
                    ldb.v, ldb2.v, dva.v, dvb.v, aca.v, peb.v)

                @blk.sync
                def _(sync, dva0=dva0):
                    for ic in range(4):
                        for hc in range(32):
                            it = ic * 32 + hc
                            if it >= 2:
                                dva.wt(sync, dva0 + 2 * (it - 1))
                            sem = ldb if it % 2 == 0 else ldb2
                            sem.dma(sync.dma_start(
                                out=wst5[it % 2][:, 0:512],
                                in_=wgT[hc * 128:(hc + 1) * 128,
                                        ic * 512:(ic + 1) * 512]))
                            sem.dma(sync.dma_start(
                                out=wst5[it % 2][:, 512:1024],
                                in_=wuT[hc * 128:(hc + 1) * 128,
                                        ic * 512:(ic + 1) * 512]))

                @blk.vector
                def _(v, ldb0=ldb0, ldb20=ldb20, peb0=peb0, aca0=aca0):
                    for ic in range(4):
                        for hc in range(32):
                            it = ic * 32 + hc
                            if it % 2 == 0:
                                ldb.wt(v, ldb0 + (it // 2 + 1) * 32)
                            else:
                                ldb2.wt(v, ldb20 + (it // 2 + 1) * 32)
                            if it >= 2:
                                peb.wt(v, peb0 + it - 1)
                            dva.inc(v.tensor_copy(wgb[it % 2][:, :],
                                                  wst5[it % 2][:, 0:512]))
                            dva.wt(v)
                            dva.inc(v.tensor_copy(wub[it % 2][:, :],
                                                  wst5[it % 2][:, 512:1024]))
                        for t2 in range(2):
                            idx = ic * 2 + t2
                            peb.wt(v, peb0 + (ic + 1) * 32)
                            aca.wt(v, aca0 + idx)
                            dvb.inc(v.scalar_tensor_tensor(
                                gbuf[:, :], gu_ps[t2][:, :], sgb[:, t2:t2 + 1],
                                swg_s[:, ic * 512:(ic + 1) * 512],
                                ALU.mult, ALU.mult))
                            dvb.inc(v.scalar_tensor_tensor(
                                ubuf[:, :], gu_ps[2 + t2][:, :], sub[:, t2:t2 + 1],
                                swu_s[:, ic * 512:(ic + 1) * 512],
                                ALU.mult, ALU.mult))
                            aca.wt(v, aca0 + idx + 1)
                            dvb.inc(v.tensor_tensor(ibuf[:, :], silb[:, :],
                                                    ubuf[:, :], ALU.mult))
                            dvb.wt(v)
                            dvb.inc(v.tensor_scalar(
                                interb[t2][:, ic * 512:(ic + 1) * 512],
                                ibuf[:, :], 10.0, -10.0, ALU.min, ALU.max))

                @blk.tensor
                def _(te, dva0=dva0, dvb0=dvb0):
                    for ic in range(4):
                        for hc in range(32):
                            it = ic * 32 + hc
                            dva.wt(te, dva0 + 2 * (it + 1))
                            if hc == 0 and ic >= 1:
                                dvb.wt(te, dvb0 + 8 * ic)
                            st = (hc == 0)
                            sp = (hc == 31)
                            for t2 in range(2):
                                qg_t = qgT[:, t2 * 4096 + hc * 128:
                                           t2 * 4096 + (hc + 1) * 128]
                                qu_t = quT[:, t2 * 4096 + hc * 128:
                                           t2 * 4096 + (hc + 1) * 128]
                                te.matmul(gu_ps[t2][:, :], qg_t, wgb[it % 2][:, :],
                                          start=st, stop=sp)
                                ins = te.matmul(gu_ps[2 + t2][:, :], qu_t,
                                                wub[it % 2][:, :], start=st, stop=sp)
                                if t2 == 1:
                                    peb.inc(ins)

                @blk.scalar
                def _(sc, dvb0=dvb0):
                    for ic in range(4):
                        for t2 in range(2):
                            idx = ic * 2 + t2
                            dvb.wt(sc, dvb0 + idx * 4 + 1)
                            aca.inc(sc.activation(silb[:, :], gbuf[:, :], AF.Silu))

            if p5s < 3:
                continue
            for t2 in range(2):
                with nc.Block() as blk:
                    dva0, pea0 = dva.v, pea.v

                    @blk.vector
                    def _(v, t2=t2):
                        quant(v, dva, interb[t2], invi_s, xsi, sib[:, t2:t2 + 1],
                              qib, IFF)

                    @blk.tensor
                    def _(te, dva0=dva0):
                        dva.wt(te, dva0 + 7)
                        for grp in range(2):
                            pb = pst5[grp % 2]
                            for j in range(8):
                                dc = grp * 8 + j
                                ins = te.transpose(pb[:, j * 128:(j + 1) * 128],
                                                   qib[:, dc * 128:(dc + 1) * 128],
                                                   identb_s[:, :])
                                if j == 7:
                                    pea.inc(ins)

                    @blk.scalar
                    def _(sc, t2=t2, pea0=pea0):
                        for grp in range(2):
                            pea.wt(sc, pea0 + grp + 1)
                            pb = pst5[grp % 2]
                            c_ = t2 * 2048 + grp * 1024
                            acb.inc(sc.copy(qiT[:, c_:c_ + 1024], pb[:, :]))

            if dbg:
                with nc.Block() as blk:
                    @blk.gpsimd
                    def _(gp, b=b):
                        r0 = b * 128
                        stb.dma(gp.dma_start(out=qg_sc[r0:r0 + 128, :],
                                             in_=qgT[:, :]))
                        stb.dma(gp.dma_start(out=qu_sc[r0:r0 + 128, :],
                                             in_=quT[:, :]))
                        stb.dma(gp.dma_start(out=qi_sc[r0:r0 + 128, :],
                                             in_=qiT[:, :]))
                        stb.dma(gp.dma_start(out=scl_sc[r0:r0 + 128, 0:2],
                                             in_=sgb[:, :]))
                        stb.dma(gp.dma_start(out=scl_sc[r0:r0 + 128, 2:4],
                                             in_=sub[:, :]))
                        stb.dma(gp.dma_start(out=scl_sc[r0:r0 + 128, 4:6],
                                             in_=sib[:, :]))
                        for t2 in range(2):
                            stb.dma(gp.dma_start(
                                out=int_sc[(b * 2 + t2) * 128:
                                           (b * 2 + t2 + 1) * 128, :],
                                in_=interb[t2][:, :]))
                        stb.wt(gp)

            if p5s < 4:
                continue
            with nc.Block() as blk:
                (ldb0, ldb20, ldd0, ldd20, lda0, lda20, dva0, dvb0, peb0,
                 gpa0, sta0, sta20) = (
                    ldb.v, ldb2.v, ldd.v, ldd2.v, lda.v, lda2.v, dva.v,
                    dvb.v, peb.v, gpa.v, sta.v, sta2.v)

                @blk.sync
                def _(sync, b=b, dva0=dva0, gpa0=gpa0, dvb0=dvb0):
                    for oc in range(8):
                        for ic in range(16):
                            it = oc * 16 + ic
                            if it >= 2:
                                dva.wt(sync, dva0 + it - 1)
                            sem = ldb if it % 2 == 0 else ldb2
                            sem.dma(sync.dma_start(
                                out=wdst[it % 2][:, :],
                                in_=wdT[ic * 128:(ic + 1) * 128,
                                        oc * 512:(oc + 1) * 512]))
                        if oc >= 2:
                            dvb.wt(sync, dvb0 + 2 * (oc - 1))
                        sem = ldd if oc % 2 == 0 else ldd2
                        sem.dma(sync.dma_start(
                            out=swdst[oc % 2][:, :],
                            in_=swd[:, oc * 512:(oc + 1) * 512]))
                        for t2 in range(2):
                            i2 = oc * 2 + t2
                            if i2 >= 4:
                                gpa.wt(sync, gpa0 + i2 - 1)
                            sem = lda if t2 == 0 else lda2
                            sem.dma(sync.dma_start(
                                out=x2r[t2][:, :],
                                in_=x2_sc[(b * 2 + t2) * 128:(b * 2 + t2 + 1) * 128,
                                          oc * 512:(oc + 1) * 512]))

                @blk.vector
                def _(v, ldb0=ldb0, ldb20=ldb20, peb0=peb0,
                      ldd0=ldd0, ldd20=ldd20, gpa0=gpa0):
                    for oc in range(8):
                        for ic in range(16):
                            it = oc * 16 + ic
                            if it % 2 == 0:
                                ldb.wt(v, ldb0 + (it // 2 + 1) * 16)
                            else:
                                ldb2.wt(v, ldb20 + (it // 2 + 1) * 16)
                            if it >= 2:
                                peb.wt(v, peb0 + it - 1)
                            dva.inc(v.tensor_copy(wdb[it % 2][:, :],
                                                  wdst[it % 2][:, :]))
                        for t2 in range(2):
                            i2 = oc * 2 + t2
                            peb.wt(v, peb0 + (oc + 1) * 16)
                            if oc % 2 == 0:
                                ldd.wt(v, ldd0 + (oc // 2 + 1) * 16)
                            else:
                                ldd2.wt(v, ldd20 + (oc // 2 + 1) * 16)
                            if i2 >= 1:
                                gpa.wt(v, gpa0 + i2)
                            dvb.inc(v.scalar_tensor_tensor(
                                tb5[:, :], d_ps[t2][:, :], sib[:, t2:t2 + 1],
                                swdst[oc % 2][:, :], ALU.mult, ALU.mult))
                            dvb.wt(v)

                @blk.tensor
                def _(te, dva0=dva0, dvb0=dvb0):
                    for oc in range(8):
                        for ic in range(16):
                            it = oc * 16 + ic
                            dva.wt(te, dva0 + it + 1)
                            if ic == 0 and oc >= 1:
                                dvb.wt(te, dvb0 + 2 * oc)
                            st = (ic == 0)
                            sp = (ic == 15)
                            for t2 in range(2):
                                qi_t = qiT[:, t2 * 2048 + ic * 128:
                                           t2 * 2048 + (ic + 1) * 128]
                                ins = te.matmul(d_ps[t2][:, :], qi_t,
                                                wdb[it % 2][:, :], start=st, stop=sp)
                                if t2 == 1:
                                    peb.inc(ins)

                @blk.gpsimd
                def _(gp, b=b, dvb0=dvb0, lda0=lda0, lda20=lda20,
                      sta0=sta0, sta20=sta20):
                    for oc in range(8):
                        for t2 in range(2):
                            i2 = oc * 2 + t2
                            dvb.wt(gp, dvb0 + i2 + 1)
                            if t2 == 0:
                                lda.wt(gp, lda0 + (oc + 1) * 16)
                            else:
                                lda2.wt(gp, lda20 + (oc + 1) * 16)
                            if i2 >= 2:
                                if i2 % 2 == 0:
                                    sta.wt(gp, sta0 + (i2 // 2) * 16)
                                else:
                                    sta2.wt(gp, sta20 + (i2 // 2) * 16)
                            gpa.inc(gp.tensor_tensor(yb[i2 % 2][:, :], tb5[:, :],
                                                     x2r[t2][:, :], ALU.add))
                            gpa.wt(gp)
                            sem = sta if i2 % 2 == 0 else sta2
                            sem.dma(gp.dma_start(
                                out=y[(b * 2 + t2) * 128:(b * 2 + t2 + 1) * 128,
                                      oc * 512:(oc + 1) * 512],
                                in_=yb[i2 % 2][:, :]))
                    sta.wt(gp)
                    sta2.wt(gp)

    with nc.Block() as blk:
        @blk.gpsimd
        def _(gp):
            sta.wt(gp)
            sta2.wt(gp)

    return nc


_CONST_CACHE = {}


def _host_constants():
    if "c" in _CONST_CACHE:
        return _CONST_CACHE["c"]
    f32 = np.float32
    freqs = 1.0 / (10000.0 ** (np.arange(0, 128, 2, dtype=np.float64) / 128.0))
    t = np.arange(S, dtype=np.float64)
    ang = np.outer(t, freqs)
    cos = np.ascontiguousarray(np.cos(ang).astype(f32))
    sin = np.ascontiguousarray(np.sin(ang).astype(f32))
    cosq = np.ascontiguousarray(np.tile(cos, (1, 4)))
    sinq = np.ascontiguousarray(np.tile(sin, (1, 4)))
    lane = np.arange(128)[:, None]
    qf = np.arange(512)[None, :]
    m = np.zeros((128, 4 * 512), dtype=f32)
    for j in range(4):
        m[:, j * 512:(j + 1) * 512] = np.where(128 * j + lane <= qf, 0.0, -1.0e9)
    identb = np.eye(128, dtype=ml_dtypes.bfloat16)
    identf = np.eye(128, dtype=f32)
    _CONST_CACHE["c"] = (cosq, sinq, cos, sin, m, identb, identf)
    return _CONST_CACHE["c"]


def kernel(L0_attn_x, L0_attn_wq, L0_attn_wk, L0_attn_wv, L0_attn_wo,
           L0_ffn_w_gate, L0_ffn_s_wgate, L0_ffn_w_up, L0_ffn_s_wup,
           L0_ffn_w_down, L0_ffn_s_wdown, L0_ffn_inv_gate, L0_ffn_inv_up,
           L0_ffn_inv_inter):
    f32 = np.float32
    x = np.ascontiguousarray(np.asarray(L0_attn_x, dtype=f32).reshape(NTOK, D))
    wq = np.asarray(L0_attn_wq, dtype=f32)
    wk = np.asarray(L0_attn_wk, dtype=f32)
    wv = np.asarray(L0_attn_wv, dtype=f32)
    wo_ = np.ascontiguousarray(np.asarray(L0_attn_wo, dtype=f32))
    wgT = np.ascontiguousarray(np.asarray(L0_ffn_w_gate, dtype=f32).T)
    wuT = np.ascontiguousarray(np.asarray(L0_ffn_w_up, dtype=f32).T)
    wdT = np.ascontiguousarray(np.asarray(L0_ffn_w_down, dtype=f32).T)
    bcast = lambda a, n: np.ascontiguousarray(
        np.broadcast_to(np.asarray(a, dtype=f32)[None, :], (128, n)))
    swg_ = bcast(L0_ffn_s_wgate, IFF)
    swu_ = bcast(L0_ffn_s_wup, IFF)
    swd_ = bcast(L0_ffn_s_wdown, D)
    invg_ = bcast(L0_ffn_inv_gate, D)
    invu_ = bcast(L0_ffn_inv_up, D)
    invi_ = bcast(L0_ffn_inv_inter, IFF)
    cosq, sinq, cosk, sink, masks_, identb, identf = _host_constants()

    nc = build_program()
    in_maps = []
    for c in range(NCORE):
        wqkv = np.ascontiguousarray(np.concatenate(
            [wq[:, 512 * c:512 * (c + 1)],
             wk[:, 128 * c:128 * (c + 1)],
             wv[:, 128 * c:128 * (c + 1)]], axis=1))
        in_maps.append({
            "x_my": np.ascontiguousarray(x[TPC * c:TPC * (c + 1)]),
            "wqkv": wqkv, "wo": wo_, "wgT": wgT, "wuT": wuT, "wdT": wdT,
            "swg": swg_, "swu": swu_, "swd": swd_,
            "invg": invg_, "invu": invu_, "invi": invi_,
            "cosq": cosq, "sinq": sinq, "cosk": cosk, "sink": sink,
            "masks": masks_, "identb": identb, "identf": identf,
            "col0": np.array([[TPC * c]], dtype=np.int32),
        })
    res = run_bass_kernel_spmd(nc, in_maps, core_ids=list(range(NCORE)))
    out = np.concatenate([r["y"] for r in res.results], axis=0)
    return out.reshape(2, S, D)
